# version 14
# baseline (speedup 1.0000x reference)
import numpy as np
import ml_dtypes
from contextlib import ExitStack

import concourse.bass as bass
import concourse.mybir as mybir

F32 = mybir.dt.float32
BF16 = mybir.dt.bfloat16
AF = mybir.ActivationFunctionType
ALU = mybir.AluOpType
AX = mybir.AxisListType

ENGINES = ("sync", "scalar", "vector", "gpsimd", "tensor")
ISQ = 1.0 / np.sqrt(128.0)
ESHIFT = -10.0
LN_EPS = 1e-5
R_EPS = 1e-12
SCALING = 2.0



class Rec:
    __slots__ = ("eng", "waits", "fn", "incs", "evt")

    def __init__(self, eng, fn):
        self.eng = eng
        self.fn = fn
        self.waits = []
        self.incs = []
        self.evt = None


class TB:

    def __init__(self, name):
        self.name = name
        self.regs = []


def _overlap(a, b):
    return a[0] < b[1] and b[0] < a[1] and a[2] < b[3] and b[2] < a[3]


class KB:
    def __init__(self, nc, es):
        self.nc = nc
        self.es = es
        self.ops = {e: [] for e in ENGINES}
        self.prog_sem = {e: es.enter_context(nc.semaphore(f"prog_{e}")) for e in ENGINES}
        self.prog_cnt = {e: 0 for e in ENGINES}
        self.pending = {e: [] for e in ENGINES}
        self.watermark = {e: {} for e in ENGINES}
        self.dma_sem = {}
        self.dma_cnt = {}
        self.dma_queue = {}
        self.bar_sem = es.enter_context(nc.semaphore("bar"))
        self.bar_cnt = 0
        self.nbar = 0

    def _resolve(self, rec):
        if rec.evt is not None:
            return rec.evt
        e = rec.eng
        q = self.pending[e]
        idx = q.index(rec)
        for r in q[: idx + 1]:
            self.prog_cnt[e] += 1
            r.evt = ("prog", e, self.prog_cnt[e])
            r.incs.append((self.prog_sem[e], 1))
        del q[: idx + 1]
        return rec.evt

    def _evt_of(self, w):
        if isinstance(w, Rec):
            return self._resolve(w)
        return w

    def _dep(self, rec, w):
        evt = self._evt_of(w)
        kind = evt[0]
        if kind == "prog":
            _, peng, val = evt
            if peng == rec.eng == "tensor":
                return
            key = ("prog", peng)
            sem = self.prog_sem[peng]
        else:
            _, stream, val = evt
            key = ("dma", stream)
            sem = self.dma_sem[stream]
        wm = self.watermark[rec.eng]
        if wm.get(key, 0) >= val:
            return
        wm[key] = val
        rec.waits.append((sem, val))

    def _track(self, rec_or_evt, reads, writes):
        for tb, box in reads:
            deps = [r[4] for r in tb.regs if _overlap(box, r)]
            for w in deps:
                self._dep_any(rec_or_evt, w)
            for r in tb.regs:
                if _overlap(box, r):
                    r[5].append(rec_or_evt)
        for tb, box in writes:
            dead = []
            for r in tb.regs:
                if _overlap(box, r):
                    self._dep_any(rec_or_evt, r[4])
                    for rd in r[5]:
                        self._dep_any(rec_or_evt, rd)
                    if box[0] <= r[0] and box[1] >= r[1] and box[2] <= r[2] and box[3] >= r[3]:
                        dead.append(r)
            for r in dead:
                tb.regs.remove(r)
            tb.regs.append([box[0], box[1], box[2], box[3], rec_or_evt, []])

    def _dep_any(self, consumer, producer):
        if isinstance(consumer, Rec):
            if producer is consumer:
                return
            self._dep(consumer, producer)
        else:
            raise AssertionError("dma consumers handled via their issue rec")

    def op(self, eng, fn, reads=(), writes=(), inc=True):
        rec = Rec(eng, fn)
        self._track(rec, list(reads), list(writes))
        self.ops[eng].append(rec)
        if inc:
            self.pending[eng].append(rec)
            self._resolve(rec)
        else:
            self.pending[eng].append(rec)
        return rec

    def wait_manual(self, eng, w):
        rec = Rec(eng, None)
        self._dep(rec, self._evt_of(w) if not isinstance(w, Rec) else w)
        self.ops[eng].append(rec)
        return rec

    def dma(self, queue, stream, fn, reads=(), writes=()):
        if stream not in self.dma_sem:
            self.dma_sem[stream] = self.es.enter_context(self.nc.semaphore(f"dma_{stream}"))
            self.dma_cnt[stream] = 0
            self.dma_queue[stream] = queue
        assert self.dma_queue[stream] == queue, stream
        prev = self.dma_cnt[stream]
        rec = Rec(queue, fn)
        if prev > 0:
            key = ("dma", stream)
            if self.watermark[queue].get(key, 0) < 16 * prev:
                self.watermark[queue][key] = 16 * prev
                rec.waits.append((self.dma_sem[stream], 16 * prev))
        self.dma_cnt[stream] += 1
        val = 16 * self.dma_cnt[stream]
        evt = ("dma", stream, val)
        for tb, box in reads:
            for r in tb.regs:
                if _overlap(box, r):
                    self._dep(rec, r[4])
                    r[5].append(evt)
        for tb, box in writes:
            dead = []
            for r in tb.regs:
                if _overlap(box, r):
                    self._dep(rec, r[4])
                    for rd in r[5]:
                        self._dep(rec, rd)
                    if box[0] <= r[0] and box[1] >= r[1] and box[2] <= r[2] and box[3] >= r[3]:
                        dead.append(r)
            for r in dead:
                tb.regs.remove(r)
            tb.regs.append([box[0], box[1], box[2], box[3], evt, []])
        rec.incs.append((self.dma_sem[stream], 16))
        self.ops[queue].append(rec)
        return evt

    def barrier(self):
        self.nbar += 1
        tgt = 5 * self.nbar
        for e in ENGINES:
            if self.pending[e]:
                self._resolve(self.pending[e][-1])
            cnt = self.prog_cnt[e]
            rec = Rec(e, lambda eng: eng.nop())
            if cnt > 0:
                key = ("prog", e)
                if self.watermark[e].get(key, 0) < cnt:
                    self.watermark[e][key] = cnt
                    rec.waits.append((self.prog_sem[e], cnt))
            if e == "sync":
                for s, c in self.dma_cnt.items():
                    key = ("dma", s)
                    if c > 0 and self.watermark[e].get(key, 0) < 16 * c:
                        self.watermark[e][key] = 16 * c
                        rec.waits.append((self.dma_sem[s], 16 * c))
            rec.incs.append((self.bar_sem, 1))
            self.ops[e].append(rec)
        for e in ENGINES:
            rec = Rec(e, None)
            rec.waits.append((self.bar_sem, tgt))
            self.watermark[e] = {}
            self.ops[e].append(rec)

    def finalize_wait_all_dma(self):
        rec = Rec("sync", None)
        for s, c in self.dma_cnt.items():
            key = ("dma", s)
            if c > 0 and self.watermark["sync"].get(key, 0) < 16 * c:
                rec.waits.append((self.dma_sem[s], 16 * c))
        self.ops["sync"].append(rec)

    def replay(self, block):
        def section(e):
            def run(eng):
                for rec in self.ops[e]:
                    for sem, val in rec.waits:
                        eng.wait_ge(sem, val)
                    if rec.fn is not None:
                        inst = rec.fn(eng)
                        for sem, n in rec.incs:
                            inst.then_inc(sem, n)
                    else:
                        assert not rec.incs
            return run

        for e in ENGINES:
            getattr(block, e)(section(e))



def build(cfg, dbg=()):
    S = cfg["S"]; D = cfg["D"]; NH = cfg["NH"]; NKV = cfg["NKV"]
    HD = 128
    NT = S // 128
    NSLOT = NT // 2
    HC = D // 128
    KVW = NKV * 128
    G = NH // NKV
    assert G == 4
    OWN = NSLOT * 128
    OCW = min(512, D)
    NOC = D // OCW
    TCW = min(512, OWN)
    NTC = OWN // TCW
    SPT = min(8, NSLOT)
    NTTB = NSLOT // SPT
    QUAD = min(4, SPT)
    NQ = SPT // QUAD
    VOC = min(512, KVW)
    NVOC = KVW // VOC

    nc = bass.Bass("TRN2", target_bir_lowering=False, debug=False)
    es = ExitStack()
    kb = KB(nc, es)

    def dram(name, shape, dt, kind="ExternalInput"):
        t = nc.dram_tensor(name, shape, dt, kind=kind)
        return t.ap()

    x_all = dram("x_all", [NT, 128, D], F32)
    x_own = dram("x_own", [NSLOT, 128, D], F32)
    wkT = dram("wkT", [HC, 128, KVW], F32)
    wvT = dram("wvT", [HC, 128, KVW], F32)
    wqT = dram("wqT", [HC, 128, NH * 128], F32)
    woT = dram("woT", [HC, 128, D], F32)
    wbT = dram("wbT", [HC, 128, D], F32)
    bb = dram("bb", [128, D], F32)
    aT = dram("aT", [HC, 128, 32], F32)
    bT = dram("bT", [32, D], F32)
    rwT = dram("rwT", [HC, 128, 4], F32)
    prior = dram("prior", [128, 4], F32)
    exp2 = dram("exp2", [4, 32], F32)
    cosq = dram("cosq", [128, OWN], F32)
    sinq = dram("sinq", [128, OWN], F32)
    cosk = dram("cosk", [128, S], F32)
    sink = dram("sink", [128, S], F32)
    masks = dram("masks", [NSLOT, 2, 128, 512], BF16)
    ident_in = dram("ident", [128, 128], BF16)
    ones_in = dram("ones_in", [128, 128], BF16)
    shift_in = dram("shift_in", [128, 1], F32)
    rb4 = dram("rb4", [128, 4], F32)

    kT_s = dram("kT_s", [NKV, NT, 128, 128], BF16, kind="Internal")
    woT_bf = dram("woT_bf", [HC, 128, D], BF16, kind="Internal")
    wbT_bf = dram("wbT_bf", [HC, 128, D], BF16, kind="Internal")
    v_s = dram("v_s", [NKV, NT, 128, 128], BF16, kind="Internal")
    qT4_s = dram("qT4_s", [NSLOT, NKV, 128, 512], BF16, kind="Internal")
    avT_s = dram("avT_s", [NSLOT, NKV, 128, 512], BF16, kind="Internal")
    x2_s = dram("x2_s", [NSLOT, 128, D], F32, kind="Internal")
    h2T_s = dram("h2T_s", [NSLOT, HC, 128, 128], BF16, kind="Internal")
    y = dram("y", [NSLOT, 128, D], F32, kind="ExternalOutput")
    dbg_out = {}
    for name in dbg:
        shp = {"kT": [NKV, NT, 128, 128], "v": [NKV, NT, 128, 128],
               "qT4": [NSLOT, NKV, 128, 512], "avT": [NSLOT, NKV, 128, 512],
               "x2": [NSLOT, 128, D], "h2T": [NSLOT, HC, 128, 128]}[name]
        dbg_out[name] = dram("dbg_" + name, shp, BF16 if name not in ("x2",) else F32,
                             kind="ExternalOutput")

    t_kT = TB("kT_s"); t_v = TB("v_s"); t_q4 = TB("qT4_s"); t_avT = TB("avT_s")
    t_x2 = TB("x2_s"); t_h2T = TB("h2T_s")
    t_wobf = TB("woT_bf"); t_wbbf = TB("wbT_bf")

    ident = es.enter_context(nc.sbuf_tensor("ident_sb", [128, 128], BF16))
    ones = es.enter_context(nc.sbuf_tensor("ones_sb", [128, 128], BF16))
    shift = es.enter_context(nc.sbuf_tensor("shift_sb", [128, 1], F32))
    t_ident = TB("ident"); t_ones = TB("ones"); t_shift = TB("shift")
    BOXF = (0, 128, 0, 1 << 30)

    kb.dma("sync", "consts", lambda e: e.dma_start(out=ident[:], in_=ident_in[:]),
           writes=[(t_ident, BOXF)])
    kb.dma("sync", "consts", lambda e: e.dma_start(out=ones[:], in_=ones_in[:]),
           writes=[(t_ones, BOXF)])
    kb.dma("sync", "consts", lambda e: e.dma_start(out=shift[:], in_=shift_in[:]),
           writes=[(t_shift, BOXF)])

    def _phase1():
      with ExitStack() as ph:
        wk_sb = ph.enter_context(nc.sbuf_tensor("wk_sb", [128, HC * KVW], BF16))
        wv_sb = ph.enter_context(nc.sbuf_tensor("wv_sb", [128, HC * KVW], BF16))
        cosk_sb = ph.enter_context(nc.sbuf_tensor("cosk_sb", [128, S], F32))
        sink_sb = ph.enter_context(nc.sbuf_tensor("sink_sb", [128, S], F32))
        xt = [ph.enter_context(nc.sbuf_tensor("p1x0", [128, D], F32))] * 2
        h1 = ph.enter_context(nc.sbuf_tensor("p1h1", [128, D], BF16))
        h1T = ph.enter_context(nc.sbuf_tensor("p1h1T", [128, HC * 128], BF16))
        kro = [ph.enter_context(nc.sbuf_tensor(f"p1kro{i}", [128, KVW], BF16)) for i in range(2)]
        vsb = [ph.enter_context(nc.sbuf_tensor(f"p1v{i}", [128, KVW], BF16)) for i in range(2)]
        u_sb = ph.enter_context(nc.sbuf_tensor("p1u", [128, KVW], BF16))
        s_sb = ph.enter_context(nc.sbuf_tensor("p1s", [128, KVW], BF16))
        stats = ph.enter_context(nc.sbuf_tensor("p1stats", [128, 6 * (D // 512 if D >= 512 else 1)], F32))
        mv = ph.enter_context(nc.sbuf_tensor("p1mv", [128, 2], F32))
        vtmp = ph.enter_context(nc.sbuf_tensor("p1vtmp", [128, 1], F32))
        rstd = ph.enter_context(nc.sbuf_tensor("p1rstd", [128, 1], F32))
        negmr = ph.enter_context(nc.sbuf_tensor("p1negmr", [128, 1], F32))
        pk = [ph.enter_context(nc.psum_tensor(f"p1pk{i}", [128, KVW], F32)) for i in range(2)]
        pv = [ph.enter_context(nc.psum_tensor("p1pv0", [128, KVW], F32))] * 2
        ptr = [ph.enter_context(nc.psum_tensor(f"p1ptr{i}", [128, 1024], BF16)) for i in range(2)]

        t_wk = TB("wk"); t_wv = TB("wv"); t_cosk = TB("cosk"); t_sink = TB("sink")
        t_xt = [TB("p1x0")] * 2
        t_h1 = TB("h1"); t_h1T = TB("h1T")
        t_kro = [TB(f"kro{i}") for i in range(2)]
        t_vsb = [TB(f"vsb{i}") for i in range(2)]
        t_us = TB("u"); t_ss = TB("s")
        t_stats = TB("stats"); t_mv = TB("mv"); t_vt = TB("vtmp"); t_rstd = TB("rstd"); t_negmr = TB("negmr")
        t_pk = [TB(f"pk{i}") for i in range(2)]
        t_pv = [TB("p1pv0")] * 2
        t_ptr = [TB(f"ptr{i}") for i in range(2)]

        nchunk = max(1, D // 512)
        cw = min(512, D)

        for j in range(2):
            h0 = j * (HC // 2); h1_ = (j + 1) * (HC // 2)
            if h0 == h1_:
                continue
            kb.dma("gpsimd", "wk", (lambda h0=h0, h1_=h1_: lambda e: e.dma_start(
                out=wk_sb[:, h0 * KVW: h1_ * KVW].rearrange("p (h n) -> p h n", n=KVW),
                in_=wkT[h0:h1_].rearrange("h p n -> p h n")))(),
                writes=[(t_wk, (0, 128, h0 * KVW, h1_ * KVW))])
            kb.dma("gpsimd", "wv", (lambda h0=h0, h1_=h1_: lambda e: e.dma_start(
                out=wv_sb[:, h0 * KVW: h1_ * KVW].rearrange("p (h n) -> p h n", n=KVW),
                in_=wvT[h0:h1_].rearrange("h p n -> p h n")))(),
                writes=[(t_wv, (0, 128, h0 * KVW, h1_ * KVW))])
        kb.dma("sync", "cosk", lambda e: e.dma_start(out=cosk_sb[:], in_=cosk[:]),
               writes=[(t_cosk, BOXF)])
        kb.dma("sync", "sink", lambda e: e.dma_start(out=sink_sb[:], in_=sink[:]),
               writes=[(t_sink, BOXF)])

        def stageA(gt):
            bi = gt % 2
            xbuf, txb = xt[bi], t_xt[bi]
            pkb, tpkb = pk[bi], t_pk[bi]
            pvb, tpvb = pv[bi], t_pv[bi]
            kb.dma("gpsimd", f"x{bi}", (lambda gt=gt, xbuf=xbuf: lambda e: e.dma_start(
                out=xbuf[:], in_=x_all[gt]))(), writes=[(txb, BOXF)])
            for ch in range(nchunk):
                kb.op("vector", (lambda ch=ch, xbuf=xbuf: lambda v: v.bn_stats(
                    out=stats[:, ch * 6:(ch + 1) * 6], in_=xbuf[:, ch * cw:(ch + 1) * cw]))(),
                    reads=[(txb, BOXF)], writes=[(t_stats, (0, 128, ch * 6, ch * 6 + 6))])
            kb.op("vector", lambda v: v.bn_aggr(out=mv[:], in_=stats[:]),
                  reads=[(t_stats, BOXF)], writes=[(t_mv, BOXF)])
            kb.op("vector", lambda v: v.tensor_scalar(
                out=vtmp[:], in0=mv[:, 1:2], scalar1=LN_EPS, scalar2=None, op0=ALU.add),
                reads=[(t_mv, BOXF)], writes=[(t_vt, BOXF)])
            kb.op("scalar", lambda s: s.activation(out=vtmp[:], in_=vtmp[:], func=AF.Ln),
                  reads=[(t_vt, BOXF)], writes=[(t_vt, BOXF)])
            kb.op("scalar", lambda s: s.activation(out=rstd[:], in_=vtmp[:], func=AF.Exp, scale=-0.5),
                  reads=[(t_vt, BOXF)], writes=[(t_rstd, BOXF)])
            kb.op("vector", lambda v: v.tensor_scalar(
                out=negmr[:], in0=mv[:, 0:1], scalar1=rstd[:], scalar2=-1.0,
                op0=ALU.mult, op1=ALU.mult),
                reads=[(t_mv, BOXF), (t_rstd, BOXF)], writes=[(t_negmr, BOXF)])
            kb.op("scalar", (lambda xbuf=xbuf: lambda sc: sc.activation(
                out=h1[:], in_=xbuf[:], func=AF.Identity,
                scale=rstd[:], bias=negmr[:]))(),
                reads=[(txb, BOXF), (t_rstd, BOXF), (t_negmr, BOXF)],
                writes=[(t_h1, BOXF)])
            for grp in range((HC + 7) // 8):
                pb, tpb = ptr[grp % 2], t_ptr[grp % 2]
                n = min(8, HC - grp * 8)
                for k in range(n):
                    hc = grp * 8 + k
                    kb.op("tensor", (lambda hc=hc, k=k, pb=pb: lambda t: t.transpose(
                        pb[:, k * 128:(k + 1) * 128], h1[:, hc * 128:(hc + 1) * 128], ident[:]))(),
                        reads=[(t_h1, (0, 128, hc * 128, hc * 128 + 128)), (t_ident, BOXF)],
                        writes=[(tpb, (0, 128, k * 128, k * 128 + 128))],
                        inc=(k == n - 1))
                kb.op("scalar", (lambda grp=grp, n=n, pb=pb: lambda sc: sc.copy(
                    out=h1T[:, grp * 1024: grp * 1024 + n * 128], in_=pb[:, 0: n * 128]))(),
                    reads=[(tpb, (0, 128, 0, n * 128))],
                    writes=[(t_h1T, (0, 128, grp * 1024, grp * 1024 + n * 128))])
            for hc in range(HC):
                for kv in range(NKV):
                    kb.op("tensor", (lambda hc=hc, kv=kv, pkb=pkb: lambda t: t.matmul(
                        pkb[:, kv * 128:(kv + 1) * 128],
                        wk_sb[:, (hc * NKV + kv) * 128:(hc * NKV + kv) * 128 + 128],
                        h1T[:, hc * 128:(hc + 1) * 128],
                        start=(hc == 0 and kv % 4 == 0), stop=(hc == HC - 1),
                        skip_group_check=True))(),
                        reads=[(t_wk, (0, 128, (hc * NKV + kv) * 128, (hc * NKV + kv + 1) * 128)),
                               (t_h1T, (0, 128, hc * 128, hc * 128 + 128))],
                        writes=[(tpkb, (0, 128, kv * 128, kv * 128 + 128))],
                        inc=(hc == HC - 1 and kv == NKV - 1))
            for hc in range(HC):
                for oc in range(NVOC):
                    kb.op("tensor", (lambda hc=hc, oc=oc, pvb=pvb: lambda t: t.matmul(
                        pvb[:, oc * VOC:(oc + 1) * VOC],
                        h1T[:, hc * 128:(hc + 1) * 128],
                        wv_sb[:, hc * KVW + oc * VOC: hc * KVW + (oc + 1) * VOC],
                        start=(hc == 0 and (oc * VOC) % 512 == 0), stop=(hc == HC - 1),
                        skip_group_check=True))(),
                        reads=[(t_wv, (0, 128, hc * KVW + oc * VOC, hc * KVW + (oc + 1) * VOC)),
                               (t_h1T, (0, 128, hc * 128, hc * 128 + 128))],
                        writes=[(tpvb, (0, 128, oc * VOC, (oc + 1) * VOC))],
                        inc=(hc == HC - 1 and oc == NVOC - 1))

            vb, tvb = vsb[bi], t_vsb[bi]
            kb.op("scalar", (lambda vb=vb, pvb=pvb: lambda sc: sc.copy(out=vb[:], in_=pvb[:]))(),
                  reads=[(tpvb, BOXF)], writes=[(tvb, BOXF)])
            kb.dma("sync", f"vout{bi}", (lambda gt=gt, vb=vb: lambda e: e.dma_start(
                out=v_s[:, gt].rearrange("k p c -> p k c"), in_=vb.rearrange("p (k c) -> p k c", c=128)))(),
                reads=[(tvb, BOXF)], writes=[(t_v, (0, 128, gt * KVW, (gt + 1) * KVW))])

        def stageB(gt):
            bi = gt % 2
            pkb, tpkb = pk[bi], t_pk[bi]
            ckc = (lambda gt=gt: cosk_sb[:, gt * 128:(gt + 1) * 128])
            skc = (lambda gt=gt: sink_sb[:, gt * 128:(gt + 1) * 128])
            for kv in range(NKV):
                kb.op("vector", (lambda kv=kv, ckc=ckc, pkb=pkb: lambda v: v.tensor_tensor(
                    out=u_sb[:, kv * 128:(kv + 1) * 128], in0=pkb[:, kv * 128:(kv + 1) * 128],
                    in1=ckc(), op=ALU.mult))(),
                    reads=[(tpkb, (0, 128, kv * 128, (kv + 1) * 128)), (t_cosk, BOXF)],
                    writes=[(t_us, (0, 128, kv * 128, (kv + 1) * 128))])
                kb.op("vector", (lambda kv=kv, skc=skc, pkb=pkb: lambda v: v.tensor_tensor(
                    out=s_sb[0:64, kv * 128:(kv + 1) * 128], in0=pkb[64:128, kv * 128:(kv + 1) * 128],
                    in1=skc()[0:64, :], op=ALU.mult))(),
                    reads=[(tpkb, (64, 128, kv * 128, (kv + 1) * 128)), (t_sink, BOXF)],
                    writes=[(t_ss, (0, 64, kv * 128, (kv + 1) * 128))])
                kb.op("vector", (lambda kv=kv, skc=skc, pkb=pkb: lambda v: v.tensor_tensor(
                    out=s_sb[64:128, kv * 128:(kv + 1) * 128], in0=pkb[0:64, kv * 128:(kv + 1) * 128],
                    in1=skc()[64:128, :], op=ALU.mult))(),
                    reads=[(tpkb, (0, 64, kv * 128, (kv + 1) * 128)), (t_sink, BOXF)],
                    writes=[(t_ss, (64, 128, kv * 128, (kv + 1) * 128))])
            kbuf, tkb = kro[bi], t_kro[bi]
            kb.op("vector", (lambda kbuf=kbuf: lambda v: v.tensor_tensor(
                out=kbuf[:], in0=u_sb[:], in1=s_sb[:], op=ALU.add))(),
                reads=[(t_us, BOXF), (t_ss, BOXF)], writes=[(tkb, BOXF)])
            kb.dma("sync", f"kout{bi}", (lambda gt=gt, kbuf=kbuf: lambda e: e.dma_start(
                out=kT_s[:, gt].rearrange("k p c -> p k c"), in_=kbuf.rearrange("p (k c) -> p k c", c=128)))(),
                reads=[(tkb, BOXF)], writes=[(t_kT, (0, 128, gt * KVW, (gt + 1) * KVW))])

        pc_jobs = [("wo", hc) for hc in range(HC)] + [("wb", hc) for hc in range(HC)]
        pc_i = 0

        def precast_some(k):
            nonlocal pc_i
            for _ in range(k):
                if pc_i >= len(pc_jobs):
                    return
                which, hc = pc_jobs[pc_i]; pc_i += 1
                if which == "wo":
                    kb.dma("gpsimd", "pcast", (lambda hc=hc: lambda e: e.dma_start(
                        out=woT_bf[hc], in_=woT[hc]))(),
                        writes=[(t_wobf, (0, 128, hc * D, (hc + 1) * D))])
                else:
                    kb.dma("gpsimd", "pcast", (lambda hc=hc: lambda e: e.dma_start(
                        out=wbT_bf[hc], in_=wbT[hc]))(),
                        writes=[(t_wbbf, (0, 128, hc * D, (hc + 1) * D))])

        for gt in range(NT):
            stageA(gt)
            precast_some((2 * len(pc_jobs) + NT - 1) // NT if NT else 0)
            if gt > 0:
                stageB(gt - 1)
        stageB(NT - 1)
        precast_some(len(pc_jobs))

        if "kT" in dbg:
            kb.dma("sync", "dbgk", lambda e: e.dma_start(out=dbg_out["kT"][:], in_=kT_s[:]),
                   reads=[(t_kT, BOXF)])
        if "v" in dbg:
            kb.dma("sync", "dbgv", lambda e: e.dma_start(out=dbg_out["v"][:], in_=v_s[:]),
                   reads=[(t_v, BOXF)])
        kb.barrier()
    _phase1()

    def _phase3():
      with ExitStack() as ph:
        xt = [ph.enter_context(nc.sbuf_tensor(f"p3x{i}", [128, D], F32)) for i in range(2)]
        h1 = ph.enter_context(nc.sbuf_tensor("p3h1", [128, D], BF16))
        h1T = ph.enter_context(nc.sbuf_tensor("p3h1T", [128, HC * OWN], BF16))
        wq_sb = [ph.enter_context(nc.sbuf_tensor(f"p3wq{i}", [128, HC * 128], BF16)) for i in range(2)]
        cosq_sb = ph.enter_context(nc.sbuf_tensor("p3cosq", [128, OWN], F32))
        sinq_sb = ph.enter_context(nc.sbuf_tensor("p3sinq", [128, OWN], F32))
        u_sb = ph.enter_context(nc.sbuf_tensor("p3u", [128, TCW], BF16))
        s_sb = ph.enter_context(nc.sbuf_tensor("p3s", [128, TCW], BF16))
        qro = [ph.enter_context(nc.sbuf_tensor(f"p3qro{i}", [128, TCW], BF16)) for i in range(2)]
        stats = ph.enter_context(nc.sbuf_tensor("p3stats", [128, 6 * max(1, D // 512)], F32))
        mv = ph.enter_context(nc.sbuf_tensor("p3mv", [128, 2], F32))
        vtmp = ph.enter_context(nc.sbuf_tensor("p3vtmp", [128, 1], F32))
        rstd = ph.enter_context(nc.sbuf_tensor("p3rstd", [128, 1], F32))
        negmr = ph.enter_context(nc.sbuf_tensor("p3negmr", [128, 1], F32))
        pq = [ph.enter_context(nc.psum_tensor(f"p3pq{i}", [128, TCW], F32)) for i in range(2)]
        ptr = [ph.enter_context(nc.psum_tensor(f"p3ptr{i}", [128, 1024], BF16)) for i in range(2)]

        t_xt = [TB(f"p3xt{i}") for i in range(2)]
        t_h1 = TB("p3h1"); t_h1T = TB("p3h1T")
        t_wq = [TB(f"p3wq{i}") for i in range(2)]
        t_cq = TB("p3cosq"); t_sq = TB("p3sinq")
        t_us = TB("p3u"); t_ss = TB("p3s")
        t_qro = [TB(f"p3qro{i}") for i in range(2)]
        t_stats = TB("p3stats"); t_mv = TB("p3mv"); t_vt = TB("p3vt")
        t_rstd = TB("p3rstd"); t_negmr = TB("p3negmr")
        t_pq = [TB(f"p3pq{i}") for i in range(2)]
        t_ptr = [TB(f"p3ptr{i}") for i in range(2)]

        nchunk = max(1, D // 512)
        cw = min(512, D)
        h1T_v = h1T.rearrange("p (h c) -> p h c", c=OWN)

        kb.dma("sync", "cosq", lambda e: e.dma_start(out=cosq_sb[:], in_=cosq[:]),
               writes=[(t_cq, BOXF)])
        kb.dma("sync", "sinq", lambda e: e.dma_start(out=sinq_sb[:], in_=sinq[:]),
               writes=[(t_sq, BOXF)])

        for sl in range(NSLOT):
            bi = sl % 2
            xbuf, txb = xt[bi], t_xt[bi]
            kb.dma("sync", f"p3x{bi}", (lambda sl=sl, xbuf=xbuf: lambda e: e.dma_start(
                out=xbuf[:], in_=x_own[sl]))(), writes=[(txb, BOXF)])
            for ch in range(nchunk):
                kb.op("vector", (lambda ch=ch, xbuf=xbuf: lambda v: v.bn_stats(
                    out=stats[:, ch * 6:(ch + 1) * 6], in_=xbuf[:, ch * cw:(ch + 1) * cw]))(),
                    reads=[(txb, BOXF)], writes=[(t_stats, (0, 128, ch * 6, ch * 6 + 6))])
            kb.op("vector", lambda v: v.bn_aggr(out=mv[:], in_=stats[:]),
                  reads=[(t_stats, BOXF)], writes=[(t_mv, BOXF)])
            kb.op("vector", lambda v: v.tensor_scalar(
                out=vtmp[:], in0=mv[:, 1:2], scalar1=LN_EPS, scalar2=None, op0=ALU.add),
                reads=[(t_mv, BOXF)], writes=[(t_vt, BOXF)])
            kb.op("scalar", lambda s: s.activation(out=vtmp[:], in_=vtmp[:], func=AF.Ln),
                  reads=[(t_vt, BOXF)], writes=[(t_vt, BOXF)])
            kb.op("scalar", lambda s: s.activation(out=rstd[:], in_=vtmp[:], func=AF.Exp, scale=-0.5),
                  reads=[(t_vt, BOXF)], writes=[(t_rstd, BOXF)])
            kb.op("vector", lambda v: v.tensor_scalar(
                out=negmr[:], in0=mv[:, 0:1], scalar1=rstd[:], scalar2=-1.0,
                op0=ALU.mult, op1=ALU.mult),
                reads=[(t_mv, BOXF), (t_rstd, BOXF)], writes=[(t_negmr, BOXF)])
            kb.op("scalar", (lambda xbuf=xbuf: lambda sc: sc.activation(
                out=h1[:], in_=xbuf[:], func=AF.Identity,
                scale=rstd[:], bias=negmr[:]))(),
                reads=[(txb, BOXF), (t_rstd, BOXF), (t_negmr, BOXF)],
                writes=[(t_h1, BOXF)])
            for grp in range((HC + 7) // 8):
                pb, tpb = ptr[grp % 2], t_ptr[grp % 2]
                n = min(8, HC - grp * 8)
                for k in range(n):
                    hc = grp * 8 + k
                    kb.op("tensor", (lambda hc=hc, k=k, pb=pb: lambda t: t.transpose(
                        pb[:, k * 128:(k + 1) * 128], h1[:, hc * 128:(hc + 1) * 128], ident[:]))(),
                        reads=[(t_h1, (0, 128, hc * 128, hc * 128 + 128)), (t_ident, BOXF)],
                        writes=[(tpb, (0, 128, k * 128, k * 128 + 128))],
                        inc=(k == n - 1))
                g0 = grp * 8
                kb.op("scalar", (lambda g0=g0, n=n, pb=pb, sl=sl: lambda sc: sc.copy(
                    out=h1T_v[:, g0:g0 + n, sl * 128:sl * 128 + 128],
                    in_=pb[:, 0:n * 128].rearrange("p (h c) -> p h c", c=128)))(),
                    reads=[(tpb, (0, 128, 0, n * 128))],
                    writes=[(t_h1T, (0, 128, 0, 1 << 30))] if False else
                           [(t_h1T, (0, 128, g0 * OWN + sl * 128, (g0 + n - 1) * OWN + sl * 128 + 128))])

        for h in range(NH):
            bi = h % 2
            wq, twq = wq_sb[bi], t_wq[bi]
            kb.dma("gpsimd", f"p3wq{bi}", (lambda h=h, wq=wq: lambda e: e.dma_start(
                out=wq.rearrange("p (k n) -> p k n", n=128),
                in_=wqT[:, :, h * 128:(h + 1) * 128].rearrange("k p n -> p k n")))(),
                writes=[(twq, BOXF)])
            kv, j = h // 4, h % 4
            for tc in range(NTC):
                pqb, tpq = pq[tc % 2], t_pq[tc % 2]
                for hc in range(HC):
                    kb.op("tensor", (lambda hc=hc, tc=tc, pqb=pqb, wq=wq: lambda t: t.matmul(
                        pqb[:], wq[:, hc * 128:(hc + 1) * 128],
                        h1T[:, hc * OWN + tc * TCW: hc * OWN + (tc + 1) * TCW],
                        start=(hc == 0), stop=(hc == HC - 1)))(),
                        reads=[(twq, (0, 128, hc * 128, hc * 128 + 128)),
                               (t_h1T, (0, 128, hc * OWN + tc * TCW, hc * OWN + (tc + 1) * TCW))],
                        writes=[(tpq, BOXF)],
                        inc=(hc == HC - 1))
                kb.op("vector", (lambda pqb=pqb, tc=tc: lambda v: v.tensor_tensor(
                    out=u_sb[:], in0=pqb[:], in1=cosq_sb[:, tc * TCW:(tc + 1) * TCW],
                    op=ALU.mult))(),
                    reads=[(tpq, BOXF), (t_cq, BOXF)], writes=[(t_us, BOXF)])
                kb.op("vector", (lambda pqb=pqb, tc=tc: lambda v: v.tensor_tensor(
                    out=s_sb[0:64, :], in0=pqb[64:128, :],
                    in1=sinq_sb[0:64, tc * TCW:(tc + 1) * TCW], op=ALU.mult))(),
                    reads=[(tpq, BOXF), (t_sq, BOXF)], writes=[(t_ss, (0, 64, 0, TCW))])
                kb.op("vector", (lambda pqb=pqb, tc=tc: lambda v: v.tensor_tensor(
                    out=s_sb[64:128, :], in0=pqb[0:64, :],
                    in1=sinq_sb[64:128, tc * TCW:(tc + 1) * TCW], op=ALU.mult))(),
                    reads=[(tpq, BOXF), (t_sq, BOXF)], writes=[(t_ss, (64, 128, 0, TCW))])
                qb, tqb = qro[tc % 2], t_qro[tc % 2]
                kb.op("vector", (lambda qb=qb: lambda v: v.tensor_tensor(
                    out=qb[:], in0=u_sb[:], in1=s_sb[:], op=ALU.add))(),
                    reads=[(t_us, BOXF), (t_ss, BOXF)], writes=[(tqb, BOXF)])
                ns = TCW // 128
                s0 = tc * ns
                kb.dma("sync", f"p3q{tc % 2}", (lambda qb=qb, s0=s0, ns=ns, kv=kv, j=j:
                    lambda e: e.dma_start(
                        out=qT4_s[s0:s0 + ns, kv, :, j * 128:(j + 1) * 128].rearrange("s p c -> p s c"),
                        in_=qb.rearrange("p (s c) -> p s c", c=128)))(),
                    reads=[(tqb, BOXF)],
                    writes=[(t_q4, (0, 128, (s0 * NKV + kv) * 512 + j * 128,
                                    ((s0 + ns - 1) * NKV + kv) * 512 + (j + 1) * 128))])

        if "qT4" in dbg:
            kb.dma("sync", "dbgq", lambda e: e.dma_start(out=dbg_out["qT4"][:], in_=qT4_s[:]),
                   reads=[(t_q4, BOXF)])
        kb.barrier()
    _phase3()

    def _phase4():
      with ExitStack() as ph:
        kt_sb = [ph.enter_context(nc.sbuf_tensor(f"p4k{i}", [128, NT * 128], BF16)) for i in range(2)]
        v_sb = [ph.enter_context(nc.sbuf_tensor(f"p4v{i}", [128, NT * 128], BF16)) for i in range(2)]
        q4 = [ph.enter_context(nc.sbuf_tensor(f"p4q{i}", [128, 512], BF16)) for i in range(2)]
        msk = ph.enter_context(nc.sbuf_tensor("p4m", [128, NSLOT * 2 * 512], BF16))
        exp_sb = [ph.enter_context(nc.sbuf_tensor(f"p4e{i}", [128, 1024], BF16)) for i in range(3)]
        expm = ph.enter_context(nc.sbuf_tensor("p4em", [128, 1024], BF16))
        dls = ph.enter_context(nc.sbuf_tensor("p4dls", [128, 512], F32))
        dinv = [ph.enter_context(nc.sbuf_tensor(f"p4dinv{i}", [128, 512], F32)) for i in range(2)]
        avo = [ph.enter_context(nc.sbuf_tensor(f"p4av{i}", [128, 512], BF16)) for i in range(2)]
        ps = [ph.enter_context(nc.psum_tensor(f"p4ps{i}", [128, 1024], F32)) for i in range(3)]
        pav = [ph.enter_context(nc.psum_tensor("p4pav0", [128, 512], F32))] * 2
        pd = [ph.enter_context(nc.psum_tensor("p4pd0", [128, 512], F32))] * 2

        t_k = [TB(f"p4k{i}") for i in range(2)]
        t_vv = [TB(f"p4v{i}") for i in range(2)]
        t_qq = [TB(f"p4q{i}") for i in range(2)]
        t_m = TB("p4m")
        t_e = [TB(f"p4e{i}") for i in range(3)]
        t_em = TB("p4em")
        t_dls = TB("p4dls"); t_dinv = [TB(f"p4dinv{i}") for i in range(2)]
        t_avo = [TB(f"p4avo{i}") for i in range(2)]
        t_ps = [TB(f"p4ps{i}") for i in range(3)]
        t_pav = [TB("p4pav0")] * 2
        t_pd = [TB("p4pd0")] * 2

        kb.dma("sync", "p4m", lambda e: e.dma_start(
            out=msk.rearrange("p (s j c) -> p s j c", j=2, c=512),
            in_=masks.rearrange("s j p c -> p s j c")),
            writes=[(t_m, BOXF)])

        def kvload(kv2):
            kbuf2, tkb2 = kt_sb[kv2 % 2], t_k[kv2 % 2]
            vbuf2, tvb2 = v_sb[kv2 % 2], t_vv[kv2 % 2]
            kb.dma("sync", f"p4kt{kv2 % 2}", (lambda kv2=kv2, kbuf2=kbuf2: lambda e: e.dma_start(
                out=kbuf2.rearrange("p (t c) -> p t c", c=128),
                in_=kT_s[kv2].rearrange("t p c -> p t c")))(),
                reads=[(t_kT, (0, 128, kv2 * NT * 128, (kv2 + 1) * NT * 128))],
                writes=[(tkb2, BOXF)])
            kb.dma("sync", f"p4vt{kv2 % 2}", (lambda kv2=kv2, vbuf2=vbuf2: lambda e: e.dma_start(
                out=vbuf2.rearrange("p (t c) -> p t c", c=128),
                in_=v_s[kv2].rearrange("t p c -> p t c")))(),
                reads=[(t_v, (0, 128, kv2 * NT * 128, (kv2 + 1) * NT * 128))],
                writes=[(tvb2, BOXF)])

        kvload(0)
        for kv in range(NKV):
            kbi = kv % 2
            kbuf, tkb = kt_sb[kbi], t_k[kbi]
            vbuf, tvb = v_sb[kbi], t_vv[kbi]
            def qload(sl2, kv2):
                qbuf2, tqb2 = q4[sl2 % 2], t_qq[sl2 % 2]
                kb.dma("sync", f"p4q{sl2 % 2}", (lambda sl2=sl2, kv2=kv2, qbuf2=qbuf2: lambda e: e.dma_start(
                    out=qbuf2[:], in_=qT4_s[sl2, kv2]))(),
                    reads=[(t_q4, (0, 128, (sl2 * NKV + kv2) * 512, (sl2 * NKV + kv2 + 1) * 512))],
                    writes=[(tqb2, BOXF)])

            qload(0, kv)
            for sl in range(NSLOT):
                qbi = sl % 2
                qbuf, tqb = q4[qbi], t_qq[qbi]
                if sl + 1 < NSLOT:
                    qload(sl + 1, kv)
                if sl == NSLOT // 2 and kv + 1 < NKV:
                    kvload(kv + 1)
                KT = 2 * (sl + 1)
                pavb, tpavb = pav[qbi], t_pav[qbi]
                pdb, tpdb = pd[qbi], t_pd[qbi]
                npair = KT // 2

                def scores_pair(pr):
                    pp = pr % 3
                    psb, tpsb = ps[pp], t_ps[pp]
                    for half in range(2):
                        kt = 2 * pr + half
                        kb.op("tensor", (lambda kt=kt, half=half, psb=psb, kbuf=kbuf, qbuf=qbuf:
                            lambda t: t.matmul(
                                psb[:, half * 512:(half + 1) * 512],
                                kbuf[:, kt * 128:(kt + 1) * 128], qbuf[:],
                                start=True, stop=True, skip_group_check=True))(),
                            reads=[(tkb, (0, 128, kt * 128, (kt + 1) * 128)), (tqb, BOXF)],
                            writes=[(tpsb, (0, 128, half * 512, (half + 1) * 512))],
                            inc=(half == 1))
                    eb, teb = exp_sb[pp], t_e[pp]
                    kb.op("scalar", (lambda psb=psb, eb=eb: lambda s: s.activation(
                        out=eb[:], in_=psb[:], func=AF.Exp,
                        bias=shift[:], scale=float(ISQ)))(),
                        reads=[(tpsb, BOXF), (t_shift, BOXF)],
                        writes=[(teb, BOXF)])

                def av_pair(pr):
                    pp = pr % 3
                    eb, teb = exp_sb[pp], t_e[pp]
                    if pr == npair - 1:
                        mc = sl * 2 * 512
                        kb.op("vector", (lambda eb=eb, mc=mc: lambda v: v.tensor_tensor(
                            out=expm[:], in0=eb[:], in1=msk[:, mc:mc + 1024],
                            op=ALU.mult))(),
                            reads=[(teb, BOXF), (t_m, (0, 128, mc, mc + 1024))],
                            writes=[(t_em, BOXF)])
                        rhs_b, trhs = expm, t_em
                    else:
                        rhs_b, trhs = eb, teb
                    for half in range(2):
                        kt = 2 * pr + half
                        kb.op("tensor", (lambda kt=kt, KT=KT, half=half, vbuf=vbuf, rhs_b=rhs_b, pavb=pavb:
                            lambda t: t.matmul(
                                pavb[:], vbuf[:, kt * 128:(kt + 1) * 128],
                                rhs_b[:, half * 512:(half + 1) * 512],
                                start=(kt == 0), stop=(kt == KT - 1)))(),
                            reads=[(tvb, (0, 128, kt * 128, (kt + 1) * 128)),
                                   (trhs, (0, 128, half * 512, (half + 1) * 512))],
                            writes=[(tpavb, BOXF)], inc=(kt == KT - 1))
                        kb.op("tensor", (lambda kt=kt, KT=KT, half=half, rhs_b=rhs_b, pdb=pdb:
                            lambda t: t.matmul(
                                pdb[:], ones[:], rhs_b[:, half * 512:(half + 1) * 512],
                                start=(kt == 0), stop=(kt == KT - 1)))(),
                            reads=[(t_ones, BOXF),
                                   (trhs, (0, 128, half * 512, (half + 1) * 512))],
                            writes=[(tpdb, BOXF)], inc=(kt == KT - 1))

                for pr in range(min(2, npair)):
                    scores_pair(pr)
                for pr in range(npair):
                    if pr + 2 < npair:
                        scores_pair(pr + 2)
                    av_pair(pr)
                kb.op("scalar", (lambda pdb=pdb: lambda s: s.activation(
                    out=dls[:], in_=pdb[:], func=AF.Ln))(),
                    reads=[(tpdb, BOXF)], writes=[(t_dls, BOXF)])
                dib, tdib = dinv[qbi], t_dinv[qbi]
                kb.op("scalar", (lambda dib=dib: lambda s: s.activation(
                    out=dib[:], in_=dls[:], func=AF.Exp, scale=-1.0))(),
                    reads=[(t_dls, BOXF)], writes=[(tdib, BOXF)])
                ab, tab = avo[qbi], t_avo[qbi]
                kb.op("vector", (lambda pavb=pavb, ab=ab, dib=dib: lambda v: v.tensor_tensor(
                    out=ab[:], in0=pavb[:], in1=dib[:], op=ALU.mult))(),
                    reads=[(tpavb, BOXF), (tdib, BOXF)], writes=[(tab, BOXF)])
                kb.dma("sync", f"p4av{qbi}", (lambda sl=sl, kv=kv, ab=ab: lambda e: e.dma_start(
                    out=avT_s[sl, kv], in_=ab[:]))(),
                    reads=[(tab, BOXF)],
                    writes=[(t_avT, (0, 128, (sl * NKV + kv) * 512, (sl * NKV + kv + 1) * 512))])

        if "avT" in dbg:
            kb.dma("sync", "dbgav", lambda e: e.dma_start(out=dbg_out["avT"][:], in_=avT_s[:]),
                   reads=[(t_avT, BOXF)])
        kb.barrier()
    _phase4()

    SPT5 = min(4, NSLOT)
    NTTB5 = NSLOT // SPT5

    def _phase5():
      with ExitStack() as ph:
        avT_sb = ph.enter_context(nc.sbuf_tensor("p5avT", [128, SPT5 * NKV * 512], BF16))
        wo_sb = [ph.enter_context(nc.sbuf_tensor(f"p5wo{i}", [128, (NH // 2 if NH > 1 else 1) * OCW], BF16)) for i in range(2)]
        a_sb = [ph.enter_context(nc.sbuf_tensor(f"p5a{i}", [128, D], BF16)) for i in range(2 * SPT5)]
        xo = [ph.enter_context(nc.sbuf_tensor("p5xo0", [128, D], F32))] * 2
        t1 = ph.enter_context(nc.sbuf_tensor("p5t1", [128, D], F32))
        x2b = [ph.enter_context(nc.sbuf_tensor("p5x20", [128, D], F32))] * 2
        h2 = [ph.enter_context(nc.sbuf_tensor(f"p5h2_{i}", [128, D], BF16)) for i in range(2)]
        h2T = ph.enter_context(nc.sbuf_tensor("p5h2T", [128, HC * 128], BF16))
        stats = ph.enter_context(nc.sbuf_tensor("p5stats", [128, 6 * max(1, D // 512)], F32))
        mv = ph.enter_context(nc.sbuf_tensor("p5mv", [128, 2], F32))
        vtmp = ph.enter_context(nc.sbuf_tensor("p5vtmp", [128, 1], F32))
        rstd = ph.enter_context(nc.sbuf_tensor("p5rstd", [128, 1], F32))
        negmr = ph.enter_context(nc.sbuf_tensor("p5negmr", [128, 1], F32))
        pw = [ph.enter_context(nc.psum_tensor(f"p5pw{i}", [128, OCW], F32)) for i in range(SPT5)]

        t_avs = TB("p5avT"); t_wo = [TB(f"p5wo{i}") for i in range(2)]
        t_a = [TB(f"p5a{i}") for i in range(2 * SPT5)]
        t_xo = [TB("p5xo0")] * 2
        t_t1 = TB("p5t1"); t_x2b = [TB("p5x20")] * 2
        t_h2 = [TB(f"p5h2_{i}") for i in range(2)]; t_h2Tb = TB("p5h2T")
        t_stats = TB("p5stats"); t_mv = TB("p5mv"); t_vt = TB("p5vt")
        t_rstd = TB("p5rstd"); t_negmr = TB("p5negmr")
        t_pw = [TB(f"p5pw{i}") for i in range(SPT5)]

        nchunk = max(1, D // 512)
        cw = min(512, D)

        def ln_chain(src_buf, t_src, out_buf, t_out, out_dt_note=None):
            for ch in range(nchunk):
                kb.op("vector", (lambda ch=ch: lambda v: v.bn_stats(
                    out=stats[:, ch * 6:(ch + 1) * 6], in_=src_buf[:, ch * cw:(ch + 1) * cw]))(),
                    reads=[(t_src, BOXF)], writes=[(t_stats, (0, 128, ch * 6, ch * 6 + 6))])
            kb.op("vector", lambda v: v.bn_aggr(out=mv[:], in_=stats[:]),
                  reads=[(t_stats, BOXF)], writes=[(t_mv, BOXF)])
            kb.op("vector", lambda v: v.tensor_scalar(
                out=vtmp[:], in0=mv[:, 1:2], scalar1=LN_EPS, scalar2=None, op0=ALU.add),
                reads=[(t_mv, BOXF)], writes=[(t_vt, BOXF)])
            kb.op("scalar", lambda s: s.activation(out=vtmp[:], in_=vtmp[:], func=AF.Ln),
                  reads=[(t_vt, BOXF)], writes=[(t_vt, BOXF)])
            kb.op("scalar", lambda s: s.activation(out=rstd[:], in_=vtmp[:], func=AF.Exp, scale=-0.5),
                  reads=[(t_vt, BOXF)], writes=[(t_rstd, BOXF)])
            kb.op("vector", lambda v: v.tensor_scalar(
                out=negmr[:], in0=mv[:, 0:1], scalar1=rstd[:], scalar2=-1.0,
                op0=ALU.mult, op1=ALU.mult),
                reads=[(t_mv, BOXF), (t_rstd, BOXF)], writes=[(t_negmr, BOXF)])
            kb.op("scalar", lambda sc: sc.activation(
                out=out_buf[:], in_=src_buf[:], func=AF.Identity,
                scale=rstd[:], bias=negmr[:]),
                reads=[(t_src, BOXF), (t_rstd, BOXF), (t_negmr, BOXF)],
                writes=[(t_out, BOXF)])

        def load_avt(ttb):
            for s in range(SPT5):
                sl = ttb * SPT5 + s
                kb.dma("sync", f"p5av{s % 2}", (lambda sl=sl, s=s: lambda e: e.dma_start(
                    out=avT_sb[:, s * NKV * 512:(s + 1) * NKV * 512].rearrange("p (k c) -> p k c", c=512),
                    in_=avT_s[sl].rearrange("k p c -> p k c")))(),
                    reads=[(t_avT, (0, 128, sl * NKV * 512, (sl + 1) * NKV * 512))],
                    writes=[(t_avs, (0, 128, s * NKV * 512, (s + 1) * NKV * 512))])

        def ln_stage5(ttb):
            for s in range(SPT5):
                sl = ttb * SPT5 + s
                si = (ttb % 2) * SPT5 + s
                xb_, txb = xo[s % 2], t_xo[s % 2]
                kb.dma("gpsimd", f"p5x{s % 2}", (lambda sl=sl, xb_=xb_: lambda e: e.dma_start(
                    out=xb_[:], in_=x_own[sl]))(), writes=[(txb, BOXF)])
                ln_chain(a_sb[si], t_a[si], t1, t_t1)
                x2buf, tx2 = x2b[s % 2], t_x2b[s % 2]
                kb.op("gpsimd", (lambda x2buf=x2buf, xb_=xb_: lambda g: g.tensor_tensor(
                    out=x2buf[:], in0=t1[:], in1=xb_[:], op=ALU.add))(),
                    reads=[(t_t1, BOXF), (txb, BOXF)], writes=[(tx2, BOXF)])
                kb.dma("gpsimd", f"p5x2o{s % 2}", (lambda sl=sl, x2buf=x2buf: lambda e: e.dma_start(
                    out=x2_s[sl], in_=x2buf[:]))(),
                    reads=[(tx2, BOXF)],
                    writes=[(t_x2, (0, 128, sl * D, (sl + 1) * D))])
                h2b, t_h2b = h2[s % 2], t_h2[s % 2]
                ln_chain(x2buf, tx2, h2b, t_h2b)
                for hc in range(HC):
                    q = "scalar"
                    kb.dma(q, f"p5tr{q}{hc % 2}", (lambda hc=hc, h2b=h2b: lambda e: e.dma_start(
                        out=h2T[:, hc * 128:(hc + 1) * 128],
                        in_=h2b[:, hc * 128:(hc + 1) * 128], transpose=True))(),
                        reads=[(t_h2b, (0, 128, hc * 128, hc * 128 + 128))],
                        writes=[(t_h2Tb, (0, 128, hc * 128, hc * 128 + 128))])
                kb.dma("scalar", f"p5h2T{s % 2}", (lambda sl=sl: lambda e: e.dma_start(
                    out=h2T_s[sl].rearrange("h p c -> p h c"),
                    in_=h2T.rearrange("p (h c) -> p h c", c=128)))(),
                    reads=[(t_h2Tb, BOXF)],
                    writes=[(t_h2T, (0, 128, sl * D, (sl + 1) * D))])

        load_avt(0)
        for ttb in range(NTTB5):
            NHH = max(1, NH // 2)
            for oc in range(NOC):
                for hf in range(2 if NH > 1 else 1):
                    bi5 = (oc * 2 + hf) % 2
                    wob, twob = wo_sb[bi5], t_wo[bi5]
                    kb.dma("sync", f"p5wo{bi5}", (lambda oc=oc, hf=hf, wob=wob: lambda e: e.dma_start(
                        out=wob.rearrange("p (h n) -> p h n", n=OCW),
                        in_=woT_bf[hf * NHH:(hf + 1) * NHH, :, oc * OCW:(oc + 1) * OCW].rearrange("h p n -> p h n")))(),
                        reads=[(t_wobf, (0, 128, hf * NHH * D + oc * OCW, ((hf * NHH + NHH - 1) * D) + (oc + 1) * OCW))],
                        writes=[(twob, BOXF)])
                    for hc in range(hf * NHH, hf * NHH + NHH):
                        for s in range(SPT5):
                            cb = (s * NKV * 512) + (hc // 4) * 512 + (hc % 4) * 128
                            kb.op("tensor", (lambda hc=hc, hf=hf, s=s, cb=cb, wob=wob: lambda t: t.matmul(
                                pw[s][:], avT_sb[:, cb:cb + 128],
                                wob[:, (hc - hf * NHH) * OCW:(hc - hf * NHH + 1) * OCW],
                                start=(hc == 0), stop=(hc == NH - 1)))(),
                                reads=[(t_avs, (0, 128, cb, cb + 128)),
                                       (twob, (0, 128, (hc - hf * NHH) * OCW, (hc - hf * NHH + 1) * OCW))],
                                writes=[(t_pw[s], BOXF)],
                                inc=(hc == NH - 1))
                for s in range(SPT5):
                    si = (ttb % 2) * SPT5 + s
                    kb.op("vector", (lambda si=si, s=s, oc=oc: lambda v: v.tensor_copy(
                        out=a_sb[si][:, oc * OCW:(oc + 1) * OCW], in_=pw[s][:]))(),
                        reads=[(t_pw[s], BOXF)],
                        writes=[(t_a[si], (0, 128, oc * OCW, (oc + 1) * OCW))])
            if ttb + 1 < NTTB5:
                load_avt(ttb + 1)
            if ttb > 0:
                ln_stage5(ttb - 1)
            if ttb == NTTB5 - 1:
                ln_stage5(ttb)

        if "x2" in dbg:
            kb.dma("sync", "dbgx2", lambda e: e.dma_start(out=dbg_out["x2"][:], in_=x2_s[:]),
                   reads=[(t_x2, BOXF)])
        if "h2T" in dbg:
            kb.dma("sync", "dbgh2T", lambda e: e.dma_start(out=dbg_out["h2T"][:], in_=h2T_s[:]),
                   reads=[(t_h2T, BOXF)])
        kb.barrier()
    if cfg.get("stop_after", 9) >= 5:
        _phase5()

    TW6 = SPT5 * 128

    def _phase6():
      with ExitStack() as ph:
        h2T_sb = [ph.enter_context(nc.sbuf_tensor(f"p6h2T{i}", [128, HC * TW6], BF16)) for i in range(2)]
        wb_sb = [ph.enter_context(nc.sbuf_tensor(f"p6wb{i}", [128, max(1, HC // 4) * OCW], BF16)) for i in range(2)]
        f_sb = [ph.enter_context(nc.sbuf_tensor(f"p6f{i}", [128, D], BF16)) for i in range(2 * SPT5)]
        aT_sb = ph.enter_context(nc.sbuf_tensor("p6aT", [128, HC * 32], BF16))
        bT_sb = ph.enter_context(nc.sbuf_tensor("p6bT", [32, D], BF16))
        rw_sb = ph.enter_context(nc.sbuf_tensor("p6rw", [128, HC * 4], BF16))
        rb_sb = ph.enter_context(nc.sbuf_tensor("p6rb", [128, 4], F32))
        pr_sb = ph.enter_context(nc.sbuf_tensor("p6pr", [128, 4], F32))
        e2_sb = ph.enter_context(nc.sbuf_tensor("p6e2", [4, 32], BF16))
        bb_sb = ph.enter_context(nc.sbuf_tensor("p6bb", [128, D], BF16))
        rl = ph.enter_context(nc.sbuf_tensor("p6rl", [128, 4], F32))
        re = ph.enter_context(nc.sbuf_tensor("p6re", [128, 4], F32))
        rp = ph.enter_context(nc.sbuf_tensor("p6rp", [128, 4], F32))
        rs = ph.enter_context(nc.sbuf_tensor("p6rs", [128, 1], F32))
        rr = ph.enter_context(nc.sbuf_tensor("p6rr", [128, 1], F32))
        alpha = ph.enter_context(nc.sbuf_tensor("p6alpha", [128, 4], BF16))
        alpha_fm = ph.enter_context(nc.sbuf_tensor("p6afm", [4, TW6], BF16))
        zf = ph.enter_context(nc.sbuf_tensor("p6zf", [32, TW6], BF16))
        zs = ph.enter_context(nc.sbuf_tensor("p6zs", [32, TW6], BF16))
        x2i = [ph.enter_context(nc.sbuf_tensor("p6x20", [128, D], F32))] * 2
        t1 = ph.enter_context(nc.sbuf_tensor("p6t1", [128, D], BF16))
        yb = [ph.enter_context(nc.sbuf_tensor("p6y0", [128, D], F32))] * 2
        stats = ph.enter_context(nc.sbuf_tensor("p6stats", [128, 6 * max(1, D // 512)], F32))
        mv = ph.enter_context(nc.sbuf_tensor("p6mv", [128, 2], F32))
        vtmp = ph.enter_context(nc.sbuf_tensor("p6vtmp", [128, 1], F32))
        rstd = ph.enter_context(nc.sbuf_tensor("p6rstd", [128, 1], F32))
        negmr = ph.enter_context(nc.sbuf_tensor("p6negmr", [128, 1], F32))
        pw = [ph.enter_context(nc.psum_tensor(f"p6pw{i}", [128, OCW], F32)) for i in range(SPT5)]
        prt = ph.enter_context(nc.psum_tensor("p6prt", [128, 512], F32))
        pal = ph.enter_context(nc.psum_tensor("p6pal", [128, 1024], BF16))
        pae = ph.enter_context(nc.psum_tensor("p6pae", [128, 512], F32))
        pzz = ph.enter_context(nc.psum_tensor("p6pzz", [128, 512], F32))

        t_h2s = [TB(f"p6h2T{i}") for i in range(2)]; t_wb = [TB(f"p6wb{i}") for i in range(2)]
        t_f = [TB(f"p6f{i}") for i in range(2 * SPT5)]
        t_aT = TB("p6aT"); t_bT = TB("p6bT"); t_rw = TB("p6rw"); t_rb = TB("p6rb")
        t_pr = TB("p6pr"); t_e2 = TB("p6e2"); t_bb = TB("p6bb")
        t_rl = TB("p6rl"); t_re = TB("p6re"); t_rp = TB("p6rp"); t_rs = TB("p6rs"); t_rr = TB("p6rr")
        t_al = TB("p6alpha"); t_afm = TB("p6afm"); t_zf = TB("p6zf"); t_zs = TB("p6zs")
        t_x2i = [TB("p6x20")] * 2
        t_t1 = TB("p6t1"); t_yb = [TB("p6y0")] * 2
        t_stats = TB("p6stats"); t_mv = TB("p6mv"); t_vt = TB("p6vt")
        t_rstd = TB("p6rstd"); t_negmr = TB("p6negmr")
        t_pw = [TB(f"p6pw{i}") for i in range(SPT5)]
        t_prt = TB("p6prt"); t_pal = TB("p6pal"); t_pae = TB("p6pae"); t_pzz = TB("p6pzz")

        nchunk = max(1, D // 512)
        cw = min(512, D)
        h2T_v = [b.rearrange("p (h w) -> p h w", w=TW6) for b in h2T_sb]

        def ln_chain6(src_buf, t_src, out_buf, t_out):
            for ch in range(nchunk):
                kb.op("vector", (lambda ch=ch: lambda v: v.bn_stats(
                    out=stats[:, ch * 6:(ch + 1) * 6], in_=src_buf[:, ch * cw:(ch + 1) * cw]))(),
                    reads=[(t_src, BOXF)], writes=[(t_stats, (0, 128, ch * 6, ch * 6 + 6))])
            kb.op("vector", lambda v: v.bn_aggr(out=mv[:], in_=stats[:]),
                  reads=[(t_stats, BOXF)], writes=[(t_mv, BOXF)])
            kb.op("vector", lambda v: v.tensor_scalar(
                out=vtmp[:], in0=mv[:, 1:2], scalar1=LN_EPS, scalar2=None, op0=ALU.add),
                reads=[(t_mv, BOXF)], writes=[(t_vt, BOXF)])
            kb.op("scalar", lambda s: s.activation(out=vtmp[:], in_=vtmp[:], func=AF.Ln),
                  reads=[(t_vt, BOXF)], writes=[(t_vt, BOXF)])
            kb.op("scalar", lambda s: s.activation(out=rstd[:], in_=vtmp[:], func=AF.Exp, scale=-0.5),
                  reads=[(t_vt, BOXF)], writes=[(t_rstd, BOXF)])
            kb.op("vector", lambda v: v.tensor_scalar(
                out=negmr[:], in0=mv[:, 0:1], scalar1=rstd[:], scalar2=-1.0,
                op0=ALU.mult, op1=ALU.mult),
                reads=[(t_mv, BOXF), (t_rstd, BOXF)], writes=[(t_negmr, BOXF)])
            kb.op("scalar", lambda sc: sc.activation(
                out=out_buf[:], in_=src_buf[:], func=AF.Identity,
                scale=rstd[:], bias=negmr[:]),
                reads=[(t_src, BOXF), (t_rstd, BOXF), (t_negmr, BOXF)],
                writes=[(t_out, BOXF)])

        kb.dma("gpsimd", "p6aT", lambda e: e.dma_start(
            out=aT_sb.rearrange("p (h n) -> p h n", n=32),
            in_=aT.rearrange("h p n -> p h n")), writes=[(t_aT, BOXF)])
        kb.dma("gpsimd", "p6bT", lambda e: e.dma_start(out=bT_sb[:], in_=bT[:]),
               writes=[(t_bT, BOXF)])
        kb.dma("gpsimd", "p6rw", lambda e: e.dma_start(
            out=rw_sb.rearrange("p (h n) -> p h n", n=4),
            in_=rwT.rearrange("h p n -> p h n")), writes=[(t_rw, BOXF)])
        kb.dma("sync", "p6rb", lambda e: e.dma_start(out=rb_sb[:], in_=rb4[:]),
               writes=[(t_rb, BOXF)])
        kb.dma("sync", "p6pr", lambda e: e.dma_start(out=pr_sb[:], in_=prior[:]),
               writes=[(t_pr, BOXF)])
        kb.op("vector", lambda v: v.tensor_scalar(out=pr_sb[:], in0=pr_sb[:], scalar1=R_EPS, scalar2=None, op0=ALU.add),
              reads=[(t_pr, BOXF)], writes=[(t_pr, BOXF)])
        kb.dma("gpsimd", "p6e2", lambda e: e.dma_start(out=e2_sb[:], in_=exp2[:]),
               writes=[(t_e2, BOXF)])
        kb.dma("gpsimd", "p6bb", lambda e: e.dma_start(out=bb_sb[:], in_=bb[:]),
               writes=[(t_bb, BOXF)])

        def load_h2t(ttb):
            hv2, t_hs2 = h2T_v[ttb % 2], t_h2s[ttb % 2]
            for s in range(SPT5):
                sl = ttb * SPT5 + s
                kb.dma("sync", f"p6h{s % 2}", (lambda sl=sl, s=s, hv2=hv2: lambda e: e.dma_start(
                    out=hv2[:, :, s * 128:(s + 1) * 128],
                    in_=h2T_s[sl].rearrange("h p c -> p h c")))(),
                    reads=[(t_h2T, (0, 128, sl * D, (sl + 1) * D))],
                    writes=[(t_hs2, (0, 128, s * 128, (HC - 1) * TW6 + (s + 1) * 128))])

        def ln3_stage(ttb):
            for s in range(SPT5):
                sl = ttb * SPT5 + s
                si = (ttb % 2) * SPT5 + s
                x2buf, tx2 = x2i[s % 2], t_x2i[s % 2]
                kb.dma("gpsimd", f"p6x2{s % 2}", (lambda sl=sl, x2buf=x2buf: lambda e: e.dma_start(
                    out=x2buf[:], in_=x2_s[sl]))(),
                    reads=[(t_x2, (0, 128, sl * D, (sl + 1) * D))],
                    writes=[(tx2, BOXF)])
                ln_chain6(f_sb[si], t_f[si], t1, t_t1)
                ybuf, tyb = yb[s % 2], t_yb[s % 2]
                kb.op("gpsimd", (lambda ybuf=ybuf, x2buf=x2buf: lambda g: g.tensor_tensor(
                    out=ybuf[:], in0=t1[:], in1=x2buf[:], op=ALU.add))(),
                    reads=[(t_t1, BOXF), (tx2, BOXF)], writes=[(tyb, BOXF)])
                kb.dma("gpsimd", f"p6y{s % 2}", (lambda sl=sl, ybuf=ybuf: lambda e: e.dma_start(
                    out=y[sl], in_=ybuf[:]))(),
                    reads=[(tyb, BOXF)])

        load_h2t(0)
        for ttb in range(NTTB5):
            hv, t_hs = h2T_v[ttb % 2], t_h2s[ttb % 2]
            for s in range(SPT5):
                for hc in range(HC):
                    kb.op("tensor", (lambda hc=hc, s=s, hv=hv: lambda t: t.matmul(
                        prt[:, 0:4], hv[:, hc, s * 128:(s + 1) * 128],
                        rw_sb[:, hc * 4:(hc + 1) * 4],
                        start=(hc == 0), stop=(hc == HC - 1)))(),
                        reads=[(t_hs, (0, 128, hc * TW6 + s * 128, hc * TW6 + (s + 1) * 128)),
                               (t_rw, (0, 128, hc * 4, (hc + 1) * 4))],
                        writes=[(t_prt, BOXF)], inc=(hc == HC - 1))
                kb.op("vector", lambda v: v.tensor_tensor(
                    out=rl[:], in0=prt[:, 0:4], in1=rb_sb[:], op=ALU.add),
                    reads=[(t_prt, BOXF), (t_rb, BOXF)], writes=[(t_rl, BOXF)])
                kb.op("scalar", lambda s_: s_.activation(out=re[:], in_=rl[:], func=AF.Exp),
                      reads=[(t_rl, BOXF)], writes=[(t_re, BOXF)])
                kb.op("vector", lambda v: v.tensor_tensor(
                    out=rp[:], in0=re[:], in1=pr_sb[:], op=ALU.mult),
                    reads=[(t_re, BOXF), (t_pr, BOXF)], writes=[(t_rp, BOXF)])
                kb.op("vector", lambda v: v.tensor_reduce(
                    out=rs[:], in_=rp[:], axis=AX.X, op=ALU.add),
                    reads=[(t_rp, BOXF)], writes=[(t_rs, BOXF)])
                kb.op("vector", lambda v: v.reciprocal(out=rr[:], in_=rs[:]),
                      reads=[(t_rs, BOXF)], writes=[(t_rr, BOXF)])
                kb.op("vector", lambda v: v.tensor_scalar(
                    out=alpha[:], in0=rp[:], scalar1=rr[:], scalar2=None, op0=ALU.mult),
                    reads=[(t_rp, BOXF), (t_rr, BOXF)], writes=[(t_al, BOXF)])
                kb.op("tensor", lambda t: t.transpose(pal[0:4, 0:128], alpha[:], ident[:]),
                      reads=[(t_al, BOXF), (t_ident, BOXF)], writes=[(t_pal, BOXF)])
                kb.op("vector", (lambda s=s: lambda v: v.tensor_copy(
                    out=alpha_fm[:, s * 128:(s + 1) * 128], in_=pal[0:4, 0:128]))(),
                    reads=[(t_pal, BOXF)],
                    writes=[(t_afm, (0, 4, s * 128, (s + 1) * 128))])
            kb.op("tensor", lambda t: t.matmul(
                pae[0:32, 0:TW6], e2_sb[:], alpha_fm[:], start=True, stop=True),
                reads=[(t_e2, BOXF), (t_afm, BOXF)], writes=[(t_pae, BOXF)])
            for hc in range(HC):
                kb.op("tensor", (lambda hc=hc, hv=hv: lambda t: t.matmul(
                    pzz[0:32, 0:TW6], aT_sb[:, hc * 32:(hc + 1) * 32],
                    hv[:, hc, :], start=(hc == 0), stop=(hc == HC - 1)))(),
                    reads=[(t_aT, (0, 128, hc * 32, (hc + 1) * 32)),
                           (t_hs, (0, 128, hc * TW6, (hc + 1) * TW6))],
                    writes=[(t_pzz, BOXF)], inc=(hc == HC - 1))
            kb.op("vector", lambda v: v.tensor_copy(out=zf[:], in_=pzz[0:32, 0:TW6]),
                  reads=[(t_pzz, BOXF)], writes=[(t_zf, BOXF)])
            kb.op("vector", lambda v: v.tensor_tensor(
                out=zs[:], in0=zf[:], in1=pae[0:32, 0:TW6], op=ALU.mult),
                reads=[(t_zf, BOXF), (t_pae, BOXF)], writes=[(t_zs, BOXF)])
            HH = max(1, HC // 4)
            for oc in range(NOC):
                for hf in range(4 if HC >= 4 else 1):
                    bi6 = (oc * 4 + hf) % 2
                    wbb, twbb = wb_sb[bi6], t_wb[bi6]
                    kb.dma("sync", f"p6wb{bi6}", (lambda oc=oc, hf=hf, wbb=wbb: lambda e: e.dma_start(
                        out=wbb.rearrange("p (h n) -> p h n", n=OCW),
                        in_=wbT_bf[hf * HH:(hf + 1) * HH, :, oc * OCW:(oc + 1) * OCW].rearrange("h p n -> p h n")))(),
                        reads=[(t_wbbf, (0, 128, hf * HH * D + oc * OCW, ((hf * HH + HH - 1) * D) + (oc + 1) * OCW))],
                        writes=[(twbb, BOXF)])
                    for hc in range(hf * HH, hf * HH + HH):
                        for s in range(SPT5):
                            kb.op("tensor", (lambda hc=hc, hf=hf, s=s, wbb=wbb, hv=hv: lambda t: t.matmul(
                                pw[s][:], hv[:, hc, s * 128:(s + 1) * 128],
                                wbb[:, (hc - hf * HH) * OCW:(hc - hf * HH + 1) * OCW],
                                start=(hc == 0), stop=False))(),
                                reads=[(t_hs, (0, 128, hc * TW6 + s * 128, hc * TW6 + (s + 1) * 128)),
                                       (twbb, (0, 128, (hc - hf * HH) * OCW, (hc - hf * HH + 1) * OCW))],
                                writes=[(t_pw[s], BOXF)], inc=False)
                for s in range(SPT5):
                    kb.op("tensor", (lambda oc=oc, s=s: lambda t: t.matmul(
                        pw[s][:], zs[:, s * 128:(s + 1) * 128],
                        bT_sb[:, oc * OCW:(oc + 1) * OCW],
                        start=False, stop=True))(),
                        reads=[(t_zs, (0, 32, s * 128, (s + 1) * 128)),
                               (t_bT, (0, 32, oc * OCW, (oc + 1) * OCW))],
                        writes=[(t_pw[s], BOXF)], inc=True)
                for s in range(SPT5):
                    si = (ttb % 2) * SPT5 + s
                    kb.op("vector", (lambda si=si, oc=oc: lambda v: v.tensor_tensor(
                        out=f_sb[si][:, oc * OCW:(oc + 1) * OCW], in0=pw[si % SPT5][:],
                        in1=bb_sb[:, oc * OCW:(oc + 1) * OCW], op=ALU.add))(),
                        reads=[(t_pw[s], BOXF), (t_bb, BOXF)],
                        writes=[(t_f[si], (0, 128, oc * OCW, (oc + 1) * OCW))])
            if ttb + 1 < NTTB5:
                load_h2t(ttb + 1)
            if ttb > 0:
                ln3_stage(ttb - 1)
            if ttb == NTTB5 - 1:
                ln3_stage(ttb)
    if cfg.get("stop_after", 9) >= 6:
        _phase6()

    kb.finalize_wait_all_dma()
    block = es.enter_context(nc.Block())
    kb.replay(block)
    es.close()
    return nc



def rope_tables(S, HD=128):
    inv = 1.0 / (10000.0 ** (np.arange(0, HD, 2, dtype=np.float32) / HD))
    t = np.arange(S, dtype=np.float32)
    fr = np.outer(t, inv)
    cos64 = np.cos(fr).astype(np.float32).T
    sin64 = np.sin(fr).astype(np.float32).T
    cos128 = np.concatenate([cos64, cos64], 0)
    sin128s = np.concatenate([-sin64, sin64], 0)
    return cos128, sin128s


def own_tiles(parity, NT):
    return [t for t in range(NT) if t % 4 in (parity, 3 - parity)]


def make_masks(parity, cfg):
    S = cfg["S"]; NT = S // 128; NSLOT = NT // 2
    tiles = own_tiles(parity, NT)
    m = np.zeros((NSLOT, 2, 128, 512), np.float32)
    for i, T in enumerate(tiles):
        KT = 2 * (i + 1)
        for j, kt in enumerate((KT - 2, KT - 1)):
            q_glob = T * 128 + np.arange(128)
            k_glob = kt * 128 + np.arange(128)
            mk = (k_glob[:, None] <= q_glob[None, :]).astype(np.float32)
            m[i, j] = np.tile(mk, (1, 4))
    return m.astype(ml_dtypes.bfloat16)


def prep_core_inputs(cfg, parity, xb, Wq, Wk, Wv, Wo, Wb, b_base, lora_A, lora_B,
                     router_W, router_b, domain_prior, active):
    S = cfg["S"]; D = cfg["D"]; NH = cfg["NH"]; NKV = cfg["NKV"]
    NT = S // 128; NSLOT = NT // 2; HC = D // 128
    tiles = own_tiles(parity, NT)
    x_tiles = np.ascontiguousarray(xb.reshape(NT, 128, D))
    x_own = np.ascontiguousarray(x_tiles[tiles])
    cos128, sin128s = rope_tables(S)
    own_cols = np.concatenate([np.arange(T * 128, (T + 1) * 128) for T in tiles])
    act = np.asarray(active)
    d = {
        "x_all": x_tiles,
        "x_own": x_own,
        "wkT": np.ascontiguousarray(Wk.T.reshape(HC, 128, NKV * 128)),
        "wvT": np.ascontiguousarray(Wv.T.reshape(HC, 128, NKV * 128)),
        "wqT": np.ascontiguousarray(Wq.T.reshape(HC, 128, NH * 128)),
        "woT": np.ascontiguousarray(Wo.T.reshape(HC, 128, D)),
        "wbT": np.ascontiguousarray(Wb.T.reshape(HC, 128, D)),
        "bb": np.ascontiguousarray(np.broadcast_to(b_base, (128, D))),
        "aT": np.ascontiguousarray(lora_A[act].transpose(2, 0, 1).reshape(D, 32).reshape(HC, 128, 32)),
        "bT": np.ascontiguousarray(lora_B[act].transpose(0, 2, 1).reshape(32, D)),
        "rwT": np.ascontiguousarray(router_W[act].T.reshape(HC, 128, 4)),
        "prior": np.ascontiguousarray(np.broadcast_to(domain_prior[act], (128, 4))).astype(np.float32),
        "exp2": np.repeat(np.eye(4, dtype=np.float32) * SCALING, 8, axis=1),
        "cosq": np.ascontiguousarray(cos128[:, own_cols]),
        "sinq": np.ascontiguousarray(sin128s[:, own_cols]),
        "cosk": cos128,
        "sink": sin128s,
        "masks": make_masks(parity, cfg),
        "ident": np.eye(128, dtype=ml_dtypes.bfloat16),
        "ones_in": np.ones((128, 128), dtype=ml_dtypes.bfloat16),
        "shift_in": np.full((128, 1), ESHIFT, np.float32),
        "rb4": np.ascontiguousarray(np.broadcast_to(router_b[act], (128, 4))).astype(np.float32),
    }
    return d



REAL_CFG = dict(S=4096, D=4096, NH=32, NKV=8)


def kernel(L0_attn_hidden_states, L0_attn_Wq, L0_attn_Wk, L0_attn_Wv, L0_attn_Wo,
           L0_ffn_W_base, L0_ffn_b_base, L0_ffn_lora_A, L0_ffn_lora_B,
           L0_ffn_router_W, L0_ffn_router_b, L0_ffn_domain_prior, L0_ffn_active):
    from concourse.bass_utils import run_bass_kernel_spmd
    cfg = REAL_CFG
    S = cfg["S"]; D = cfg["D"]
    NT = S // 128
    x = np.asarray(L0_attn_hidden_states)
    act = np.asarray(L0_ffn_active)
    HCr = D // 128

    shared = {
        "wkT": np.ascontiguousarray(L0_attn_Wk.T.reshape(HCr, 128, -1)),
        "wvT": np.ascontiguousarray(L0_attn_Wv.T.reshape(HCr, 128, -1)),
        "wqT": np.ascontiguousarray(L0_attn_Wq.T.reshape(HCr, 128, -1)),
        "woT": np.ascontiguousarray(L0_attn_Wo.T.reshape(HCr, 128, D)),
        "wbT": np.ascontiguousarray(L0_ffn_W_base.T.reshape(HCr, 128, D)),
        "bb": np.ascontiguousarray(np.broadcast_to(L0_ffn_b_base, (128, D))),
        "aT": np.ascontiguousarray(L0_ffn_lora_A[act].transpose(2, 0, 1).reshape(D, 32).reshape(HCr, 128, 32)),
        "bT": np.ascontiguousarray(L0_ffn_lora_B[act].transpose(0, 2, 1).reshape(32, D)),
        "rwT": np.ascontiguousarray(L0_ffn_router_W[act].T.reshape(HCr, 128, 4)),
        "prior": np.ascontiguousarray(np.broadcast_to(L0_ffn_domain_prior[act], (128, 4))).astype(np.float32),
        "rb4": np.ascontiguousarray(np.broadcast_to(L0_ffn_router_b[act], (128, 4))).astype(np.float32),
        "exp2": np.repeat(np.eye(4, dtype=np.float32) * SCALING, 8, axis=1),
        "ident": np.eye(128, dtype=ml_dtypes.bfloat16),
        "ones_in": np.ones((128, 128), dtype=ml_dtypes.bfloat16),
        "shift_in": np.full((128, 1), ESHIFT, np.float32),
    }
    cos128, sin128s = rope_tables(S)
    shared["cosk"] = cos128
    shared["sink"] = sin128s
    par = {}
    for p in (0, 1):
        tiles = own_tiles(p, NT)
        own_cols = np.concatenate([np.arange(T * 128, (T + 1) * 128) for T in tiles])
        par[p] = {
            "cosq": np.ascontiguousarray(cos128[:, own_cols]),
            "sinq": np.ascontiguousarray(sin128s[:, own_cols]),
            "masks": make_masks(p, cfg),
        }
    x_tiles = [np.ascontiguousarray(x[b].reshape(NT, 128, D)) for b in range(4)]
    in_maps = []
    for c in range(8):
        b, p = c // 2, c % 2
        tiles = own_tiles(p, NT)
        m = dict(shared)
        m.update(par[p])
        m["x_all"] = x_tiles[b]
        m["x_own"] = np.ascontiguousarray(x_tiles[b][tiles])
        in_maps.append(m)

    nc = build(cfg)
    res = run_bass_kernel_spmd(nc, in_maps, core_ids=list(range(8)))
    out = np.empty((4, NT, 128, D), np.float32)
    for c in range(8):
        b, p = c // 2, c % 2
        tiles = own_tiles(p, NT)
        out[b, tiles] = res.results[c]["y"]
    return out.reshape(4, S, D)


# version 15
# speedup vs baseline: 1.0512x; 1.0512x over previous
import numpy as np
import ml_dtypes
from contextlib import ExitStack

import concourse.bass as bass
import concourse.mybir as mybir

F32 = mybir.dt.float32
BF16 = mybir.dt.bfloat16
AF = mybir.ActivationFunctionType
ALU = mybir.AluOpType
AX = mybir.AxisListType

ENGINES = ("sync", "scalar", "vector", "gpsimd", "tensor")
ISQ = 1.0 / np.sqrt(128.0)
ESHIFT = -10.0
LN_EPS = 1e-5
R_EPS = 1e-12
SCALING = 2.0



class Rec:
    __slots__ = ("eng", "waits", "fn", "incs", "evt")

    def __init__(self, eng, fn):
        self.eng = eng
        self.fn = fn
        self.waits = []
        self.incs = []
        self.evt = None


class TB:

    def __init__(self, name):
        self.name = name
        self.regs = []


def _overlap(a, b):
    return a[0] < b[1] and b[0] < a[1] and a[2] < b[3] and b[2] < a[3]


class KB:
    def __init__(self, nc, es):
        self.nc = nc
        self.es = es
        self.ops = {e: [] for e in ENGINES}
        self.prog_sem = {e: es.enter_context(nc.semaphore(f"prog_{e}")) for e in ENGINES}
        self.prog_cnt = {e: 0 for e in ENGINES}
        self.pending = {e: [] for e in ENGINES}
        self.watermark = {e: {} for e in ENGINES}
        self.dma_sem = {}
        self.dma_cnt = {}
        self.dma_queue = {}
        self.bar_sem = es.enter_context(nc.semaphore("bar"))
        self.bar_cnt = 0
        self.nbar = 0

    def _resolve(self, rec):
        if rec.evt is not None:
            return rec.evt
        e = rec.eng
        q = self.pending[e]
        idx = q.index(rec)
        for r in q[: idx + 1]:
            self.prog_cnt[e] += 1
            r.evt = ("prog", e, self.prog_cnt[e])
            r.incs.append((self.prog_sem[e], 1))
        del q[: idx + 1]
        return rec.evt

    def _evt_of(self, w):
        if isinstance(w, Rec):
            return self._resolve(w)
        return w

    def _dep(self, rec, w):
        evt = self._evt_of(w)
        kind = evt[0]
        if kind == "prog":
            _, peng, val = evt
            if peng == rec.eng == "tensor":
                return
            key = ("prog", peng)
            sem = self.prog_sem[peng]
        else:
            _, stream, val = evt
            key = ("dma", stream)
            sem = self.dma_sem[stream]
        wm = self.watermark[rec.eng]
        if wm.get(key, 0) >= val:
            return
        wm[key] = val
        rec.waits.append((sem, val))

    def _track(self, rec_or_evt, reads, writes):
        for tb, box in reads:
            deps = [r[4] for r in tb.regs if _overlap(box, r)]
            for w in deps:
                self._dep_any(rec_or_evt, w)
            for r in tb.regs:
                if _overlap(box, r):
                    r[5].append(rec_or_evt)
        for tb, box in writes:
            dead = []
            for r in tb.regs:
                if _overlap(box, r):
                    self._dep_any(rec_or_evt, r[4])
                    for rd in r[5]:
                        self._dep_any(rec_or_evt, rd)
                    if box[0] <= r[0] and box[1] >= r[1] and box[2] <= r[2] and box[3] >= r[3]:
                        dead.append(r)
            for r in dead:
                tb.regs.remove(r)
            tb.regs.append([box[0], box[1], box[2], box[3], rec_or_evt, []])

    def _dep_any(self, consumer, producer):
        if isinstance(consumer, Rec):
            if producer is consumer:
                return
            self._dep(consumer, producer)
        else:
            raise AssertionError("dma consumers handled via their issue rec")

    def op(self, eng, fn, reads=(), writes=(), inc=True):
        rec = Rec(eng, fn)
        self._track(rec, list(reads), list(writes))
        self.ops[eng].append(rec)
        if inc:
            self.pending[eng].append(rec)
            self._resolve(rec)
        else:
            self.pending[eng].append(rec)
        return rec

    def wait_manual(self, eng, w):
        rec = Rec(eng, None)
        self._dep(rec, self._evt_of(w) if not isinstance(w, Rec) else w)
        self.ops[eng].append(rec)
        return rec

    def dma(self, queue, stream, fn, reads=(), writes=()):
        if stream not in self.dma_sem:
            self.dma_sem[stream] = self.es.enter_context(self.nc.semaphore(f"dma_{stream}"))
            self.dma_cnt[stream] = 0
            self.dma_queue[stream] = queue
        assert self.dma_queue[stream] == queue, stream
        prev = self.dma_cnt[stream]
        rec = Rec(queue, fn)
        if prev > 0:
            key = ("dma", stream)
            if self.watermark[queue].get(key, 0) < 16 * prev:
                self.watermark[queue][key] = 16 * prev
                rec.waits.append((self.dma_sem[stream], 16 * prev))
        self.dma_cnt[stream] += 1
        val = 16 * self.dma_cnt[stream]
        evt = ("dma", stream, val)
        for tb, box in reads:
            for r in tb.regs:
                if _overlap(box, r):
                    self._dep(rec, r[4])
                    r[5].append(evt)
        for tb, box in writes:
            dead = []
            for r in tb.regs:
                if _overlap(box, r):
                    self._dep(rec, r[4])
                    for rd in r[5]:
                        self._dep(rec, rd)
                    if box[0] <= r[0] and box[1] >= r[1] and box[2] <= r[2] and box[3] >= r[3]:
                        dead.append(r)
            for r in dead:
                tb.regs.remove(r)
            tb.regs.append([box[0], box[1], box[2], box[3], evt, []])
        rec.incs.append((self.dma_sem[stream], 16))
        self.ops[queue].append(rec)
        return evt

    def barrier(self):
        self.nbar += 1
        tgt = 5 * self.nbar
        for e in ENGINES:
            if self.pending[e]:
                self._resolve(self.pending[e][-1])
            cnt = self.prog_cnt[e]
            rec = Rec(e, lambda eng: eng.nop())
            if cnt > 0:
                key = ("prog", e)
                if self.watermark[e].get(key, 0) < cnt:
                    self.watermark[e][key] = cnt
                    rec.waits.append((self.prog_sem[e], cnt))
            if e == "sync":
                for s, c in self.dma_cnt.items():
                    key = ("dma", s)
                    if c > 0 and self.watermark[e].get(key, 0) < 16 * c:
                        self.watermark[e][key] = 16 * c
                        rec.waits.append((self.dma_sem[s], 16 * c))
            rec.incs.append((self.bar_sem, 1))
            self.ops[e].append(rec)
        for e in ENGINES:
            rec = Rec(e, None)
            rec.waits.append((self.bar_sem, tgt))
            self.watermark[e] = {}
            self.ops[e].append(rec)

    def finalize_wait_all_dma(self):
        rec = Rec("sync", None)
        for s, c in self.dma_cnt.items():
            key = ("dma", s)
            if c > 0 and self.watermark["sync"].get(key, 0) < 16 * c:
                rec.waits.append((self.dma_sem[s], 16 * c))
        self.ops["sync"].append(rec)

    def replay(self, block):
        def section(e):
            def run(eng):
                for rec in self.ops[e]:
                    for sem, val in rec.waits:
                        eng.wait_ge(sem, val)
                    if rec.fn is not None:
                        inst = rec.fn(eng)
                        for sem, n in rec.incs:
                            inst.then_inc(sem, n)
                    else:
                        assert not rec.incs
            return run

        for e in ENGINES:
            getattr(block, e)(section(e))



def build(cfg, dbg=()):
    S = cfg["S"]; D = cfg["D"]; NH = cfg["NH"]; NKV = cfg["NKV"]
    HD = 128
    NT = S // 128
    NSLOT = NT // 2
    HC = D // 128
    KVW = NKV * 128
    G = NH // NKV
    assert G == 4
    OWN = NSLOT * 128
    OCW = min(512, D)
    NOC = D // OCW
    TCW = min(512, OWN)
    NTC = OWN // TCW
    SPT = min(8, NSLOT)
    NTTB = NSLOT // SPT
    QUAD = min(4, SPT)
    NQ = SPT // QUAD
    VOC = min(512, KVW)
    NVOC = KVW // VOC

    nc = bass.Bass("TRN2", target_bir_lowering=False, debug=False)
    es = ExitStack()
    kb = KB(nc, es)

    def dram(name, shape, dt, kind="ExternalInput"):
        t = nc.dram_tensor(name, shape, dt, kind=kind)
        return t.ap()

    x_all = dram("x_all", [NT, 128, D], F32)
    x_own = dram("x_own", [NSLOT, 128, D], F32)
    wkT = dram("wkT", [HC, 128, KVW], F32)
    wvT = dram("wvT", [HC, 128, KVW], F32)
    wqT = dram("wqT", [HC, 128, NH * 128], F32)
    woT = dram("woT", [HC, 128, D], F32)
    wbT = dram("wbT", [HC, 128, D], F32)
    bb = dram("bb", [128, D], F32)
    aT = dram("aT", [HC, 128, 32], F32)
    bT = dram("bT", [32, D], F32)
    rwT = dram("rwT", [HC, 128, 4], F32)
    prior = dram("prior", [128, 4], F32)
    exp2 = dram("exp2", [4, 32], F32)
    cosq = dram("cosq", [128, OWN], F32)
    sinq = dram("sinq", [128, OWN], F32)
    cosk = dram("cosk", [128, S], F32)
    sink = dram("sink", [128, S], F32)
    masks = dram("masks", [NSLOT, 2, 128, 512], BF16)
    ident_in = dram("ident", [128, 128], BF16)
    ones_in = dram("ones_in", [128, 128], BF16)
    shift_in = dram("shift_in", [128, 1], F32)
    rb4 = dram("rb4", [128, 4], F32)

    kT_s = dram("kT_s", [NKV, NT, 128, 128], BF16, kind="Internal")
    woT_bf = dram("woT_bf", [HC, 128, D], BF16, kind="Internal")
    wbT_bf = dram("wbT_bf", [HC, 128, D], BF16, kind="Internal")
    v_s = dram("v_s", [NKV, NT, 128, 128], BF16, kind="Internal")
    qT4_s = dram("qT4_s", [NSLOT, NKV, 128, 512], BF16, kind="Internal")
    avT_s = dram("avT_s", [NSLOT, NKV, 128, 512], BF16, kind="Internal")
    x2_s = dram("x2_s", [NSLOT, 128, D], F32, kind="Internal")
    h2T_s = dram("h2T_s", [NSLOT, HC, 128, 128], BF16, kind="Internal")
    y = dram("y", [NSLOT, 128, D], F32, kind="ExternalOutput")
    dbg_out = {}
    for name in dbg:
        shp = {"kT": [NKV, NT, 128, 128], "v": [NKV, NT, 128, 128],
               "qT4": [NSLOT, NKV, 128, 512], "avT": [NSLOT, NKV, 128, 512],
               "x2": [NSLOT, 128, D], "h2T": [NSLOT, HC, 128, 128]}[name]
        dbg_out[name] = dram("dbg_" + name, shp, BF16 if name not in ("x2",) else F32,
                             kind="ExternalOutput")

    t_kT = TB("kT_s"); t_v = TB("v_s"); t_q4 = TB("qT4_s"); t_avT = TB("avT_s")
    t_x2 = TB("x2_s"); t_h2T = TB("h2T_s")
    t_wobf = TB("woT_bf"); t_wbbf = TB("wbT_bf")

    ident = es.enter_context(nc.sbuf_tensor("ident_sb", [128, 128], BF16))
    ones = es.enter_context(nc.sbuf_tensor("ones_sb", [128, 128], BF16))
    shift = es.enter_context(nc.sbuf_tensor("shift_sb", [128, 1], F32))
    t_ident = TB("ident"); t_ones = TB("ones"); t_shift = TB("shift")
    BOXF = (0, 128, 0, 1 << 30)

    kb.dma("sync", "consts", lambda e: e.dma_start(out=ident[:], in_=ident_in[:]),
           writes=[(t_ident, BOXF)])
    kb.dma("sync", "consts", lambda e: e.dma_start(out=ones[:], in_=ones_in[:]),
           writes=[(t_ones, BOXF)])
    kb.dma("sync", "consts", lambda e: e.dma_start(out=shift[:], in_=shift_in[:]),
           writes=[(t_shift, BOXF)])

    def _phase1():
      with ExitStack() as ph:
        wk_sb = ph.enter_context(nc.sbuf_tensor("wk_sb", [128, HC * KVW], BF16))
        wv_sb = ph.enter_context(nc.sbuf_tensor("wv_sb", [128, HC * KVW], BF16))
        cosk_sb = ph.enter_context(nc.sbuf_tensor("cosk_sb", [128, S], F32))
        sink_sb = ph.enter_context(nc.sbuf_tensor("sink_sb", [128, S], F32))
        xt = [ph.enter_context(nc.sbuf_tensor("p1x0", [128, D], F32))] * 2
        h1 = ph.enter_context(nc.sbuf_tensor("p1h1", [128, D], BF16))
        h1T = ph.enter_context(nc.sbuf_tensor("p1h1T", [128, HC * 128], BF16))
        kro = [ph.enter_context(nc.sbuf_tensor(f"p1kro{i}", [128, KVW], BF16)) for i in range(2)]
        vsb = [ph.enter_context(nc.sbuf_tensor(f"p1v{i}", [128, KVW], BF16)) for i in range(2)]
        u_sb = ph.enter_context(nc.sbuf_tensor("p1u", [128, KVW], BF16))
        s_sb = ph.enter_context(nc.sbuf_tensor("p1s", [128, KVW], BF16))
        stats = ph.enter_context(nc.sbuf_tensor("p1stats", [128, 6 * (D // 512 if D >= 512 else 1)], F32))
        mv = ph.enter_context(nc.sbuf_tensor("p1mv", [128, 2], F32))
        vtmp = ph.enter_context(nc.sbuf_tensor("p1vtmp", [128, 1], F32))
        rstd = ph.enter_context(nc.sbuf_tensor("p1rstd", [128, 1], F32))
        negmr = ph.enter_context(nc.sbuf_tensor("p1negmr", [128, 1], F32))
        pk = [ph.enter_context(nc.psum_tensor(f"p1pk{i}", [128, KVW], F32)) for i in range(2)]
        pv = [ph.enter_context(nc.psum_tensor("p1pv0", [128, KVW], F32))] * 2
        ptr = [ph.enter_context(nc.psum_tensor(f"p1ptr{i}", [128, 1024], BF16)) for i in range(2)]

        t_wk = TB("wk"); t_wv = TB("wv"); t_cosk = TB("cosk"); t_sink = TB("sink")
        t_xt = [TB("p1x0")] * 2
        t_h1 = TB("h1"); t_h1T = TB("h1T")
        t_kro = [TB(f"kro{i}") for i in range(2)]
        t_vsb = [TB(f"vsb{i}") for i in range(2)]
        t_us = TB("u"); t_ss = TB("s")
        t_stats = TB("stats"); t_mv = TB("mv"); t_vt = TB("vtmp"); t_rstd = TB("rstd"); t_negmr = TB("negmr")
        t_pk = [TB(f"pk{i}") for i in range(2)]
        t_pv = [TB("p1pv0")] * 2
        t_ptr = [TB(f"ptr{i}") for i in range(2)]

        nchunk = max(1, D // 512)
        cw = min(512, D)

        for j in range(2):
            h0 = j * (HC // 2); h1_ = (j + 1) * (HC // 2)
            if h0 == h1_:
                continue
            kb.dma("gpsimd", "wk", (lambda h0=h0, h1_=h1_: lambda e: e.dma_start(
                out=wk_sb[:, h0 * KVW: h1_ * KVW].rearrange("p (h n) -> p h n", n=KVW),
                in_=wkT[h0:h1_].rearrange("h p n -> p h n")))(),
                writes=[(t_wk, (0, 128, h0 * KVW, h1_ * KVW))])
            kb.dma("gpsimd", "wv", (lambda h0=h0, h1_=h1_: lambda e: e.dma_start(
                out=wv_sb[:, h0 * KVW: h1_ * KVW].rearrange("p (h n) -> p h n", n=KVW),
                in_=wvT[h0:h1_].rearrange("h p n -> p h n")))(),
                writes=[(t_wv, (0, 128, h0 * KVW, h1_ * KVW))])
        kb.dma("sync", "cosk", lambda e: e.dma_start(out=cosk_sb[:], in_=cosk[:]),
               writes=[(t_cosk, BOXF)])
        kb.dma("sync", "sink", lambda e: e.dma_start(out=sink_sb[:], in_=sink[:]),
               writes=[(t_sink, BOXF)])

        def stageA(gt):
            bi = gt % 2
            xbuf, txb = xt[bi], t_xt[bi]
            pkb, tpkb = pk[bi], t_pk[bi]
            pvb, tpvb = pv[bi], t_pv[bi]
            kb.dma("gpsimd", f"x{bi}", (lambda gt=gt, xbuf=xbuf: lambda e: e.dma_start(
                out=xbuf[:], in_=x_all[gt]))(), writes=[(txb, BOXF)])
            for ch in range(nchunk):
                kb.op("vector", (lambda ch=ch, xbuf=xbuf: lambda v: v.bn_stats(
                    out=stats[:, ch * 6:(ch + 1) * 6], in_=xbuf[:, ch * cw:(ch + 1) * cw]))(),
                    reads=[(txb, BOXF)], writes=[(t_stats, (0, 128, ch * 6, ch * 6 + 6))])
            kb.op("vector", lambda v: v.bn_aggr(out=mv[:], in_=stats[:]),
                  reads=[(t_stats, BOXF)], writes=[(t_mv, BOXF)])
            kb.op("vector", lambda v: v.tensor_scalar(
                out=vtmp[:], in0=mv[:, 1:2], scalar1=LN_EPS, scalar2=None, op0=ALU.add),
                reads=[(t_mv, BOXF)], writes=[(t_vt, BOXF)])
            kb.op("scalar", lambda s: s.activation(out=vtmp[:], in_=vtmp[:], func=AF.Ln),
                  reads=[(t_vt, BOXF)], writes=[(t_vt, BOXF)])
            kb.op("scalar", lambda s: s.activation(out=rstd[:], in_=vtmp[:], func=AF.Exp, scale=-0.5),
                  reads=[(t_vt, BOXF)], writes=[(t_rstd, BOXF)])
            kb.op("vector", lambda v: v.tensor_scalar(
                out=negmr[:], in0=mv[:, 0:1], scalar1=rstd[:], scalar2=-1.0,
                op0=ALU.mult, op1=ALU.mult),
                reads=[(t_mv, BOXF), (t_rstd, BOXF)], writes=[(t_negmr, BOXF)])
            kb.op("scalar", (lambda xbuf=xbuf: lambda sc: sc.activation(
                out=h1[:], in_=xbuf[:], func=AF.Identity,
                scale=rstd[:], bias=negmr[:]))(),
                reads=[(txb, BOXF), (t_rstd, BOXF), (t_negmr, BOXF)],
                writes=[(t_h1, BOXF)])
            for grp in range((HC + 7) // 8):
                pb, tpb = ptr[grp % 2], t_ptr[grp % 2]
                n = min(8, HC - grp * 8)
                for k in range(n):
                    hc = grp * 8 + k
                    kb.op("tensor", (lambda hc=hc, k=k, pb=pb: lambda t: t.transpose(
                        pb[:, k * 128:(k + 1) * 128], h1[:, hc * 128:(hc + 1) * 128], ident[:]))(),
                        reads=[(t_h1, (0, 128, hc * 128, hc * 128 + 128)), (t_ident, BOXF)],
                        writes=[(tpb, (0, 128, k * 128, k * 128 + 128))],
                        inc=(k == n - 1))
                kb.op("scalar", (lambda grp=grp, n=n, pb=pb: lambda sc: sc.copy(
                    out=h1T[:, grp * 1024: grp * 1024 + n * 128], in_=pb[:, 0: n * 128]))(),
                    reads=[(tpb, (0, 128, 0, n * 128))],
                    writes=[(t_h1T, (0, 128, grp * 1024, grp * 1024 + n * 128))])
            for hc in range(HC):
                for kv in range(NKV):
                    kb.op("tensor", (lambda hc=hc, kv=kv, pkb=pkb: lambda t: t.matmul(
                        pkb[:, kv * 128:(kv + 1) * 128],
                        wk_sb[:, (hc * NKV + kv) * 128:(hc * NKV + kv) * 128 + 128],
                        h1T[:, hc * 128:(hc + 1) * 128],
                        start=(hc == 0 and kv % 4 == 0), stop=(hc == HC - 1),
                        skip_group_check=True))(),
                        reads=[(t_wk, (0, 128, (hc * NKV + kv) * 128, (hc * NKV + kv + 1) * 128)),
                               (t_h1T, (0, 128, hc * 128, hc * 128 + 128))],
                        writes=[(tpkb, (0, 128, kv * 128, kv * 128 + 128))],
                        inc=(hc == HC - 1 and kv == NKV - 1))
            for hc in range(HC):
                for oc in range(NVOC):
                    kb.op("tensor", (lambda hc=hc, oc=oc, pvb=pvb: lambda t: t.matmul(
                        pvb[:, oc * VOC:(oc + 1) * VOC],
                        h1T[:, hc * 128:(hc + 1) * 128],
                        wv_sb[:, hc * KVW + oc * VOC: hc * KVW + (oc + 1) * VOC],
                        start=(hc == 0 and (oc * VOC) % 512 == 0), stop=(hc == HC - 1),
                        skip_group_check=True))(),
                        reads=[(t_wv, (0, 128, hc * KVW + oc * VOC, hc * KVW + (oc + 1) * VOC)),
                               (t_h1T, (0, 128, hc * 128, hc * 128 + 128))],
                        writes=[(tpvb, (0, 128, oc * VOC, (oc + 1) * VOC))],
                        inc=(hc == HC - 1 and oc == NVOC - 1))

            vb, tvb = vsb[bi], t_vsb[bi]
            kb.op("scalar", (lambda vb=vb, pvb=pvb: lambda sc: sc.copy(out=vb[:], in_=pvb[:]))(),
                  reads=[(tpvb, BOXF)], writes=[(tvb, BOXF)])
            kb.dma("sync", f"vout{bi}", (lambda gt=gt, vb=vb: lambda e: e.dma_start(
                out=v_s[:, gt].rearrange("k p c -> p k c"), in_=vb.rearrange("p (k c) -> p k c", c=128)))(),
                reads=[(tvb, BOXF)], writes=[(t_v, (0, 128, gt * KVW, (gt + 1) * KVW))])

        def stageB(gt):
            bi = gt % 2
            pkb, tpkb = pk[bi], t_pk[bi]
            ckc = (lambda gt=gt: cosk_sb[:, gt * 128:(gt + 1) * 128])
            skc = (lambda gt=gt: sink_sb[:, gt * 128:(gt + 1) * 128])
            for kv in range(NKV):
                kb.op("vector", (lambda kv=kv, ckc=ckc, pkb=pkb: lambda v: v.tensor_tensor(
                    out=u_sb[:, kv * 128:(kv + 1) * 128], in0=pkb[:, kv * 128:(kv + 1) * 128],
                    in1=ckc(), op=ALU.mult))(),
                    reads=[(tpkb, (0, 128, kv * 128, (kv + 1) * 128)), (t_cosk, BOXF)],
                    writes=[(t_us, (0, 128, kv * 128, (kv + 1) * 128))])
                kb.op("vector", (lambda kv=kv, skc=skc, pkb=pkb: lambda v: v.tensor_tensor(
                    out=s_sb[0:64, kv * 128:(kv + 1) * 128], in0=pkb[64:128, kv * 128:(kv + 1) * 128],
                    in1=skc()[0:64, :], op=ALU.mult))(),
                    reads=[(tpkb, (64, 128, kv * 128, (kv + 1) * 128)), (t_sink, BOXF)],
                    writes=[(t_ss, (0, 64, kv * 128, (kv + 1) * 128))])
                kb.op("vector", (lambda kv=kv, skc=skc, pkb=pkb: lambda v: v.tensor_tensor(
                    out=s_sb[64:128, kv * 128:(kv + 1) * 128], in0=pkb[0:64, kv * 128:(kv + 1) * 128],
                    in1=skc()[64:128, :], op=ALU.mult))(),
                    reads=[(tpkb, (0, 64, kv * 128, (kv + 1) * 128)), (t_sink, BOXF)],
                    writes=[(t_ss, (64, 128, kv * 128, (kv + 1) * 128))])
            kbuf, tkb = kro[bi], t_kro[bi]
            kb.op("vector", (lambda kbuf=kbuf: lambda v: v.tensor_tensor(
                out=kbuf[:], in0=u_sb[:], in1=s_sb[:], op=ALU.add))(),
                reads=[(t_us, BOXF), (t_ss, BOXF)], writes=[(tkb, BOXF)])
            kb.dma("sync", f"kout{bi}", (lambda gt=gt, kbuf=kbuf: lambda e: e.dma_start(
                out=kT_s[:, gt].rearrange("k p c -> p k c"), in_=kbuf.rearrange("p (k c) -> p k c", c=128)))(),
                reads=[(tkb, BOXF)], writes=[(t_kT, (0, 128, gt * KVW, (gt + 1) * KVW))])

        pc_jobs = [("wo", hc) for hc in range(HC)] + [("wb", hc) for hc in range(HC)]
        pc_i = 0

        def precast_some(k):
            nonlocal pc_i
            for _ in range(k):
                if pc_i >= len(pc_jobs):
                    return
                which, hc = pc_jobs[pc_i]; pc_i += 1
                if which == "wo":
                    kb.dma("gpsimd", "pcast", (lambda hc=hc: lambda e: e.dma_start(
                        out=woT_bf[hc], in_=woT[hc]))(),
                        writes=[(t_wobf, (0, 128, hc * D, (hc + 1) * D))])
                else:
                    kb.dma("gpsimd", "pcast", (lambda hc=hc: lambda e: e.dma_start(
                        out=wbT_bf[hc], in_=wbT[hc]))(),
                        writes=[(t_wbbf, (0, 128, hc * D, (hc + 1) * D))])

        for gt in range(NT):
            stageA(gt)
            precast_some((2 * len(pc_jobs) + NT - 1) // NT if NT else 0)
            if gt > 0:
                stageB(gt - 1)
        stageB(NT - 1)
        precast_some(len(pc_jobs))

        if "kT" in dbg:
            kb.dma("sync", "dbgk", lambda e: e.dma_start(out=dbg_out["kT"][:], in_=kT_s[:]),
                   reads=[(t_kT, BOXF)])
        if "v" in dbg:
            kb.dma("sync", "dbgv", lambda e: e.dma_start(out=dbg_out["v"][:], in_=v_s[:]),
                   reads=[(t_v, BOXF)])
        kb.barrier()
    _phase1()

    def _phase3():
      with ExitStack() as ph:
        xt = [ph.enter_context(nc.sbuf_tensor(f"p3x{i}", [128, D], F32)) for i in range(2)]
        h1 = ph.enter_context(nc.sbuf_tensor("p3h1", [128, D], BF16))
        h1T = ph.enter_context(nc.sbuf_tensor("p3h1T", [128, HC * OWN], BF16))
        wq_sb = [ph.enter_context(nc.sbuf_tensor(f"p3wq{i}", [128, HC * 128], BF16)) for i in range(2)]
        cosq_sb = ph.enter_context(nc.sbuf_tensor("p3cosq", [128, OWN], F32))
        sinq_sb = ph.enter_context(nc.sbuf_tensor("p3sinq", [128, OWN], F32))
        u_sb = ph.enter_context(nc.sbuf_tensor("p3u", [128, TCW], BF16))
        s_sb = ph.enter_context(nc.sbuf_tensor("p3s", [128, TCW], BF16))
        qro = [ph.enter_context(nc.sbuf_tensor(f"p3qro{i}", [128, TCW], BF16)) for i in range(2)]
        stats = ph.enter_context(nc.sbuf_tensor("p3stats", [128, 6 * max(1, D // 512)], F32))
        mv = ph.enter_context(nc.sbuf_tensor("p3mv", [128, 2], F32))
        vtmp = ph.enter_context(nc.sbuf_tensor("p3vtmp", [128, 1], F32))
        rstd = ph.enter_context(nc.sbuf_tensor("p3rstd", [128, 1], F32))
        negmr = ph.enter_context(nc.sbuf_tensor("p3negmr", [128, 1], F32))
        pq = [ph.enter_context(nc.psum_tensor(f"p3pq{i}", [128, TCW], F32)) for i in range(2)]
        ptr = [ph.enter_context(nc.psum_tensor(f"p3ptr{i}", [128, 1024], BF16)) for i in range(2)]

        t_xt = [TB(f"p3xt{i}") for i in range(2)]
        t_h1 = TB("p3h1"); t_h1T = TB("p3h1T")
        t_wq = [TB(f"p3wq{i}") for i in range(2)]
        t_cq = TB("p3cosq"); t_sq = TB("p3sinq")
        t_us = TB("p3u"); t_ss = TB("p3s")
        t_qro = [TB(f"p3qro{i}") for i in range(2)]
        t_stats = TB("p3stats"); t_mv = TB("p3mv"); t_vt = TB("p3vt")
        t_rstd = TB("p3rstd"); t_negmr = TB("p3negmr")
        t_pq = [TB(f"p3pq{i}") for i in range(2)]
        t_ptr = [TB(f"p3ptr{i}") for i in range(2)]

        nchunk = max(1, D // 512)
        cw = min(512, D)
        h1T_v = h1T.rearrange("p (h c) -> p h c", c=OWN)

        kb.dma("sync", "cosq", lambda e: e.dma_start(out=cosq_sb[:], in_=cosq[:]),
               writes=[(t_cq, BOXF)])
        kb.dma("sync", "sinq", lambda e: e.dma_start(out=sinq_sb[:], in_=sinq[:]),
               writes=[(t_sq, BOXF)])

        for sl in range(NSLOT):
            bi = sl % 2
            xbuf, txb = xt[bi], t_xt[bi]
            kb.dma("sync", f"p3x{bi}", (lambda sl=sl, xbuf=xbuf: lambda e: e.dma_start(
                out=xbuf[:], in_=x_own[sl]))(), writes=[(txb, BOXF)])
            for ch in range(nchunk):
                kb.op("vector", (lambda ch=ch, xbuf=xbuf: lambda v: v.bn_stats(
                    out=stats[:, ch * 6:(ch + 1) * 6], in_=xbuf[:, ch * cw:(ch + 1) * cw]))(),
                    reads=[(txb, BOXF)], writes=[(t_stats, (0, 128, ch * 6, ch * 6 + 6))])
            kb.op("vector", lambda v: v.bn_aggr(out=mv[:], in_=stats[:]),
                  reads=[(t_stats, BOXF)], writes=[(t_mv, BOXF)])
            kb.op("vector", lambda v: v.tensor_scalar(
                out=vtmp[:], in0=mv[:, 1:2], scalar1=LN_EPS, scalar2=None, op0=ALU.add),
                reads=[(t_mv, BOXF)], writes=[(t_vt, BOXF)])
            kb.op("scalar", lambda s: s.activation(out=vtmp[:], in_=vtmp[:], func=AF.Ln),
                  reads=[(t_vt, BOXF)], writes=[(t_vt, BOXF)])
            kb.op("scalar", lambda s: s.activation(out=rstd[:], in_=vtmp[:], func=AF.Exp, scale=-0.5),
                  reads=[(t_vt, BOXF)], writes=[(t_rstd, BOXF)])
            kb.op("vector", lambda v: v.tensor_scalar(
                out=negmr[:], in0=mv[:, 0:1], scalar1=rstd[:], scalar2=-1.0,
                op0=ALU.mult, op1=ALU.mult),
                reads=[(t_mv, BOXF), (t_rstd, BOXF)], writes=[(t_negmr, BOXF)])
            kb.op("scalar", (lambda xbuf=xbuf: lambda sc: sc.activation(
                out=h1[:], in_=xbuf[:], func=AF.Identity,
                scale=rstd[:], bias=negmr[:]))(),
                reads=[(txb, BOXF), (t_rstd, BOXF), (t_negmr, BOXF)],
                writes=[(t_h1, BOXF)])
            for grp in range((HC + 7) // 8):
                pb, tpb = ptr[grp % 2], t_ptr[grp % 2]
                n = min(8, HC - grp * 8)
                for k in range(n):
                    hc = grp * 8 + k
                    kb.op("tensor", (lambda hc=hc, k=k, pb=pb: lambda t: t.transpose(
                        pb[:, k * 128:(k + 1) * 128], h1[:, hc * 128:(hc + 1) * 128], ident[:]))(),
                        reads=[(t_h1, (0, 128, hc * 128, hc * 128 + 128)), (t_ident, BOXF)],
                        writes=[(tpb, (0, 128, k * 128, k * 128 + 128))],
                        inc=(k == n - 1))
                g0 = grp * 8
                kb.op("scalar", (lambda g0=g0, n=n, pb=pb, sl=sl: lambda sc: sc.copy(
                    out=h1T_v[:, g0:g0 + n, sl * 128:sl * 128 + 128],
                    in_=pb[:, 0:n * 128].rearrange("p (h c) -> p h c", c=128)))(),
                    reads=[(tpb, (0, 128, 0, n * 128))],
                    writes=[(t_h1T, (0, 128, 0, 1 << 30))] if False else
                           [(t_h1T, (0, 128, g0 * OWN + sl * 128, (g0 + n - 1) * OWN + sl * 128 + 128))])

        for h in range(NH):
            bi = h % 2
            wq, twq = wq_sb[bi], t_wq[bi]
            kb.dma("gpsimd", f"p3wq{bi}", (lambda h=h, wq=wq: lambda e: e.dma_start(
                out=wq.rearrange("p (k n) -> p k n", n=128),
                in_=wqT[:, :, h * 128:(h + 1) * 128].rearrange("k p n -> p k n")))(),
                writes=[(twq, BOXF)])
            kv, j = h // 4, h % 4
            for tc in range(NTC):
                pqb, tpq = pq[tc % 2], t_pq[tc % 2]
                for hc in range(HC):
                    kb.op("tensor", (lambda hc=hc, tc=tc, pqb=pqb, wq=wq: lambda t: t.matmul(
                        pqb[:], wq[:, hc * 128:(hc + 1) * 128],
                        h1T[:, hc * OWN + tc * TCW: hc * OWN + (tc + 1) * TCW],
                        start=(hc == 0), stop=(hc == HC - 1)))(),
                        reads=[(twq, (0, 128, hc * 128, hc * 128 + 128)),
                               (t_h1T, (0, 128, hc * OWN + tc * TCW, hc * OWN + (tc + 1) * TCW))],
                        writes=[(tpq, BOXF)],
                        inc=(hc == HC - 1))
                kb.op("vector", (lambda pqb=pqb, tc=tc: lambda v: v.tensor_tensor(
                    out=u_sb[:], in0=pqb[:], in1=cosq_sb[:, tc * TCW:(tc + 1) * TCW],
                    op=ALU.mult))(),
                    reads=[(tpq, BOXF), (t_cq, BOXF)], writes=[(t_us, BOXF)])
                kb.op("vector", (lambda pqb=pqb, tc=tc: lambda v: v.tensor_tensor(
                    out=s_sb[0:64, :], in0=pqb[64:128, :],
                    in1=sinq_sb[0:64, tc * TCW:(tc + 1) * TCW], op=ALU.mult))(),
                    reads=[(tpq, BOXF), (t_sq, BOXF)], writes=[(t_ss, (0, 64, 0, TCW))])
                kb.op("vector", (lambda pqb=pqb, tc=tc: lambda v: v.tensor_tensor(
                    out=s_sb[64:128, :], in0=pqb[0:64, :],
                    in1=sinq_sb[64:128, tc * TCW:(tc + 1) * TCW], op=ALU.mult))(),
                    reads=[(tpq, BOXF), (t_sq, BOXF)], writes=[(t_ss, (64, 128, 0, TCW))])
                qb, tqb = qro[tc % 2], t_qro[tc % 2]
                kb.op("vector", (lambda qb=qb: lambda v: v.tensor_tensor(
                    out=qb[:], in0=u_sb[:], in1=s_sb[:], op=ALU.add))(),
                    reads=[(t_us, BOXF), (t_ss, BOXF)], writes=[(tqb, BOXF)])
                ns = TCW // 128
                s0 = tc * ns
                kb.dma("sync", f"p3q{tc % 2}", (lambda qb=qb, s0=s0, ns=ns, kv=kv, j=j:
                    lambda e: e.dma_start(
                        out=qT4_s[s0:s0 + ns, kv, :, j * 128:(j + 1) * 128].rearrange("s p c -> p s c"),
                        in_=qb.rearrange("p (s c) -> p s c", c=128)))(),
                    reads=[(tqb, BOXF)],
                    writes=[(t_q4, (0, 128, (s0 * NKV + kv) * 512 + j * 128,
                                    ((s0 + ns - 1) * NKV + kv) * 512 + (j + 1) * 128))])

        if "qT4" in dbg:
            kb.dma("sync", "dbgq", lambda e: e.dma_start(out=dbg_out["qT4"][:], in_=qT4_s[:]),
                   reads=[(t_q4, BOXF)])
        kb.barrier()
    _phase3()

    def _phase4():
      with ExitStack() as ph:
        kt_sb = [ph.enter_context(nc.sbuf_tensor(f"p4k{i}", [128, NT * 128], BF16)) for i in range(2)]
        v_sb = [ph.enter_context(nc.sbuf_tensor(f"p4v{i}", [128, NT * 128], BF16)) for i in range(2)]
        q4 = [ph.enter_context(nc.sbuf_tensor(f"p4q{i}", [128, 512], BF16)) for i in range(2)]
        msk = ph.enter_context(nc.sbuf_tensor("p4m", [128, NSLOT * 2 * 512], BF16))
        exp_sb = [ph.enter_context(nc.sbuf_tensor(f"p4e{i}", [128, 1024], BF16)) for i in range(3)]
        expm = ph.enter_context(nc.sbuf_tensor("p4em", [128, 1024], BF16))
        dls = ph.enter_context(nc.sbuf_tensor("p4dls", [128, 512], F32))
        dinv = [ph.enter_context(nc.sbuf_tensor(f"p4dinv{i}", [128, 512], F32)) for i in range(2)]
        avo = [ph.enter_context(nc.sbuf_tensor(f"p4av{i}", [128, 512], BF16)) for i in range(2)]
        ps = [ph.enter_context(nc.psum_tensor(f"p4ps{i}", [128, 1024], F32)) for i in range(3)]
        pav = [ph.enter_context(nc.psum_tensor("p4pav0", [128, 512], F32))] * 2
        pd = [ph.enter_context(nc.psum_tensor("p4pd0", [128, 512], F32))] * 2

        t_k = [TB(f"p4k{i}") for i in range(2)]
        t_vv = [TB(f"p4v{i}") for i in range(2)]
        t_qq = [TB(f"p4q{i}") for i in range(2)]
        t_m = TB("p4m")
        t_e = [TB(f"p4e{i}") for i in range(3)]
        t_em = TB("p4em")
        t_dls = TB("p4dls"); t_dinv = [TB(f"p4dinv{i}") for i in range(2)]
        t_avo = [TB(f"p4avo{i}") for i in range(2)]
        t_ps = [TB(f"p4ps{i}") for i in range(3)]
        t_pav = [TB("p4pav0")] * 2
        t_pd = [TB("p4pd0")] * 2

        kb.dma("sync", "p4m", lambda e: e.dma_start(
            out=msk.rearrange("p (s j c) -> p s j c", j=2, c=512),
            in_=masks.rearrange("s j p c -> p s j c")),
            writes=[(t_m, BOXF)])

        def kvload(kv2):
            kbuf2, tkb2 = kt_sb[kv2 % 2], t_k[kv2 % 2]
            vbuf2, tvb2 = v_sb[kv2 % 2], t_vv[kv2 % 2]
            kb.dma("sync", f"p4kt{kv2 % 2}", (lambda kv2=kv2, kbuf2=kbuf2: lambda e: e.dma_start(
                out=kbuf2.rearrange("p (t c) -> p t c", c=128),
                in_=kT_s[kv2].rearrange("t p c -> p t c")))(),
                reads=[(t_kT, (0, 128, kv2 * NT * 128, (kv2 + 1) * NT * 128))],
                writes=[(tkb2, BOXF)])
            kb.dma("sync", f"p4vt{kv2 % 2}", (lambda kv2=kv2, vbuf2=vbuf2: lambda e: e.dma_start(
                out=vbuf2.rearrange("p (t c) -> p t c", c=128),
                in_=v_s[kv2].rearrange("t p c -> p t c")))(),
                reads=[(t_v, (0, 128, kv2 * NT * 128, (kv2 + 1) * NT * 128))],
                writes=[(tvb2, BOXF)])

        kvload(0)
        for kv in range(NKV):
            kbi = kv % 2
            kbuf, tkb = kt_sb[kbi], t_k[kbi]
            vbuf, tvb = v_sb[kbi], t_vv[kbi]
            def qload(sl2, kv2):
                qbuf2, tqb2 = q4[sl2 % 2], t_qq[sl2 % 2]
                kb.dma("sync", f"p4q{sl2 % 2}", (lambda sl2=sl2, kv2=kv2, qbuf2=qbuf2: lambda e: e.dma_start(
                    out=qbuf2[:], in_=qT4_s[sl2, kv2]))(),
                    reads=[(t_q4, (0, 128, (sl2 * NKV + kv2) * 512, (sl2 * NKV + kv2 + 1) * 512))],
                    writes=[(tqb2, BOXF)])

            qload(0, kv)
            for sl in range(NSLOT):
                qbi = sl % 2
                qbuf, tqb = q4[qbi], t_qq[qbi]
                if sl + 1 < NSLOT:
                    qload(sl + 1, kv)
                if sl == NSLOT // 2 and kv + 1 < NKV:
                    kvload(kv + 1)
                KT = 2 * (sl + 1)
                pavb, tpavb = pav[qbi], t_pav[qbi]
                pdb, tpdb = pd[qbi], t_pd[qbi]
                npair = KT // 2

                def scores_pair(pr):
                    pp = pr % 3
                    psb, tpsb = ps[pp], t_ps[pp]
                    for half in range(2):
                        kt = 2 * pr + half
                        kb.op("tensor", (lambda kt=kt, half=half, psb=psb, kbuf=kbuf, qbuf=qbuf:
                            lambda t: t.matmul(
                                psb[:, half * 512:(half + 1) * 512],
                                kbuf[:, kt * 128:(kt + 1) * 128], qbuf[:],
                                start=True, stop=True, skip_group_check=True))(),
                            reads=[(tkb, (0, 128, kt * 128, (kt + 1) * 128)), (tqb, BOXF)],
                            writes=[(tpsb, (0, 128, half * 512, (half + 1) * 512))],
                            inc=(half == 1))
                    eb, teb = exp_sb[pp], t_e[pp]
                    kb.op("scalar", (lambda psb=psb, eb=eb: lambda s: s.activation(
                        out=eb[:], in_=psb[:], func=AF.Exp,
                        bias=shift[:], scale=float(ISQ)))(),
                        reads=[(tpsb, BOXF), (t_shift, BOXF)],
                        writes=[(teb, BOXF)])

                def av_pair(pr):
                    pp = pr % 3
                    eb, teb = exp_sb[pp], t_e[pp]
                    if pr == npair - 1:
                        mc = sl * 2 * 512
                        kb.op("vector", (lambda eb=eb, mc=mc: lambda v: v.tensor_tensor(
                            out=expm[:], in0=eb[:], in1=msk[:, mc:mc + 1024],
                            op=ALU.mult))(),
                            reads=[(teb, BOXF), (t_m, (0, 128, mc, mc + 1024))],
                            writes=[(t_em, BOXF)])
                        rhs_b, trhs = expm, t_em
                    else:
                        rhs_b, trhs = eb, teb
                    for half in range(2):
                        kt = 2 * pr + half
                        kb.op("tensor", (lambda kt=kt, KT=KT, half=half, vbuf=vbuf, rhs_b=rhs_b, pavb=pavb:
                            lambda t: t.matmul(
                                pavb[:], vbuf[:, kt * 128:(kt + 1) * 128],
                                rhs_b[:, half * 512:(half + 1) * 512],
                                start=(kt == 0), stop=(kt == KT - 1)))(),
                            reads=[(tvb, (0, 128, kt * 128, (kt + 1) * 128)),
                                   (trhs, (0, 128, half * 512, (half + 1) * 512))],
                            writes=[(tpavb, BOXF)], inc=(kt == KT - 1))
                        kb.op("tensor", (lambda kt=kt, KT=KT, half=half, rhs_b=rhs_b, pdb=pdb:
                            lambda t: t.matmul(
                                pdb[:], ones[:], rhs_b[:, half * 512:(half + 1) * 512],
                                start=(kt == 0), stop=(kt == KT - 1)))(),
                            reads=[(t_ones, BOXF),
                                   (trhs, (0, 128, half * 512, (half + 1) * 512))],
                            writes=[(tpdb, BOXF)], inc=(kt == KT - 1))

                for pr in range(min(2, npair)):
                    scores_pair(pr)
                for pr in range(npair):
                    if pr + 2 < npair:
                        scores_pair(pr + 2)
                    av_pair(pr)
                kb.op("scalar", (lambda pdb=pdb: lambda s: s.activation(
                    out=dls[:], in_=pdb[:], func=AF.Ln))(),
                    reads=[(tpdb, BOXF)], writes=[(t_dls, BOXF)])
                dib, tdib = dinv[qbi], t_dinv[qbi]
                kb.op("scalar", (lambda dib=dib: lambda s: s.activation(
                    out=dib[:], in_=dls[:], func=AF.Exp, scale=-1.0))(),
                    reads=[(t_dls, BOXF)], writes=[(tdib, BOXF)])
                ab, tab = avo[qbi], t_avo[qbi]
                kb.op("vector", (lambda pavb=pavb, ab=ab, dib=dib: lambda v: v.tensor_tensor(
                    out=ab[:], in0=pavb[:], in1=dib[:], op=ALU.mult))(),
                    reads=[(tpavb, BOXF), (tdib, BOXF)], writes=[(tab, BOXF)])
                kb.dma("sync", f"p4av{qbi}", (lambda sl=sl, kv=kv, ab=ab: lambda e: e.dma_start(
                    out=avT_s[sl, kv], in_=ab[:]))(),
                    reads=[(tab, BOXF)],
                    writes=[(t_avT, (0, 128, (sl * NKV + kv) * 512, (sl * NKV + kv + 1) * 512))])

        if "avT" in dbg:
            kb.dma("sync", "dbgav", lambda e: e.dma_start(out=dbg_out["avT"][:], in_=avT_s[:]),
                   reads=[(t_avT, BOXF)])
        kb.barrier()
    _phase4()

    SPT5 = min(4, NSLOT)
    NTTB5 = NSLOT // SPT5

    def _phase5():
      with ExitStack() as ph:
        avT_sb = ph.enter_context(nc.sbuf_tensor("p5avT", [128, SPT5 * NKV * 512], BF16))
        wo_sb = [ph.enter_context(nc.sbuf_tensor(f"p5wo{i}", [128, (NH // 2 if NH > 1 else 1) * OCW], BF16)) for i in range(2)]
        a_sb = [ph.enter_context(nc.sbuf_tensor(f"p5a{i}", [128, D], BF16)) for i in range(2 * SPT5)]
        xo = [ph.enter_context(nc.sbuf_tensor("p5xo0", [128, D], F32))] * 2
        t1 = ph.enter_context(nc.sbuf_tensor("p5t1", [128, D], F32))
        x2b = [ph.enter_context(nc.sbuf_tensor("p5x20", [128, D], F32))] * 2
        h2 = [ph.enter_context(nc.sbuf_tensor(f"p5h2_{i}", [128, D], BF16)) for i in range(2)]
        h2T = ph.enter_context(nc.sbuf_tensor("p5h2T", [128, HC * 128], BF16))
        stats = ph.enter_context(nc.sbuf_tensor("p5stats", [128, 6 * max(1, D // 512)], F32))
        mv = ph.enter_context(nc.sbuf_tensor("p5mv", [128, 2], F32))
        vtmp = ph.enter_context(nc.sbuf_tensor("p5vtmp", [128, 1], F32))
        rstd = ph.enter_context(nc.sbuf_tensor("p5rstd", [128, 1], F32))
        negmr = ph.enter_context(nc.sbuf_tensor("p5negmr", [128, 1], F32))
        pw = [ph.enter_context(nc.psum_tensor(f"p5pw{i}", [128, OCW], F32)) for i in range(SPT5)]

        t_avs = TB("p5avT"); t_wo = [TB(f"p5wo{i}") for i in range(2)]
        t_a = [TB(f"p5a{i}") for i in range(2 * SPT5)]
        t_xo = [TB("p5xo0")] * 2
        t_t1 = TB("p5t1"); t_x2b = [TB("p5x20")] * 2
        t_h2 = [TB(f"p5h2_{i}") for i in range(2)]; t_h2Tb = TB("p5h2T")
        t_stats = TB("p5stats"); t_mv = TB("p5mv"); t_vt = TB("p5vt")
        t_rstd = TB("p5rstd"); t_negmr = TB("p5negmr")
        t_pw = [TB(f"p5pw{i}") for i in range(SPT5)]

        nchunk = max(1, D // 512)
        cw = min(512, D)

        def ln_chain(src_buf, t_src, out_buf, t_out, out_dt_note=None):
            for ch in range(nchunk):
                kb.op("vector", (lambda ch=ch: lambda v: v.bn_stats(
                    out=stats[:, ch * 6:(ch + 1) * 6], in_=src_buf[:, ch * cw:(ch + 1) * cw]))(),
                    reads=[(t_src, BOXF)], writes=[(t_stats, (0, 128, ch * 6, ch * 6 + 6))])
            kb.op("vector", lambda v: v.bn_aggr(out=mv[:], in_=stats[:]),
                  reads=[(t_stats, BOXF)], writes=[(t_mv, BOXF)])
            kb.op("vector", lambda v: v.tensor_scalar(
                out=vtmp[:], in0=mv[:, 1:2], scalar1=LN_EPS, scalar2=None, op0=ALU.add),
                reads=[(t_mv, BOXF)], writes=[(t_vt, BOXF)])
            kb.op("scalar", lambda s: s.activation(out=vtmp[:], in_=vtmp[:], func=AF.Ln),
                  reads=[(t_vt, BOXF)], writes=[(t_vt, BOXF)])
            kb.op("scalar", lambda s: s.activation(out=rstd[:], in_=vtmp[:], func=AF.Exp, scale=-0.5),
                  reads=[(t_vt, BOXF)], writes=[(t_rstd, BOXF)])
            kb.op("vector", lambda v: v.tensor_scalar(
                out=negmr[:], in0=mv[:, 0:1], scalar1=rstd[:], scalar2=-1.0,
                op0=ALU.mult, op1=ALU.mult),
                reads=[(t_mv, BOXF), (t_rstd, BOXF)], writes=[(t_negmr, BOXF)])
            kb.op("scalar", lambda sc: sc.activation(
                out=out_buf[:], in_=src_buf[:], func=AF.Identity,
                scale=rstd[:], bias=negmr[:]),
                reads=[(t_src, BOXF), (t_rstd, BOXF), (t_negmr, BOXF)],
                writes=[(t_out, BOXF)])

        def load_avt(ttb):
            for s in range(SPT5):
                sl = ttb * SPT5 + s
                kb.dma("sync", f"p5av{s % 2}", (lambda sl=sl, s=s: lambda e: e.dma_start(
                    out=avT_sb[:, s * NKV * 512:(s + 1) * NKV * 512].rearrange("p (k c) -> p k c", c=512),
                    in_=avT_s[sl].rearrange("k p c -> p k c")))(),
                    reads=[(t_avT, (0, 128, sl * NKV * 512, (sl + 1) * NKV * 512))],
                    writes=[(t_avs, (0, 128, s * NKV * 512, (s + 1) * NKV * 512))])

        def ln_stage5(ttb, only_s=None):
            for s in range(SPT5):
                if only_s is not None and s != only_s:
                    continue
                sl = ttb * SPT5 + s
                si = (ttb % 2) * SPT5 + s
                xb_, txb = xo[s % 2], t_xo[s % 2]
                kb.dma("gpsimd", f"p5x{s % 2}", (lambda sl=sl, xb_=xb_: lambda e: e.dma_start(
                    out=xb_[:], in_=x_own[sl]))(), writes=[(txb, BOXF)])
                ln_chain(a_sb[si], t_a[si], t1, t_t1)
                x2buf, tx2 = x2b[s % 2], t_x2b[s % 2]
                kb.op("gpsimd", (lambda x2buf=x2buf, xb_=xb_: lambda g: g.tensor_tensor(
                    out=x2buf[:], in0=t1[:], in1=xb_[:], op=ALU.add))(),
                    reads=[(t_t1, BOXF), (txb, BOXF)], writes=[(tx2, BOXF)])
                kb.dma("gpsimd", f"p5x2o{s % 2}", (lambda sl=sl, x2buf=x2buf: lambda e: e.dma_start(
                    out=x2_s[sl], in_=x2buf[:]))(),
                    reads=[(tx2, BOXF)],
                    writes=[(t_x2, (0, 128, sl * D, (sl + 1) * D))])
                h2b, t_h2b = h2[s % 2], t_h2[s % 2]
                ln_chain(x2buf, tx2, h2b, t_h2b)
                for hc in range(HC):
                    q = "scalar"
                    kb.dma(q, f"p5tr{q}{hc % 2}", (lambda hc=hc, h2b=h2b: lambda e: e.dma_start(
                        out=h2T[:, hc * 128:(hc + 1) * 128],
                        in_=h2b[:, hc * 128:(hc + 1) * 128], transpose=True))(),
                        reads=[(t_h2b, (0, 128, hc * 128, hc * 128 + 128))],
                        writes=[(t_h2Tb, (0, 128, hc * 128, hc * 128 + 128))])
                kb.dma("scalar", f"p5h2T{s % 2}", (lambda sl=sl: lambda e: e.dma_start(
                    out=h2T_s[sl].rearrange("h p c -> p h c"),
                    in_=h2T.rearrange("p (h c) -> p h c", c=128)))(),
                    reads=[(t_h2Tb, BOXF)],
                    writes=[(t_h2T, (0, 128, sl * D, (sl + 1) * D))])

        load_avt(0)
        for ttb in range(NTTB5):
            NHH = max(1, NH // 2)
            for oc in range(NOC):
                for hf in range(2 if NH > 1 else 1):
                    bi5 = (oc * 2 + hf) % 2
                    wob, twob = wo_sb[bi5], t_wo[bi5]
                    kb.dma("sync", f"p5wo{bi5}", (lambda oc=oc, hf=hf, wob=wob: lambda e: e.dma_start(
                        out=wob.rearrange("p (h n) -> p h n", n=OCW),
                        in_=woT_bf[hf * NHH:(hf + 1) * NHH, :, oc * OCW:(oc + 1) * OCW].rearrange("h p n -> p h n")))(),
                        reads=[(t_wobf, (0, 128, hf * NHH * D + oc * OCW, ((hf * NHH + NHH - 1) * D) + (oc + 1) * OCW))],
                        writes=[(twob, BOXF)])
                    for hc in range(hf * NHH, hf * NHH + NHH):
                        for s in range(SPT5):
                            cb = (s * NKV * 512) + (hc // 4) * 512 + (hc % 4) * 128
                            kb.op("tensor", (lambda hc=hc, hf=hf, s=s, cb=cb, wob=wob: lambda t: t.matmul(
                                pw[s][:], avT_sb[:, cb:cb + 128],
                                wob[:, (hc - hf * NHH) * OCW:(hc - hf * NHH + 1) * OCW],
                                start=(hc == 0), stop=(hc == NH - 1)))(),
                                reads=[(t_avs, (0, 128, cb, cb + 128)),
                                       (twob, (0, 128, (hc - hf * NHH) * OCW, (hc - hf * NHH + 1) * OCW))],
                                writes=[(t_pw[s], BOXF)],
                                inc=(hc == NH - 1))
                for s in range(SPT5):
                    si = (ttb % 2) * SPT5 + s
                    kb.op("vector", (lambda si=si, s=s, oc=oc: lambda v: v.tensor_copy(
                        out=a_sb[si][:, oc * OCW:(oc + 1) * OCW], in_=pw[s][:]))(),
                        reads=[(t_pw[s], BOXF)],
                        writes=[(t_a[si], (0, 128, oc * OCW, (oc + 1) * OCW))])
                if ttb > 0 and oc < min(NOC, SPT5):
                    ln_stage5(ttb - 1, only_s=oc)
            if ttb + 1 < NTTB5:
                load_avt(ttb + 1)
            if ttb > 0:
                for s_rem in range(min(NOC, SPT5), SPT5):
                    ln_stage5(ttb - 1, only_s=s_rem)
            if ttb == NTTB5 - 1:
                ln_stage5(ttb)

        if "x2" in dbg:
            kb.dma("sync", "dbgx2", lambda e: e.dma_start(out=dbg_out["x2"][:], in_=x2_s[:]),
                   reads=[(t_x2, BOXF)])
        if "h2T" in dbg:
            kb.dma("sync", "dbgh2T", lambda e: e.dma_start(out=dbg_out["h2T"][:], in_=h2T_s[:]),
                   reads=[(t_h2T, BOXF)])
        kb.barrier()
    if cfg.get("stop_after", 9) >= 5:
        _phase5()

    TW6 = SPT5 * 128

    def _phase6():
      with ExitStack() as ph:
        h2T_sb = [ph.enter_context(nc.sbuf_tensor(f"p6h2T{i}", [128, HC * TW6], BF16)) for i in range(2)]
        wb_sb = [ph.enter_context(nc.sbuf_tensor(f"p6wb{i}", [128, max(1, HC // 4) * OCW], BF16)) for i in range(2)]
        f_sb = [ph.enter_context(nc.sbuf_tensor(f"p6f{i}", [128, D], BF16)) for i in range(2 * SPT5)]
        aT_sb = ph.enter_context(nc.sbuf_tensor("p6aT", [128, HC * 32], BF16))
        bT_sb = ph.enter_context(nc.sbuf_tensor("p6bT", [32, D], BF16))
        rw_sb = ph.enter_context(nc.sbuf_tensor("p6rw", [128, HC * 4], BF16))
        rb_sb = ph.enter_context(nc.sbuf_tensor("p6rb", [128, 4], F32))
        pr_sb = ph.enter_context(nc.sbuf_tensor("p6pr", [128, 4], F32))
        e2_sb = ph.enter_context(nc.sbuf_tensor("p6e2", [4, 32], BF16))
        bb_sb = ph.enter_context(nc.sbuf_tensor("p6bb", [128, D], BF16))
        rl = ph.enter_context(nc.sbuf_tensor("p6rl", [128, 4], F32))
        re = ph.enter_context(nc.sbuf_tensor("p6re", [128, 4], F32))
        rp = ph.enter_context(nc.sbuf_tensor("p6rp", [128, 4], F32))
        rs = ph.enter_context(nc.sbuf_tensor("p6rs", [128, 1], F32))
        rr = ph.enter_context(nc.sbuf_tensor("p6rr", [128, 1], F32))
        alpha = ph.enter_context(nc.sbuf_tensor("p6alpha", [128, 4], BF16))
        alpha_fm = ph.enter_context(nc.sbuf_tensor("p6afm", [4, TW6], BF16))
        zf = ph.enter_context(nc.sbuf_tensor("p6zf", [32, TW6], BF16))
        zs = ph.enter_context(nc.sbuf_tensor("p6zs", [32, TW6], BF16))
        x2i = [ph.enter_context(nc.sbuf_tensor("p6x20", [128, D], F32))] * 2
        t1 = ph.enter_context(nc.sbuf_tensor("p6t1", [128, D], BF16))
        yb = [ph.enter_context(nc.sbuf_tensor("p6y0", [128, D], F32))] * 2
        stats = ph.enter_context(nc.sbuf_tensor("p6stats", [128, 6 * max(1, D // 512)], F32))
        mv = ph.enter_context(nc.sbuf_tensor("p6mv", [128, 2], F32))
        vtmp = ph.enter_context(nc.sbuf_tensor("p6vtmp", [128, 1], F32))
        rstd = ph.enter_context(nc.sbuf_tensor("p6rstd", [128, 1], F32))
        negmr = ph.enter_context(nc.sbuf_tensor("p6negmr", [128, 1], F32))
        pw = [ph.enter_context(nc.psum_tensor(f"p6pw{i}", [128, OCW], F32)) for i in range(SPT5)]
        prt = ph.enter_context(nc.psum_tensor("p6prt", [128, 512], F32))
        pal = ph.enter_context(nc.psum_tensor("p6pal", [128, 1024], BF16))
        pae = ph.enter_context(nc.psum_tensor("p6pae", [128, 512], F32))
        pzz = ph.enter_context(nc.psum_tensor("p6pzz", [128, 512], F32))

        t_h2s = [TB(f"p6h2T{i}") for i in range(2)]; t_wb = [TB(f"p6wb{i}") for i in range(2)]
        t_f = [TB(f"p6f{i}") for i in range(2 * SPT5)]
        t_aT = TB("p6aT"); t_bT = TB("p6bT"); t_rw = TB("p6rw"); t_rb = TB("p6rb")
        t_pr = TB("p6pr"); t_e2 = TB("p6e2"); t_bb = TB("p6bb")
        t_rl = TB("p6rl"); t_re = TB("p6re"); t_rp = TB("p6rp"); t_rs = TB("p6rs"); t_rr = TB("p6rr")
        t_al = TB("p6alpha"); t_afm = TB("p6afm"); t_zf = TB("p6zf"); t_zs = TB("p6zs")
        t_x2i = [TB("p6x20")] * 2
        t_t1 = TB("p6t1"); t_yb = [TB("p6y0")] * 2
        t_stats = TB("p6stats"); t_mv = TB("p6mv"); t_vt = TB("p6vt")
        t_rstd = TB("p6rstd"); t_negmr = TB("p6negmr")
        t_pw = [TB(f"p6pw{i}") for i in range(SPT5)]
        t_prt = TB("p6prt"); t_pal = TB("p6pal"); t_pae = TB("p6pae"); t_pzz = TB("p6pzz")

        nchunk = max(1, D // 512)
        cw = min(512, D)
        h2T_v = [b.rearrange("p (h w) -> p h w", w=TW6) for b in h2T_sb]

        def ln_chain6(src_buf, t_src, out_buf, t_out):
            for ch in range(nchunk):
                kb.op("vector", (lambda ch=ch: lambda v: v.bn_stats(
                    out=stats[:, ch * 6:(ch + 1) * 6], in_=src_buf[:, ch * cw:(ch + 1) * cw]))(),
                    reads=[(t_src, BOXF)], writes=[(t_stats, (0, 128, ch * 6, ch * 6 + 6))])
            kb.op("vector", lambda v: v.bn_aggr(out=mv[:], in_=stats[:]),
                  reads=[(t_stats, BOXF)], writes=[(t_mv, BOXF)])
            kb.op("vector", lambda v: v.tensor_scalar(
                out=vtmp[:], in0=mv[:, 1:2], scalar1=LN_EPS, scalar2=None, op0=ALU.add),
                reads=[(t_mv, BOXF)], writes=[(t_vt, BOXF)])
            kb.op("scalar", lambda s: s.activation(out=vtmp[:], in_=vtmp[:], func=AF.Ln),
                  reads=[(t_vt, BOXF)], writes=[(t_vt, BOXF)])
            kb.op("scalar", lambda s: s.activation(out=rstd[:], in_=vtmp[:], func=AF.Exp, scale=-0.5),
                  reads=[(t_vt, BOXF)], writes=[(t_rstd, BOXF)])
            kb.op("vector", lambda v: v.tensor_scalar(
                out=negmr[:], in0=mv[:, 0:1], scalar1=rstd[:], scalar2=-1.0,
                op0=ALU.mult, op1=ALU.mult),
                reads=[(t_mv, BOXF), (t_rstd, BOXF)], writes=[(t_negmr, BOXF)])
            kb.op("scalar", lambda sc: sc.activation(
                out=out_buf[:], in_=src_buf[:], func=AF.Identity,
                scale=rstd[:], bias=negmr[:]),
                reads=[(t_src, BOXF), (t_rstd, BOXF), (t_negmr, BOXF)],
                writes=[(t_out, BOXF)])

        kb.dma("gpsimd", "p6aT", lambda e: e.dma_start(
            out=aT_sb.rearrange("p (h n) -> p h n", n=32),
            in_=aT.rearrange("h p n -> p h n")), writes=[(t_aT, BOXF)])
        kb.dma("gpsimd", "p6bT", lambda e: e.dma_start(out=bT_sb[:], in_=bT[:]),
               writes=[(t_bT, BOXF)])
        kb.dma("gpsimd", "p6rw", lambda e: e.dma_start(
            out=rw_sb.rearrange("p (h n) -> p h n", n=4),
            in_=rwT.rearrange("h p n -> p h n")), writes=[(t_rw, BOXF)])
        kb.dma("sync", "p6rb", lambda e: e.dma_start(out=rb_sb[:], in_=rb4[:]),
               writes=[(t_rb, BOXF)])
        kb.dma("sync", "p6pr", lambda e: e.dma_start(out=pr_sb[:], in_=prior[:]),
               writes=[(t_pr, BOXF)])
        kb.op("vector", lambda v: v.tensor_scalar(out=pr_sb[:], in0=pr_sb[:], scalar1=R_EPS, scalar2=None, op0=ALU.add),
              reads=[(t_pr, BOXF)], writes=[(t_pr, BOXF)])
        kb.dma("gpsimd", "p6e2", lambda e: e.dma_start(out=e2_sb[:], in_=exp2[:]),
               writes=[(t_e2, BOXF)])
        kb.dma("gpsimd", "p6bb", lambda e: e.dma_start(out=bb_sb[:], in_=bb[:]),
               writes=[(t_bb, BOXF)])

        def load_h2t(ttb):
            hv2, t_hs2 = h2T_v[ttb % 2], t_h2s[ttb % 2]
            for s in range(SPT5):
                sl = ttb * SPT5 + s
                kb.dma("sync", f"p6h{s % 2}", (lambda sl=sl, s=s, hv2=hv2: lambda e: e.dma_start(
                    out=hv2[:, :, s * 128:(s + 1) * 128],
                    in_=h2T_s[sl].rearrange("h p c -> p h c")))(),
                    reads=[(t_h2T, (0, 128, sl * D, (sl + 1) * D))],
                    writes=[(t_hs2, (0, 128, s * 128, (HC - 1) * TW6 + (s + 1) * 128))])

        def ln3_stage(ttb, only_s=None):
            for s in range(SPT5):
                if only_s is not None and s != only_s:
                    continue
                sl = ttb * SPT5 + s
                si = (ttb % 2) * SPT5 + s
                x2buf, tx2 = x2i[s % 2], t_x2i[s % 2]
                kb.dma("gpsimd", f"p6x2{s % 2}", (lambda sl=sl, x2buf=x2buf: lambda e: e.dma_start(
                    out=x2buf[:], in_=x2_s[sl]))(),
                    reads=[(t_x2, (0, 128, sl * D, (sl + 1) * D))],
                    writes=[(tx2, BOXF)])
                ln_chain6(f_sb[si], t_f[si], t1, t_t1)
                ybuf, tyb = yb[s % 2], t_yb[s % 2]
                kb.op("gpsimd", (lambda ybuf=ybuf, x2buf=x2buf: lambda g: g.tensor_tensor(
                    out=ybuf[:], in0=t1[:], in1=x2buf[:], op=ALU.add))(),
                    reads=[(t_t1, BOXF), (tx2, BOXF)], writes=[(tyb, BOXF)])
                kb.dma("gpsimd", f"p6y{s % 2}", (lambda sl=sl, ybuf=ybuf: lambda e: e.dma_start(
                    out=y[sl], in_=ybuf[:]))(),
                    reads=[(tyb, BOXF)])

        load_h2t(0)
        for ttb in range(NTTB5):
            hv, t_hs = h2T_v[ttb % 2], t_h2s[ttb % 2]
            for s in range(SPT5):
                for hc in range(HC):
                    kb.op("tensor", (lambda hc=hc, s=s, hv=hv: lambda t: t.matmul(
                        prt[:, 0:4], hv[:, hc, s * 128:(s + 1) * 128],
                        rw_sb[:, hc * 4:(hc + 1) * 4],
                        start=(hc == 0), stop=(hc == HC - 1)))(),
                        reads=[(t_hs, (0, 128, hc * TW6 + s * 128, hc * TW6 + (s + 1) * 128)),
                               (t_rw, (0, 128, hc * 4, (hc + 1) * 4))],
                        writes=[(t_prt, BOXF)], inc=(hc == HC - 1))
                kb.op("vector", lambda v: v.tensor_tensor(
                    out=rl[:], in0=prt[:, 0:4], in1=rb_sb[:], op=ALU.add),
                    reads=[(t_prt, BOXF), (t_rb, BOXF)], writes=[(t_rl, BOXF)])
                kb.op("scalar", lambda s_: s_.activation(out=re[:], in_=rl[:], func=AF.Exp),
                      reads=[(t_rl, BOXF)], writes=[(t_re, BOXF)])
                kb.op("vector", lambda v: v.tensor_tensor(
                    out=rp[:], in0=re[:], in1=pr_sb[:], op=ALU.mult),
                    reads=[(t_re, BOXF), (t_pr, BOXF)], writes=[(t_rp, BOXF)])
                kb.op("vector", lambda v: v.tensor_reduce(
                    out=rs[:], in_=rp[:], axis=AX.X, op=ALU.add),
                    reads=[(t_rp, BOXF)], writes=[(t_rs, BOXF)])
                kb.op("vector", lambda v: v.reciprocal(out=rr[:], in_=rs[:]),
                      reads=[(t_rs, BOXF)], writes=[(t_rr, BOXF)])
                kb.op("vector", lambda v: v.tensor_scalar(
                    out=alpha[:], in0=rp[:], scalar1=rr[:], scalar2=None, op0=ALU.mult),
                    reads=[(t_rp, BOXF), (t_rr, BOXF)], writes=[(t_al, BOXF)])
                kb.op("tensor", lambda t: t.transpose(pal[0:4, 0:128], alpha[:], ident[:]),
                      reads=[(t_al, BOXF), (t_ident, BOXF)], writes=[(t_pal, BOXF)])
                kb.op("vector", (lambda s=s: lambda v: v.tensor_copy(
                    out=alpha_fm[:, s * 128:(s + 1) * 128], in_=pal[0:4, 0:128]))(),
                    reads=[(t_pal, BOXF)],
                    writes=[(t_afm, (0, 4, s * 128, (s + 1) * 128))])
            kb.op("tensor", lambda t: t.matmul(
                pae[0:32, 0:TW6], e2_sb[:], alpha_fm[:], start=True, stop=True),
                reads=[(t_e2, BOXF), (t_afm, BOXF)], writes=[(t_pae, BOXF)])
            for hc in range(HC):
                kb.op("tensor", (lambda hc=hc, hv=hv: lambda t: t.matmul(
                    pzz[0:32, 0:TW6], aT_sb[:, hc * 32:(hc + 1) * 32],
                    hv[:, hc, :], start=(hc == 0), stop=(hc == HC - 1)))(),
                    reads=[(t_aT, (0, 128, hc * 32, (hc + 1) * 32)),
                           (t_hs, (0, 128, hc * TW6, (hc + 1) * TW6))],
                    writes=[(t_pzz, BOXF)], inc=(hc == HC - 1))
            kb.op("vector", lambda v: v.tensor_copy(out=zf[:], in_=pzz[0:32, 0:TW6]),
                  reads=[(t_pzz, BOXF)], writes=[(t_zf, BOXF)])
            kb.op("vector", lambda v: v.tensor_tensor(
                out=zs[:], in0=zf[:], in1=pae[0:32, 0:TW6], op=ALU.mult),
                reads=[(t_zf, BOXF), (t_pae, BOXF)], writes=[(t_zs, BOXF)])
            HH = max(1, HC // 4)
            for oc in range(NOC):
                for hf in range(4 if HC >= 4 else 1):
                    bi6 = (oc * 4 + hf) % 2
                    wbb, twbb = wb_sb[bi6], t_wb[bi6]
                    kb.dma("sync", f"p6wb{bi6}", (lambda oc=oc, hf=hf, wbb=wbb: lambda e: e.dma_start(
                        out=wbb.rearrange("p (h n) -> p h n", n=OCW),
                        in_=wbT_bf[hf * HH:(hf + 1) * HH, :, oc * OCW:(oc + 1) * OCW].rearrange("h p n -> p h n")))(),
                        reads=[(t_wbbf, (0, 128, hf * HH * D + oc * OCW, ((hf * HH + HH - 1) * D) + (oc + 1) * OCW))],
                        writes=[(twbb, BOXF)])
                    for hc in range(hf * HH, hf * HH + HH):
                        for s in range(SPT5):
                            kb.op("tensor", (lambda hc=hc, hf=hf, s=s, wbb=wbb, hv=hv: lambda t: t.matmul(
                                pw[s][:], hv[:, hc, s * 128:(s + 1) * 128],
                                wbb[:, (hc - hf * HH) * OCW:(hc - hf * HH + 1) * OCW],
                                start=(hc == 0), stop=False))(),
                                reads=[(t_hs, (0, 128, hc * TW6 + s * 128, hc * TW6 + (s + 1) * 128)),
                                       (twbb, (0, 128, (hc - hf * HH) * OCW, (hc - hf * HH + 1) * OCW))],
                                writes=[(t_pw[s], BOXF)], inc=False)
                for s in range(SPT5):
                    kb.op("tensor", (lambda oc=oc, s=s: lambda t: t.matmul(
                        pw[s][:], zs[:, s * 128:(s + 1) * 128],
                        bT_sb[:, oc * OCW:(oc + 1) * OCW],
                        start=False, stop=True))(),
                        reads=[(t_zs, (0, 32, s * 128, (s + 1) * 128)),
                               (t_bT, (0, 32, oc * OCW, (oc + 1) * OCW))],
                        writes=[(t_pw[s], BOXF)], inc=True)
                for s in range(SPT5):
                    si = (ttb % 2) * SPT5 + s
                    kb.op("vector", (lambda si=si, oc=oc: lambda v: v.tensor_tensor(
                        out=f_sb[si][:, oc * OCW:(oc + 1) * OCW], in0=pw[si % SPT5][:],
                        in1=bb_sb[:, oc * OCW:(oc + 1) * OCW], op=ALU.add))(),
                        reads=[(t_pw[s], BOXF), (t_bb, BOXF)],
                        writes=[(t_f[si], (0, 128, oc * OCW, (oc + 1) * OCW))])
                if ttb > 0 and oc < min(NOC, SPT5):
                    ln3_stage(ttb - 1, only_s=oc)
            if ttb + 1 < NTTB5:
                load_h2t(ttb + 1)
            if ttb > 0:
                for s_rem in range(min(NOC, SPT5), SPT5):
                    ln3_stage(ttb - 1, only_s=s_rem)
            if ttb == NTTB5 - 1:
                ln3_stage(ttb)
    if cfg.get("stop_after", 9) >= 6:
        _phase6()

    kb.finalize_wait_all_dma()
    block = es.enter_context(nc.Block())
    kb.replay(block)
    es.close()
    return nc



def rope_tables(S, HD=128):
    inv = 1.0 / (10000.0 ** (np.arange(0, HD, 2, dtype=np.float32) / HD))
    t = np.arange(S, dtype=np.float32)
    fr = np.outer(t, inv)
    cos64 = np.cos(fr).astype(np.float32).T
    sin64 = np.sin(fr).astype(np.float32).T
    cos128 = np.concatenate([cos64, cos64], 0)
    sin128s = np.concatenate([-sin64, sin64], 0)
    return cos128, sin128s


def own_tiles(parity, NT):
    return [t for t in range(NT) if t % 4 in (parity, 3 - parity)]


def make_masks(parity, cfg):
    S = cfg["S"]; NT = S // 128; NSLOT = NT // 2
    tiles = own_tiles(parity, NT)
    m = np.zeros((NSLOT, 2, 128, 512), np.float32)
    for i, T in enumerate(tiles):
        KT = 2 * (i + 1)
        for j, kt in enumerate((KT - 2, KT - 1)):
            q_glob = T * 128 + np.arange(128)
            k_glob = kt * 128 + np.arange(128)
            mk = (k_glob[:, None] <= q_glob[None, :]).astype(np.float32)
            m[i, j] = np.tile(mk, (1, 4))
    return m.astype(ml_dtypes.bfloat16)


def prep_core_inputs(cfg, parity, xb, Wq, Wk, Wv, Wo, Wb, b_base, lora_A, lora_B,
                     router_W, router_b, domain_prior, active):
    S = cfg["S"]; D = cfg["D"]; NH = cfg["NH"]; NKV = cfg["NKV"]
    NT = S // 128; NSLOT = NT // 2; HC = D // 128
    tiles = own_tiles(parity, NT)
    x_tiles = np.ascontiguousarray(xb.reshape(NT, 128, D))
    x_own = np.ascontiguousarray(x_tiles[tiles])
    cos128, sin128s = rope_tables(S)
    own_cols = np.concatenate([np.arange(T * 128, (T + 1) * 128) for T in tiles])
    act = np.asarray(active)
    d = {
        "x_all": x_tiles,
        "x_own": x_own,
        "wkT": np.ascontiguousarray(Wk.T.reshape(HC, 128, NKV * 128)),
        "wvT": np.ascontiguousarray(Wv.T.reshape(HC, 128, NKV * 128)),
        "wqT": np.ascontiguousarray(Wq.T.reshape(HC, 128, NH * 128)),
        "woT": np.ascontiguousarray(Wo.T.reshape(HC, 128, D)),
        "wbT": np.ascontiguousarray(Wb.T.reshape(HC, 128, D)),
        "bb": np.ascontiguousarray(np.broadcast_to(b_base, (128, D))),
        "aT": np.ascontiguousarray(lora_A[act].transpose(2, 0, 1).reshape(D, 32).reshape(HC, 128, 32)),
        "bT": np.ascontiguousarray(lora_B[act].transpose(0, 2, 1).reshape(32, D)),
        "rwT": np.ascontiguousarray(router_W[act].T.reshape(HC, 128, 4)),
        "prior": np.ascontiguousarray(np.broadcast_to(domain_prior[act], (128, 4))).astype(np.float32),
        "exp2": np.repeat(np.eye(4, dtype=np.float32) * SCALING, 8, axis=1),
        "cosq": np.ascontiguousarray(cos128[:, own_cols]),
        "sinq": np.ascontiguousarray(sin128s[:, own_cols]),
        "cosk": cos128,
        "sink": sin128s,
        "masks": make_masks(parity, cfg),
        "ident": np.eye(128, dtype=ml_dtypes.bfloat16),
        "ones_in": np.ones((128, 128), dtype=ml_dtypes.bfloat16),
        "shift_in": np.full((128, 1), ESHIFT, np.float32),
        "rb4": np.ascontiguousarray(np.broadcast_to(router_b[act], (128, 4))).astype(np.float32),
    }
    return d



REAL_CFG = dict(S=4096, D=4096, NH=32, NKV=8)


def kernel(L0_attn_hidden_states, L0_attn_Wq, L0_attn_Wk, L0_attn_Wv, L0_attn_Wo,
           L0_ffn_W_base, L0_ffn_b_base, L0_ffn_lora_A, L0_ffn_lora_B,
           L0_ffn_router_W, L0_ffn_router_b, L0_ffn_domain_prior, L0_ffn_active):
    from concourse.bass_utils import run_bass_kernel_spmd
    cfg = REAL_CFG
    S = cfg["S"]; D = cfg["D"]
    NT = S // 128
    x = np.asarray(L0_attn_hidden_states)
    act = np.asarray(L0_ffn_active)
    HCr = D // 128

    shared = {
        "wkT": np.ascontiguousarray(L0_attn_Wk.T.reshape(HCr, 128, -1)),
        "wvT": np.ascontiguousarray(L0_attn_Wv.T.reshape(HCr, 128, -1)),
        "wqT": np.ascontiguousarray(L0_attn_Wq.T.reshape(HCr, 128, -1)),
        "woT": np.ascontiguousarray(L0_attn_Wo.T.reshape(HCr, 128, D)),
        "wbT": np.ascontiguousarray(L0_ffn_W_base.T.reshape(HCr, 128, D)),
        "bb": np.ascontiguousarray(np.broadcast_to(L0_ffn_b_base, (128, D))),
        "aT": np.ascontiguousarray(L0_ffn_lora_A[act].transpose(2, 0, 1).reshape(D, 32).reshape(HCr, 128, 32)),
        "bT": np.ascontiguousarray(L0_ffn_lora_B[act].transpose(0, 2, 1).reshape(32, D)),
        "rwT": np.ascontiguousarray(L0_ffn_router_W[act].T.reshape(HCr, 128, 4)),
        "prior": np.ascontiguousarray(np.broadcast_to(L0_ffn_domain_prior[act], (128, 4))).astype(np.float32),
        "rb4": np.ascontiguousarray(np.broadcast_to(L0_ffn_router_b[act], (128, 4))).astype(np.float32),
        "exp2": np.repeat(np.eye(4, dtype=np.float32) * SCALING, 8, axis=1),
        "ident": np.eye(128, dtype=ml_dtypes.bfloat16),
        "ones_in": np.ones((128, 128), dtype=ml_dtypes.bfloat16),
        "shift_in": np.full((128, 1), ESHIFT, np.float32),
    }
    cos128, sin128s = rope_tables(S)
    shared["cosk"] = cos128
    shared["sink"] = sin128s
    par = {}
    for p in (0, 1):
        tiles = own_tiles(p, NT)
        own_cols = np.concatenate([np.arange(T * 128, (T + 1) * 128) for T in tiles])
        par[p] = {
            "cosq": np.ascontiguousarray(cos128[:, own_cols]),
            "sinq": np.ascontiguousarray(sin128s[:, own_cols]),
            "masks": make_masks(p, cfg),
        }
    x_tiles = [np.ascontiguousarray(x[b].reshape(NT, 128, D)) for b in range(4)]
    in_maps = []
    for c in range(8):
        b, p = c // 2, c % 2
        tiles = own_tiles(p, NT)
        m = dict(shared)
        m.update(par[p])
        m["x_all"] = x_tiles[b]
        m["x_own"] = np.ascontiguousarray(x_tiles[b][tiles])
        in_maps.append(m)

    nc = build(cfg)
    res = run_bass_kernel_spmd(nc, in_maps, core_ids=list(range(8)))
    out = np.empty((4, NT, 128, D), np.float32)
    for c in range(8):
        b, p = c // 2, c % 2
        tiles = own_tiles(p, NT)
        out[b, tiles] = res.results[c]["y"]
    return out.reshape(4, S, D)


# version 16
# speedup vs baseline: 1.1095x; 1.0555x over previous
import numpy as np
import ml_dtypes
from contextlib import ExitStack

import concourse.bass as bass
import concourse.mybir as mybir

F32 = mybir.dt.float32
BF16 = mybir.dt.bfloat16
AF = mybir.ActivationFunctionType
ALU = mybir.AluOpType
AX = mybir.AxisListType

ENGINES = ("sync", "scalar", "vector", "gpsimd", "tensor")
ISQ = 1.0 / np.sqrt(128.0)
ESHIFT = -10.0
LN_EPS = 1e-5
R_EPS = 1e-12
SCALING = 2.0



class Rec:
    __slots__ = ("eng", "waits", "fn", "incs", "evt")

    def __init__(self, eng, fn):
        self.eng = eng
        self.fn = fn
        self.waits = []
        self.incs = []
        self.evt = None


class TB:

    def __init__(self, name):
        self.name = name
        self.regs = []


def _overlap(a, b):
    return a[0] < b[1] and b[0] < a[1] and a[2] < b[3] and b[2] < a[3]


class KB:
    def __init__(self, nc, es):
        self.nc = nc
        self.es = es
        self.ops = {e: [] for e in ENGINES}
        self.prog_sem = {e: es.enter_context(nc.semaphore(f"prog_{e}")) for e in ENGINES}
        self.prog_cnt = {e: 0 for e in ENGINES}
        self.pending = {e: [] for e in ENGINES}
        self.watermark = {e: {} for e in ENGINES}
        self.dma_sem = {}
        self.dma_cnt = {}
        self.dma_queue = {}
        self.bar_sem = es.enter_context(nc.semaphore("bar"))
        self.bar_cnt = 0
        self.nbar = 0

    def _resolve(self, rec):
        if rec.evt is not None:
            return rec.evt
        e = rec.eng
        q = self.pending[e]
        idx = q.index(rec)
        for r in q[: idx + 1]:
            self.prog_cnt[e] += 1
            r.evt = ("prog", e, self.prog_cnt[e])
            r.incs.append((self.prog_sem[e], 1))
        del q[: idx + 1]
        return rec.evt

    def _evt_of(self, w):
        if isinstance(w, Rec):
            return self._resolve(w)
        return w

    def _dep(self, rec, w):
        evt = self._evt_of(w)
        kind = evt[0]
        if kind == "prog":
            _, peng, val = evt
            if peng == rec.eng == "tensor":
                return
            key = ("prog", peng)
            sem = self.prog_sem[peng]
        else:
            _, stream, val = evt
            key = ("dma", stream)
            sem = self.dma_sem[stream]
        wm = self.watermark[rec.eng]
        if wm.get(key, 0) >= val:
            return
        wm[key] = val
        rec.waits.append((sem, val))

    def _track(self, rec_or_evt, reads, writes):
        for tb, box in reads:
            deps = [r[4] for r in tb.regs if _overlap(box, r)]
            for w in deps:
                self._dep_any(rec_or_evt, w)
            for r in tb.regs:
                if _overlap(box, r):
                    r[5].append(rec_or_evt)
        for tb, box in writes:
            dead = []
            for r in tb.regs:
                if _overlap(box, r):
                    self._dep_any(rec_or_evt, r[4])
                    for rd in r[5]:
                        self._dep_any(rec_or_evt, rd)
                    if box[0] <= r[0] and box[1] >= r[1] and box[2] <= r[2] and box[3] >= r[3]:
                        dead.append(r)
            for r in dead:
                tb.regs.remove(r)
            tb.regs.append([box[0], box[1], box[2], box[3], rec_or_evt, []])

    def _dep_any(self, consumer, producer):
        if isinstance(consumer, Rec):
            if producer is consumer:
                return
            self._dep(consumer, producer)
        else:
            raise AssertionError("dma consumers handled via their issue rec")

    def op(self, eng, fn, reads=(), writes=(), inc=True):
        rec = Rec(eng, fn)
        self._track(rec, list(reads), list(writes))
        self.ops[eng].append(rec)
        if inc:
            self.pending[eng].append(rec)
            self._resolve(rec)
        else:
            self.pending[eng].append(rec)
        return rec

    def wait_manual(self, eng, w):
        rec = Rec(eng, None)
        self._dep(rec, self._evt_of(w) if not isinstance(w, Rec) else w)
        self.ops[eng].append(rec)
        return rec

    def dma(self, queue, stream, fn, reads=(), writes=()):
        if stream not in self.dma_sem:
            self.dma_sem[stream] = self.es.enter_context(self.nc.semaphore(f"dma_{stream}"))
            self.dma_cnt[stream] = 0
            self.dma_queue[stream] = queue
        assert self.dma_queue[stream] == queue, stream
        prev = self.dma_cnt[stream]
        rec = Rec(queue, fn)
        if prev > 0:
            key = ("dma", stream)
            if self.watermark[queue].get(key, 0) < 16 * prev:
                self.watermark[queue][key] = 16 * prev
                rec.waits.append((self.dma_sem[stream], 16 * prev))
        self.dma_cnt[stream] += 1
        val = 16 * self.dma_cnt[stream]
        evt = ("dma", stream, val)
        for tb, box in reads:
            for r in tb.regs:
                if _overlap(box, r):
                    self._dep(rec, r[4])
                    r[5].append(evt)
        for tb, box in writes:
            dead = []
            for r in tb.regs:
                if _overlap(box, r):
                    self._dep(rec, r[4])
                    for rd in r[5]:
                        self._dep(rec, rd)
                    if box[0] <= r[0] and box[1] >= r[1] and box[2] <= r[2] and box[3] >= r[3]:
                        dead.append(r)
            for r in dead:
                tb.regs.remove(r)
            tb.regs.append([box[0], box[1], box[2], box[3], evt, []])
        rec.incs.append((self.dma_sem[stream], 16))
        self.ops[queue].append(rec)
        return evt

    def barrier(self):
        self.nbar += 1
        tgt = 5 * self.nbar
        for e in ENGINES:
            if self.pending[e]:
                self._resolve(self.pending[e][-1])
            cnt = self.prog_cnt[e]
            rec = Rec(e, lambda eng: eng.nop())
            if cnt > 0:
                key = ("prog", e)
                if self.watermark[e].get(key, 0) < cnt:
                    self.watermark[e][key] = cnt
                    rec.waits.append((self.prog_sem[e], cnt))
            if e == "sync":
                for s, c in self.dma_cnt.items():
                    key = ("dma", s)
                    if c > 0 and self.watermark[e].get(key, 0) < 16 * c:
                        self.watermark[e][key] = 16 * c
                        rec.waits.append((self.dma_sem[s], 16 * c))
            rec.incs.append((self.bar_sem, 1))
            self.ops[e].append(rec)
        for e in ENGINES:
            rec = Rec(e, None)
            rec.waits.append((self.bar_sem, tgt))
            self.watermark[e] = {}
            self.ops[e].append(rec)

    def finalize_wait_all_dma(self):
        rec = Rec("sync", None)
        for s, c in self.dma_cnt.items():
            key = ("dma", s)
            if c > 0 and self.watermark["sync"].get(key, 0) < 16 * c:
                rec.waits.append((self.dma_sem[s], 16 * c))
        self.ops["sync"].append(rec)

    def replay(self, block):
        def section(e):
            def run(eng):
                for rec in self.ops[e]:
                    for sem, val in rec.waits:
                        eng.wait_ge(sem, val)
                    if rec.fn is not None:
                        inst = rec.fn(eng)
                        for sem, n in rec.incs:
                            inst.then_inc(sem, n)
                    else:
                        assert not rec.incs
            return run

        for e in ENGINES:
            getattr(block, e)(section(e))



def build(cfg, dbg=()):
    S = cfg["S"]; D = cfg["D"]; NH = cfg["NH"]; NKV = cfg["NKV"]
    HD = 128
    NT = S // 128
    NSLOT = NT // 2
    HC = D // 128
    KVW = NKV * 128
    G = NH // NKV
    assert G == 4
    OWN = NSLOT * 128
    OCW = min(512, D)
    NOC = D // OCW
    TCW = min(512, OWN)
    NTC = OWN // TCW
    SPT = min(8, NSLOT)
    NTTB = NSLOT // SPT
    QUAD = min(4, SPT)
    NQ = SPT // QUAD
    VOC = min(512, KVW)
    NVOC = KVW // VOC

    nc = bass.Bass("TRN2", target_bir_lowering=False, debug=False)
    es = ExitStack()
    kb = KB(nc, es)

    def dram(name, shape, dt, kind="ExternalInput"):
        t = nc.dram_tensor(name, shape, dt, kind=kind)
        return t.ap()

    x_all = dram("x_all", [NT, 128, D], F32)
    x_own = dram("x_own", [NSLOT, 128, D], F32)
    wkT = dram("wkT", [HC, 128, KVW], F32)
    wvT = dram("wvT", [HC, 128, KVW], F32)
    wqT = dram("wqT", [HC, 128, NH * 128], F32)
    woT = dram("woT", [HC, 128, D], F32)
    wbT = dram("wbT", [HC, 128, D], F32)
    bb = dram("bb", [128, D], F32)
    aT = dram("aT", [HC, 128, 32], F32)
    bT = dram("bT", [32, D], F32)
    rwT = dram("rwT", [HC, 128, 4], F32)
    prior = dram("prior", [128, 4], F32)
    exp2 = dram("exp2", [4, 32], F32)
    cosq = dram("cosq", [128, OWN], F32)
    sinq = dram("sinq", [128, OWN], F32)
    cosk = dram("cosk", [128, S], F32)
    sink = dram("sink", [128, S], F32)
    masks = dram("masks", [NSLOT, 2, 128, 512], BF16)
    ident_in = dram("ident", [128, 128], BF16)
    ones_in = dram("ones_in", [128, 128], BF16)
    shift_in = dram("shift_in", [128, 1], F32)
    rb4 = dram("rb4", [128, 4], F32)

    kT_s = dram("kT_s", [NKV, NT, 128, 128], BF16, kind="Internal")
    woT_bf = dram("woT_bf", [HC, 128, D], BF16, kind="Internal")
    wbT_bf = dram("wbT_bf", [HC, 128, D], BF16, kind="Internal")
    v_s = dram("v_s", [NKV, NT, 128, 128], BF16, kind="Internal")
    qT4_s = dram("qT4_s", [NSLOT, NKV, 128, 512], BF16, kind="Internal")
    avT_s = dram("avT_s", [NSLOT, NKV, 128, 512], BF16, kind="Internal")
    x2_s = dram("x2_s", [NSLOT, 128, D], F32, kind="Internal")
    h2T_s = dram("h2T_s", [NSLOT, HC, 128, 128], BF16, kind="Internal")
    y = dram("y", [NSLOT, 128, D], F32, kind="ExternalOutput")
    dbg_out = {}
    for name in dbg:
        shp = {"kT": [NKV, NT, 128, 128], "v": [NKV, NT, 128, 128],
               "qT4": [NSLOT, NKV, 128, 512], "avT": [NSLOT, NKV, 128, 512],
               "x2": [NSLOT, 128, D], "h2T": [NSLOT, HC, 128, 128]}[name]
        dbg_out[name] = dram("dbg_" + name, shp, BF16 if name not in ("x2",) else F32,
                             kind="ExternalOutput")

    t_kT = TB("kT_s"); t_v = TB("v_s"); t_q4 = TB("qT4_s"); t_avT = TB("avT_s")
    t_x2 = TB("x2_s"); t_h2T = TB("h2T_s")
    t_wobf = TB("woT_bf"); t_wbbf = TB("wbT_bf")

    ident = es.enter_context(nc.sbuf_tensor("ident_sb", [128, 128], BF16))
    ones = es.enter_context(nc.sbuf_tensor("ones_sb", [128, 128], BF16))
    shift = es.enter_context(nc.sbuf_tensor("shift_sb", [128, 1], F32))
    t_ident = TB("ident"); t_ones = TB("ones"); t_shift = TB("shift")
    BOXF = (0, 128, 0, 1 << 30)

    kb.dma("sync", "consts", lambda e: e.dma_start(out=ident[:], in_=ident_in[:]),
           writes=[(t_ident, BOXF)])
    kb.dma("sync", "consts", lambda e: e.dma_start(out=ones[:], in_=ones_in[:]),
           writes=[(t_ones, BOXF)])
    kb.dma("sync", "consts", lambda e: e.dma_start(out=shift[:], in_=shift_in[:]),
           writes=[(t_shift, BOXF)])

    def _phase1():
      with ExitStack() as ph:
        wk_sb = ph.enter_context(nc.sbuf_tensor("wk_sb", [128, HC * KVW], BF16))
        wv_sb = ph.enter_context(nc.sbuf_tensor("wv_sb", [128, HC * KVW], BF16))
        cosk_sb = ph.enter_context(nc.sbuf_tensor("cosk_sb", [128, S], F32))
        sink_sb = ph.enter_context(nc.sbuf_tensor("sink_sb", [128, S], F32))
        xt = [ph.enter_context(nc.sbuf_tensor("p1x0", [128, D], F32))] * 2
        h1 = ph.enter_context(nc.sbuf_tensor("p1h1", [128, D], BF16))
        h1T = ph.enter_context(nc.sbuf_tensor("p1h1T", [128, HC * 128], BF16))
        kro = [ph.enter_context(nc.sbuf_tensor(f"p1kro{i}", [128, KVW], BF16)) for i in range(2)]
        vsb = [ph.enter_context(nc.sbuf_tensor(f"p1v{i}", [128, KVW], BF16)) for i in range(2)]
        u_sb = ph.enter_context(nc.sbuf_tensor("p1u", [128, KVW], BF16))
        s_sb = ph.enter_context(nc.sbuf_tensor("p1s", [128, KVW], BF16))
        stats = ph.enter_context(nc.sbuf_tensor("p1stats", [128, 6 * (D // 512 if D >= 512 else 1)], F32))
        mv = ph.enter_context(nc.sbuf_tensor("p1mv", [128, 2], F32))
        vtmp = ph.enter_context(nc.sbuf_tensor("p1vtmp", [128, 1], F32))
        rstd = ph.enter_context(nc.sbuf_tensor("p1rstd", [128, 1], F32))
        negmr = ph.enter_context(nc.sbuf_tensor("p1negmr", [128, 1], F32))
        pk = [ph.enter_context(nc.psum_tensor(f"p1pk{i}", [128, KVW], F32)) for i in range(2)]
        pv = [ph.enter_context(nc.psum_tensor("p1pv0", [128, KVW], F32))] * 2
        ptr = [ph.enter_context(nc.psum_tensor(f"p1ptr{i}", [128, 1024], BF16)) for i in range(2)]

        t_wk = TB("wk"); t_wv = TB("wv"); t_cosk = TB("cosk"); t_sink = TB("sink")
        t_xt = [TB("p1x0")] * 2
        t_h1 = TB("h1"); t_h1T = TB("h1T")
        t_kro = [TB(f"kro{i}") for i in range(2)]
        t_vsb = [TB(f"vsb{i}") for i in range(2)]
        t_us = TB("u"); t_ss = TB("s")
        t_stats = TB("stats"); t_mv = TB("mv"); t_vt = TB("vtmp"); t_rstd = TB("rstd"); t_negmr = TB("negmr")
        t_pk = [TB(f"pk{i}") for i in range(2)]
        t_pv = [TB("p1pv0")] * 2
        t_ptr = [TB(f"ptr{i}") for i in range(2)]

        nchunk = max(1, D // 512)
        cw = min(512, D)

        for j in range(2):
            h0 = j * (HC // 2); h1_ = (j + 1) * (HC // 2)
            if h0 == h1_:
                continue
            kb.dma("gpsimd", "wk", (lambda h0=h0, h1_=h1_: lambda e: e.dma_start(
                out=wk_sb[:, h0 * KVW: h1_ * KVW].rearrange("p (h n) -> p h n", n=KVW),
                in_=wkT[h0:h1_].rearrange("h p n -> p h n")))(),
                writes=[(t_wk, (0, 128, h0 * KVW, h1_ * KVW))])
            kb.dma("gpsimd", "wv", (lambda h0=h0, h1_=h1_: lambda e: e.dma_start(
                out=wv_sb[:, h0 * KVW: h1_ * KVW].rearrange("p (h n) -> p h n", n=KVW),
                in_=wvT[h0:h1_].rearrange("h p n -> p h n")))(),
                writes=[(t_wv, (0, 128, h0 * KVW, h1_ * KVW))])
        kb.dma("sync", "cosk", lambda e: e.dma_start(out=cosk_sb[:], in_=cosk[:]),
               writes=[(t_cosk, BOXF)])
        kb.dma("sync", "sink", lambda e: e.dma_start(out=sink_sb[:], in_=sink[:]),
               writes=[(t_sink, BOXF)])

        def stageA(gt):
            bi = gt % 2
            xbuf, txb = xt[bi], t_xt[bi]
            pkb, tpkb = pk[bi], t_pk[bi]
            pvb, tpvb = pv[bi], t_pv[bi]
            kb.dma("gpsimd", f"x{bi}", (lambda gt=gt, xbuf=xbuf: lambda e: e.dma_start(
                out=xbuf[:], in_=x_all[gt]))(), writes=[(txb, BOXF)])
            for ch in range(nchunk):
                kb.op("vector", (lambda ch=ch, xbuf=xbuf: lambda v: v.bn_stats(
                    out=stats[:, ch * 6:(ch + 1) * 6], in_=xbuf[:, ch * cw:(ch + 1) * cw]))(),
                    reads=[(txb, BOXF)], writes=[(t_stats, (0, 128, ch * 6, ch * 6 + 6))])
            kb.op("vector", lambda v: v.bn_aggr(out=mv[:], in_=stats[:]),
                  reads=[(t_stats, BOXF)], writes=[(t_mv, BOXF)])
            kb.op("vector", lambda v: v.tensor_scalar(
                out=vtmp[:], in0=mv[:, 1:2], scalar1=LN_EPS, scalar2=None, op0=ALU.add),
                reads=[(t_mv, BOXF)], writes=[(t_vt, BOXF)])
            kb.op("scalar", lambda s: s.activation(out=vtmp[:], in_=vtmp[:], func=AF.Ln),
                  reads=[(t_vt, BOXF)], writes=[(t_vt, BOXF)])
            kb.op("scalar", lambda s: s.activation(out=rstd[:], in_=vtmp[:], func=AF.Exp, scale=-0.5),
                  reads=[(t_vt, BOXF)], writes=[(t_rstd, BOXF)])
            kb.op("vector", lambda v: v.tensor_scalar(
                out=negmr[:], in0=mv[:, 0:1], scalar1=rstd[:], scalar2=-1.0,
                op0=ALU.mult, op1=ALU.mult),
                reads=[(t_mv, BOXF), (t_rstd, BOXF)], writes=[(t_negmr, BOXF)])
            kb.op("scalar", (lambda xbuf=xbuf: lambda sc: sc.activation(
                out=h1[:], in_=xbuf[:], func=AF.Identity,
                scale=rstd[:], bias=negmr[:]))(),
                reads=[(txb, BOXF), (t_rstd, BOXF), (t_negmr, BOXF)],
                writes=[(t_h1, BOXF)])
            for grp in range((HC + 7) // 8):
                pb, tpb = ptr[grp % 2], t_ptr[grp % 2]
                n = min(8, HC - grp * 8)
                for k in range(n):
                    hc = grp * 8 + k
                    kb.op("tensor", (lambda hc=hc, k=k, pb=pb: lambda t: t.transpose(
                        pb[:, k * 128:(k + 1) * 128], h1[:, hc * 128:(hc + 1) * 128], ident[:]))(),
                        reads=[(t_h1, (0, 128, hc * 128, hc * 128 + 128)), (t_ident, BOXF)],
                        writes=[(tpb, (0, 128, k * 128, k * 128 + 128))],
                        inc=(k == n - 1))
                kb.op("scalar", (lambda grp=grp, n=n, pb=pb: lambda sc: sc.copy(
                    out=h1T[:, grp * 1024: grp * 1024 + n * 128], in_=pb[:, 0: n * 128]))(),
                    reads=[(tpb, (0, 128, 0, n * 128))],
                    writes=[(t_h1T, (0, 128, grp * 1024, grp * 1024 + n * 128))])
            for hc in range(HC):
                for kv in range(NKV):
                    kb.op("tensor", (lambda hc=hc, kv=kv, pkb=pkb: lambda t: t.matmul(
                        pkb[:, kv * 128:(kv + 1) * 128],
                        wk_sb[:, (hc * NKV + kv) * 128:(hc * NKV + kv) * 128 + 128],
                        h1T[:, hc * 128:(hc + 1) * 128],
                        start=(hc == 0 and kv % 4 == 0), stop=(hc == HC - 1),
                        skip_group_check=True))(),
                        reads=[(t_wk, (0, 128, (hc * NKV + kv) * 128, (hc * NKV + kv + 1) * 128)),
                               (t_h1T, (0, 128, hc * 128, hc * 128 + 128))],
                        writes=[(tpkb, (0, 128, kv * 128, kv * 128 + 128))],
                        inc=(hc == HC - 1 and kv == NKV - 1))
            for hc in range(HC):
                for oc in range(NVOC):
                    kb.op("tensor", (lambda hc=hc, oc=oc, pvb=pvb: lambda t: t.matmul(
                        pvb[:, oc * VOC:(oc + 1) * VOC],
                        h1T[:, hc * 128:(hc + 1) * 128],
                        wv_sb[:, hc * KVW + oc * VOC: hc * KVW + (oc + 1) * VOC],
                        start=(hc == 0 and (oc * VOC) % 512 == 0), stop=(hc == HC - 1),
                        skip_group_check=True))(),
                        reads=[(t_wv, (0, 128, hc * KVW + oc * VOC, hc * KVW + (oc + 1) * VOC)),
                               (t_h1T, (0, 128, hc * 128, hc * 128 + 128))],
                        writes=[(tpvb, (0, 128, oc * VOC, (oc + 1) * VOC))],
                        inc=(hc == HC - 1 and oc == NVOC - 1))

            vb, tvb = vsb[bi], t_vsb[bi]
            kb.op("scalar", (lambda vb=vb, pvb=pvb: lambda sc: sc.copy(out=vb[:], in_=pvb[:]))(),
                  reads=[(tpvb, BOXF)], writes=[(tvb, BOXF)])
            kb.dma("sync", f"vout{bi}", (lambda gt=gt, vb=vb: lambda e: e.dma_start(
                out=v_s[:, gt].rearrange("k p c -> p k c"), in_=vb.rearrange("p (k c) -> p k c", c=128)))(),
                reads=[(tvb, BOXF)], writes=[(t_v, (0, 128, gt * KVW, (gt + 1) * KVW))])

        def stageB(gt):
            bi = gt % 2
            pkb, tpkb = pk[bi], t_pk[bi]
            ckc = (lambda gt=gt: cosk_sb[:, gt * 128:(gt + 1) * 128])
            skc = (lambda gt=gt: sink_sb[:, gt * 128:(gt + 1) * 128])
            for kv in range(NKV):
                kb.op("vector", (lambda kv=kv, ckc=ckc, pkb=pkb: lambda v: v.tensor_tensor(
                    out=u_sb[:, kv * 128:(kv + 1) * 128], in0=pkb[:, kv * 128:(kv + 1) * 128],
                    in1=ckc(), op=ALU.mult))(),
                    reads=[(tpkb, (0, 128, kv * 128, (kv + 1) * 128)), (t_cosk, BOXF)],
                    writes=[(t_us, (0, 128, kv * 128, (kv + 1) * 128))])
                kb.op("vector", (lambda kv=kv, skc=skc, pkb=pkb: lambda v: v.tensor_tensor(
                    out=s_sb[0:64, kv * 128:(kv + 1) * 128], in0=pkb[64:128, kv * 128:(kv + 1) * 128],
                    in1=skc()[0:64, :], op=ALU.mult))(),
                    reads=[(tpkb, (64, 128, kv * 128, (kv + 1) * 128)), (t_sink, BOXF)],
                    writes=[(t_ss, (0, 64, kv * 128, (kv + 1) * 128))])
                kb.op("vector", (lambda kv=kv, skc=skc, pkb=pkb: lambda v: v.tensor_tensor(
                    out=s_sb[64:128, kv * 128:(kv + 1) * 128], in0=pkb[0:64, kv * 128:(kv + 1) * 128],
                    in1=skc()[64:128, :], op=ALU.mult))(),
                    reads=[(tpkb, (0, 64, kv * 128, (kv + 1) * 128)), (t_sink, BOXF)],
                    writes=[(t_ss, (64, 128, kv * 128, (kv + 1) * 128))])
            kbuf, tkb = kro[bi], t_kro[bi]
            kb.op("vector", (lambda kbuf=kbuf: lambda v: v.tensor_tensor(
                out=kbuf[:], in0=u_sb[:], in1=s_sb[:], op=ALU.add))(),
                reads=[(t_us, BOXF), (t_ss, BOXF)], writes=[(tkb, BOXF)])
            kb.dma("sync", f"kout{bi}", (lambda gt=gt, kbuf=kbuf: lambda e: e.dma_start(
                out=kT_s[:, gt].rearrange("k p c -> p k c"), in_=kbuf.rearrange("p (k c) -> p k c", c=128)))(),
                reads=[(tkb, BOXF)], writes=[(t_kT, (0, 128, gt * KVW, (gt + 1) * KVW))])

        pc_jobs = [("wo", hc) for hc in range(HC)] + [("wb", hc) for hc in range(HC)]
        pc_i = 0

        def precast_some(k):
            nonlocal pc_i
            for _ in range(k):
                if pc_i >= len(pc_jobs):
                    return
                which, hc = pc_jobs[pc_i]; pc_i += 1
                if which == "wo":
                    kb.dma("gpsimd", "pcast", (lambda hc=hc: lambda e: e.dma_start(
                        out=woT_bf[hc], in_=woT[hc]))(),
                        writes=[(t_wobf, (0, 128, hc * D, (hc + 1) * D))])
                else:
                    kb.dma("gpsimd", "pcast", (lambda hc=hc: lambda e: e.dma_start(
                        out=wbT_bf[hc], in_=wbT[hc]))(),
                        writes=[(t_wbbf, (0, 128, hc * D, (hc + 1) * D))])

        for gt in range(NT):
            stageA(gt)
            precast_some((2 * len(pc_jobs) + NT - 1) // NT if NT else 0)
            if gt > 0:
                stageB(gt - 1)
        stageB(NT - 1)
        precast_some(len(pc_jobs))

        if "kT" in dbg:
            kb.dma("sync", "dbgk", lambda e: e.dma_start(out=dbg_out["kT"][:], in_=kT_s[:]),
                   reads=[(t_kT, BOXF)])
        if "v" in dbg:
            kb.dma("sync", "dbgv", lambda e: e.dma_start(out=dbg_out["v"][:], in_=v_s[:]),
                   reads=[(t_v, BOXF)])
        kb.barrier()
    _phase1()

    def _phase3():
      with ExitStack() as ph:
        xt = [ph.enter_context(nc.sbuf_tensor(f"p3x{i}", [128, D], F32)) for i in range(2)]
        h1 = ph.enter_context(nc.sbuf_tensor("p3h1", [128, D], BF16))
        h1T = ph.enter_context(nc.sbuf_tensor("p3h1T", [128, HC * OWN], BF16))
        wq_sb = [ph.enter_context(nc.sbuf_tensor(f"p3wq{i}", [128, HC * 128], BF16)) for i in range(2)]
        cosq_sb = ph.enter_context(nc.sbuf_tensor("p3cosq", [128, OWN], F32))
        sinq_sb = ph.enter_context(nc.sbuf_tensor("p3sinq", [128, OWN], F32))
        u_sb = ph.enter_context(nc.sbuf_tensor("p3u", [128, TCW], BF16))
        s_sb = ph.enter_context(nc.sbuf_tensor("p3s", [128, TCW], BF16))
        qro = [ph.enter_context(nc.sbuf_tensor(f"p3qro{i}", [128, TCW], BF16)) for i in range(2)]
        stats = ph.enter_context(nc.sbuf_tensor("p3stats", [128, 6 * max(1, D // 512)], F32))
        mv = ph.enter_context(nc.sbuf_tensor("p3mv", [128, 2], F32))
        vtmp = ph.enter_context(nc.sbuf_tensor("p3vtmp", [128, 1], F32))
        rstd = ph.enter_context(nc.sbuf_tensor("p3rstd", [128, 1], F32))
        negmr = ph.enter_context(nc.sbuf_tensor("p3negmr", [128, 1], F32))
        pq = [ph.enter_context(nc.psum_tensor(f"p3pq{i}", [128, TCW], F32)) for i in range(2)]
        ptr = [ph.enter_context(nc.psum_tensor(f"p3ptr{i}", [128, 1024], BF16)) for i in range(2)]

        t_xt = [TB(f"p3xt{i}") for i in range(2)]
        t_h1 = TB("p3h1"); t_h1T = TB("p3h1T")
        t_wq = [TB(f"p3wq{i}") for i in range(2)]
        t_cq = TB("p3cosq"); t_sq = TB("p3sinq")
        t_us = TB("p3u"); t_ss = TB("p3s")
        t_qro = [TB(f"p3qro{i}") for i in range(2)]
        t_stats = TB("p3stats"); t_mv = TB("p3mv"); t_vt = TB("p3vt")
        t_rstd = TB("p3rstd"); t_negmr = TB("p3negmr")
        t_pq = [TB(f"p3pq{i}") for i in range(2)]
        t_ptr = [TB(f"p3ptr{i}") for i in range(2)]

        nchunk = max(1, D // 512)
        cw = min(512, D)
        h1T_v = h1T.rearrange("p (h c) -> p h c", c=OWN)

        kb.dma("sync", "cosq", lambda e: e.dma_start(out=cosq_sb[:], in_=cosq[:]),
               writes=[(t_cq, BOXF)])
        kb.dma("sync", "sinq", lambda e: e.dma_start(out=sinq_sb[:], in_=sinq[:]),
               writes=[(t_sq, BOXF)])

        for sl in range(NSLOT):
            bi = sl % 2
            xbuf, txb = xt[bi], t_xt[bi]
            kb.dma("sync", f"p3x{bi}", (lambda sl=sl, xbuf=xbuf: lambda e: e.dma_start(
                out=xbuf[:], in_=x_own[sl]))(), writes=[(txb, BOXF)])
            for ch in range(nchunk):
                kb.op("vector", (lambda ch=ch, xbuf=xbuf: lambda v: v.bn_stats(
                    out=stats[:, ch * 6:(ch + 1) * 6], in_=xbuf[:, ch * cw:(ch + 1) * cw]))(),
                    reads=[(txb, BOXF)], writes=[(t_stats, (0, 128, ch * 6, ch * 6 + 6))])
            kb.op("vector", lambda v: v.bn_aggr(out=mv[:], in_=stats[:]),
                  reads=[(t_stats, BOXF)], writes=[(t_mv, BOXF)])
            kb.op("vector", lambda v: v.tensor_scalar(
                out=vtmp[:], in0=mv[:, 1:2], scalar1=LN_EPS, scalar2=None, op0=ALU.add),
                reads=[(t_mv, BOXF)], writes=[(t_vt, BOXF)])
            kb.op("scalar", lambda s: s.activation(out=vtmp[:], in_=vtmp[:], func=AF.Ln),
                  reads=[(t_vt, BOXF)], writes=[(t_vt, BOXF)])
            kb.op("scalar", lambda s: s.activation(out=rstd[:], in_=vtmp[:], func=AF.Exp, scale=-0.5),
                  reads=[(t_vt, BOXF)], writes=[(t_rstd, BOXF)])
            kb.op("vector", lambda v: v.tensor_scalar(
                out=negmr[:], in0=mv[:, 0:1], scalar1=rstd[:], scalar2=-1.0,
                op0=ALU.mult, op1=ALU.mult),
                reads=[(t_mv, BOXF), (t_rstd, BOXF)], writes=[(t_negmr, BOXF)])
            kb.op("scalar", (lambda xbuf=xbuf: lambda sc: sc.activation(
                out=h1[:], in_=xbuf[:], func=AF.Identity,
                scale=rstd[:], bias=negmr[:]))(),
                reads=[(txb, BOXF), (t_rstd, BOXF), (t_negmr, BOXF)],
                writes=[(t_h1, BOXF)])
            for grp in range((HC + 7) // 8):
                pb, tpb = ptr[grp % 2], t_ptr[grp % 2]
                n = min(8, HC - grp * 8)
                for k in range(n):
                    hc = grp * 8 + k
                    kb.op("tensor", (lambda hc=hc, k=k, pb=pb: lambda t: t.transpose(
                        pb[:, k * 128:(k + 1) * 128], h1[:, hc * 128:(hc + 1) * 128], ident[:]))(),
                        reads=[(t_h1, (0, 128, hc * 128, hc * 128 + 128)), (t_ident, BOXF)],
                        writes=[(tpb, (0, 128, k * 128, k * 128 + 128))],
                        inc=(k == n - 1))
                g0 = grp * 8
                kb.op("scalar", (lambda g0=g0, n=n, pb=pb, sl=sl: lambda sc: sc.copy(
                    out=h1T_v[:, g0:g0 + n, sl * 128:sl * 128 + 128],
                    in_=pb[:, 0:n * 128].rearrange("p (h c) -> p h c", c=128)))(),
                    reads=[(tpb, (0, 128, 0, n * 128))],
                    writes=[(t_h1T, (0, 128, 0, 1 << 30))] if False else
                           [(t_h1T, (0, 128, g0 * OWN + sl * 128, (g0 + n - 1) * OWN + sl * 128 + 128))])

        for h in range(NH):
            bi = h % 2
            wq, twq = wq_sb[bi], t_wq[bi]
            kb.dma("gpsimd", f"p3wq{bi}", (lambda h=h, wq=wq: lambda e: e.dma_start(
                out=wq.rearrange("p (k n) -> p k n", n=128),
                in_=wqT[:, :, h * 128:(h + 1) * 128].rearrange("k p n -> p k n")))(),
                writes=[(twq, BOXF)])
            kv, j = h // 4, h % 4
            for tc in range(NTC):
                pqb, tpq = pq[tc % 2], t_pq[tc % 2]
                for hc in range(HC):
                    kb.op("tensor", (lambda hc=hc, tc=tc, pqb=pqb, wq=wq: lambda t: t.matmul(
                        pqb[:], wq[:, hc * 128:(hc + 1) * 128],
                        h1T[:, hc * OWN + tc * TCW: hc * OWN + (tc + 1) * TCW],
                        start=(hc == 0), stop=(hc == HC - 1)))(),
                        reads=[(twq, (0, 128, hc * 128, hc * 128 + 128)),
                               (t_h1T, (0, 128, hc * OWN + tc * TCW, hc * OWN + (tc + 1) * TCW))],
                        writes=[(tpq, BOXF)],
                        inc=(hc == HC - 1))
                kb.op("vector", (lambda pqb=pqb, tc=tc: lambda v: v.tensor_tensor(
                    out=u_sb[:], in0=pqb[:], in1=cosq_sb[:, tc * TCW:(tc + 1) * TCW],
                    op=ALU.mult))(),
                    reads=[(tpq, BOXF), (t_cq, BOXF)], writes=[(t_us, BOXF)])
                kb.op("vector", (lambda pqb=pqb, tc=tc: lambda v: v.tensor_tensor(
                    out=s_sb[0:64, :], in0=pqb[64:128, :],
                    in1=sinq_sb[0:64, tc * TCW:(tc + 1) * TCW], op=ALU.mult))(),
                    reads=[(tpq, BOXF), (t_sq, BOXF)], writes=[(t_ss, (0, 64, 0, TCW))])
                kb.op("vector", (lambda pqb=pqb, tc=tc: lambda v: v.tensor_tensor(
                    out=s_sb[64:128, :], in0=pqb[0:64, :],
                    in1=sinq_sb[64:128, tc * TCW:(tc + 1) * TCW], op=ALU.mult))(),
                    reads=[(tpq, BOXF), (t_sq, BOXF)], writes=[(t_ss, (64, 128, 0, TCW))])
                qb, tqb = qro[tc % 2], t_qro[tc % 2]
                kb.op("vector", (lambda qb=qb: lambda v: v.tensor_tensor(
                    out=qb[:], in0=u_sb[:], in1=s_sb[:], op=ALU.add))(),
                    reads=[(t_us, BOXF), (t_ss, BOXF)], writes=[(tqb, BOXF)])
                ns = TCW // 128
                s0 = tc * ns
                kb.dma("sync", f"p3q{tc % 2}", (lambda qb=qb, s0=s0, ns=ns, kv=kv, j=j:
                    lambda e: e.dma_start(
                        out=qT4_s[s0:s0 + ns, kv, :, j * 128:(j + 1) * 128].rearrange("s p c -> p s c"),
                        in_=qb.rearrange("p (s c) -> p s c", c=128)))(),
                    reads=[(tqb, BOXF)],
                    writes=[(t_q4, (0, 128, (s0 * NKV + kv) * 512 + j * 128,
                                    ((s0 + ns - 1) * NKV + kv) * 512 + (j + 1) * 128))])

        if "qT4" in dbg:
            kb.dma("sync", "dbgq", lambda e: e.dma_start(out=dbg_out["qT4"][:], in_=qT4_s[:]),
                   reads=[(t_q4, BOXF)])
        kb.barrier()
    _phase3()

    def _phase4():
      with ExitStack() as ph:
        kt_sb = [ph.enter_context(nc.sbuf_tensor(f"p4k{i}", [128, NT * 128], BF16)) for i in range(2)]
        v_sb = [ph.enter_context(nc.sbuf_tensor(f"p4v{i}", [128, NT * 128], BF16)) for i in range(2)]
        q4 = [ph.enter_context(nc.sbuf_tensor(f"p4q{i}", [128, 512], BF16)) for i in range(2)]
        msk = ph.enter_context(nc.sbuf_tensor("p4m", [128, NSLOT * 2 * 512], BF16))
        exp_sb = [ph.enter_context(nc.sbuf_tensor(f"p4e{i}", [128, 1024], BF16)) for i in range(3)]
        expm = ph.enter_context(nc.sbuf_tensor("p4em", [128, 1024], BF16))
        dls = ph.enter_context(nc.sbuf_tensor("p4dls", [128, 512], F32))
        dinv = [ph.enter_context(nc.sbuf_tensor(f"p4dinv{i}", [128, 512], F32)) for i in range(2)]
        avo = [ph.enter_context(nc.sbuf_tensor(f"p4av{i}", [128, 512], BF16)) for i in range(2)]
        ps = [ph.enter_context(nc.psum_tensor(f"p4ps{i}", [128, 1024], F32)) for i in range(3)]
        pav = [ph.enter_context(nc.psum_tensor("p4pav0", [128, 512], F32))] * 2
        pd = [ph.enter_context(nc.psum_tensor("p4pd0", [128, 512], F32))] * 2

        t_k = [TB(f"p4k{i}") for i in range(2)]
        t_vv = [TB(f"p4v{i}") for i in range(2)]
        t_qq = [TB(f"p4q{i}") for i in range(2)]
        t_m = TB("p4m")
        t_e = [TB(f"p4e{i}") for i in range(3)]
        t_em = TB("p4em")
        t_dls = TB("p4dls"); t_dinv = [TB(f"p4dinv{i}") for i in range(2)]
        t_avo = [TB(f"p4avo{i}") for i in range(2)]
        t_ps = [TB(f"p4ps{i}") for i in range(3)]
        t_pav = [TB("p4pav0")] * 2
        t_pd = [TB("p4pd0")] * 2

        kb.dma("sync", "p4m", lambda e: e.dma_start(
            out=msk.rearrange("p (s j c) -> p s j c", j=2, c=512),
            in_=masks.rearrange("s j p c -> p s j c")),
            writes=[(t_m, BOXF)])

        def kvload(kv2):
            kbuf2, tkb2 = kt_sb[kv2 % 2], t_k[kv2 % 2]
            vbuf2, tvb2 = v_sb[kv2 % 2], t_vv[kv2 % 2]
            kb.dma("sync", f"p4kt{kv2 % 2}", (lambda kv2=kv2, kbuf2=kbuf2: lambda e: e.dma_start(
                out=kbuf2.rearrange("p (t c) -> p t c", c=128),
                in_=kT_s[kv2].rearrange("t p c -> p t c")))(),
                reads=[(t_kT, (0, 128, kv2 * NT * 128, (kv2 + 1) * NT * 128))],
                writes=[(tkb2, BOXF)])
            kb.dma("sync", f"p4vt{kv2 % 2}", (lambda kv2=kv2, vbuf2=vbuf2: lambda e: e.dma_start(
                out=vbuf2.rearrange("p (t c) -> p t c", c=128),
                in_=v_s[kv2].rearrange("t p c -> p t c")))(),
                reads=[(t_v, (0, 128, kv2 * NT * 128, (kv2 + 1) * NT * 128))],
                writes=[(tvb2, BOXF)])

        kvload(0)
        for kv in range(NKV):
            kbi = kv % 2
            kbuf, tkb = kt_sb[kbi], t_k[kbi]
            vbuf, tvb = v_sb[kbi], t_vv[kbi]
            def qload(sl2, kv2):
                qbuf2, tqb2 = q4[sl2 % 2], t_qq[sl2 % 2]
                kb.dma("sync", f"p4q{sl2 % 2}", (lambda sl2=sl2, kv2=kv2, qbuf2=qbuf2: lambda e: e.dma_start(
                    out=qbuf2[:], in_=qT4_s[sl2, kv2]))(),
                    reads=[(t_q4, (0, 128, (sl2 * NKV + kv2) * 512, (sl2 * NKV + kv2 + 1) * 512))],
                    writes=[(tqb2, BOXF)])

            qload(0, kv)
            for sl in range(NSLOT):
                qbi = sl % 2
                qbuf, tqb = q4[qbi], t_qq[qbi]
                if sl + 1 < NSLOT:
                    qload(sl + 1, kv)
                if sl == NSLOT // 2 and kv + 1 < NKV:
                    kvload(kv + 1)
                KT = 2 * (sl + 1)
                pavb, tpavb = pav[qbi], t_pav[qbi]
                pdb, tpdb = pd[qbi], t_pd[qbi]
                npair = KT // 2

                def scores_pair(pr):
                    pp = pr % 3
                    psb, tpsb = ps[pp], t_ps[pp]
                    for half in range(2):
                        kt = 2 * pr + half
                        kb.op("tensor", (lambda kt=kt, half=half, psb=psb, kbuf=kbuf, qbuf=qbuf:
                            lambda t: t.matmul(
                                psb[:, half * 512:(half + 1) * 512],
                                kbuf[:, kt * 128:(kt + 1) * 128], qbuf[:],
                                start=True, stop=True, skip_group_check=True))(),
                            reads=[(tkb, (0, 128, kt * 128, (kt + 1) * 128)), (tqb, BOXF)],
                            writes=[(tpsb, (0, 128, half * 512, (half + 1) * 512))],
                            inc=(half == 1))
                    eb, teb = exp_sb[pp], t_e[pp]
                    kb.op("scalar", (lambda psb=psb, eb=eb: lambda s: s.activation(
                        out=eb[:], in_=psb[:], func=AF.Exp,
                        bias=shift[:], scale=float(ISQ)))(),
                        reads=[(tpsb, BOXF), (t_shift, BOXF)],
                        writes=[(teb, BOXF)])

                def av_pair(pr):
                    pp = pr % 3
                    eb, teb = exp_sb[pp], t_e[pp]
                    if pr == npair - 1:
                        mc = sl * 2 * 512
                        kb.op("vector", (lambda eb=eb, mc=mc: lambda v: v.tensor_tensor(
                            out=expm[:], in0=eb[:], in1=msk[:, mc:mc + 1024],
                            op=ALU.mult))(),
                            reads=[(teb, BOXF), (t_m, (0, 128, mc, mc + 1024))],
                            writes=[(t_em, BOXF)])
                        rhs_b, trhs = expm, t_em
                    else:
                        rhs_b, trhs = eb, teb
                    for half in range(2):
                        kt = 2 * pr + half
                        kb.op("tensor", (lambda kt=kt, KT=KT, half=half, vbuf=vbuf, rhs_b=rhs_b, pavb=pavb:
                            lambda t: t.matmul(
                                pavb[:], vbuf[:, kt * 128:(kt + 1) * 128],
                                rhs_b[:, half * 512:(half + 1) * 512],
                                start=(kt == 0), stop=(kt == KT - 1)))(),
                            reads=[(tvb, (0, 128, kt * 128, (kt + 1) * 128)),
                                   (trhs, (0, 128, half * 512, (half + 1) * 512))],
                            writes=[(tpavb, BOXF)], inc=(kt == KT - 1))
                        kb.op("tensor", (lambda kt=kt, KT=KT, half=half, rhs_b=rhs_b, pdb=pdb:
                            lambda t: t.matmul(
                                pdb[:], ones[:], rhs_b[:, half * 512:(half + 1) * 512],
                                start=(kt == 0), stop=(kt == KT - 1)))(),
                            reads=[(t_ones, BOXF),
                                   (trhs, (0, 128, half * 512, (half + 1) * 512))],
                            writes=[(tpdb, BOXF)], inc=(kt == KT - 1))

                for pr in range(min(2, npair)):
                    scores_pair(pr)
                for pr in range(npair):
                    if pr + 2 < npair:
                        scores_pair(pr + 2)
                    av_pair(pr)
                kb.op("scalar", (lambda pdb=pdb: lambda s: s.activation(
                    out=dls[:], in_=pdb[:], func=AF.Ln))(),
                    reads=[(tpdb, BOXF)], writes=[(t_dls, BOXF)])
                dib, tdib = dinv[qbi], t_dinv[qbi]
                kb.op("scalar", (lambda dib=dib: lambda s: s.activation(
                    out=dib[:], in_=dls[:], func=AF.Exp, scale=-1.0))(),
                    reads=[(t_dls, BOXF)], writes=[(tdib, BOXF)])
                ab, tab = avo[qbi], t_avo[qbi]
                kb.op("vector", (lambda pavb=pavb, ab=ab, dib=dib: lambda v: v.tensor_tensor(
                    out=ab[:], in0=pavb[:], in1=dib[:], op=ALU.mult))(),
                    reads=[(tpavb, BOXF), (tdib, BOXF)], writes=[(tab, BOXF)])
                kb.dma("sync", f"p4av{qbi}", (lambda sl=sl, kv=kv, ab=ab: lambda e: e.dma_start(
                    out=avT_s[sl, kv], in_=ab[:]))(),
                    reads=[(tab, BOXF)],
                    writes=[(t_avT, (0, 128, (sl * NKV + kv) * 512, (sl * NKV + kv + 1) * 512))])

        if "avT" in dbg:
            kb.dma("sync", "dbgav", lambda e: e.dma_start(out=dbg_out["avT"][:], in_=avT_s[:]),
                   reads=[(t_avT, BOXF)])
        kb.barrier()
    _phase4()

    SPT5 = min(4, NSLOT)
    NTTB5 = NSLOT // SPT5

    def _phase5():
      with ExitStack() as ph:
        avT_sb = ph.enter_context(nc.sbuf_tensor("p5avT", [128, SPT5 * NKV * 512], BF16))
        wo_sb = [ph.enter_context(nc.sbuf_tensor(f"p5wo{i}", [128, (NH // 2 if NH > 1 else 1) * OCW], BF16)) for i in range(2)]
        a_sb = [ph.enter_context(nc.sbuf_tensor(f"p5a{i}", [128, D], BF16)) for i in range(2 * SPT5)]
        xo = [ph.enter_context(nc.sbuf_tensor("p5xo0", [128, D], F32))] * 2
        t1 = ph.enter_context(nc.sbuf_tensor("p5t1", [128, D], F32))
        x2b = [ph.enter_context(nc.sbuf_tensor("p5x20", [128, D], F32))] * 2
        h2 = [ph.enter_context(nc.sbuf_tensor(f"p5h2_{i}", [128, D], BF16)) for i in range(2)]
        h2T = ph.enter_context(nc.sbuf_tensor("p5h2T", [128, HC * 128], BF16))
        stats = ph.enter_context(nc.sbuf_tensor("p5stats", [128, 6 * max(1, D // 512)], F32))
        mv = ph.enter_context(nc.sbuf_tensor("p5mv", [128, 2], F32))
        vtmp = ph.enter_context(nc.sbuf_tensor("p5vtmp", [128, 1], F32))
        rstd = ph.enter_context(nc.sbuf_tensor("p5rstd", [128, 1], F32))
        negmr = ph.enter_context(nc.sbuf_tensor("p5negmr", [128, 1], F32))
        pw = [ph.enter_context(nc.psum_tensor(f"p5pw{i}", [128, OCW], F32)) for i in range(SPT5)]

        t_avs = TB("p5avT"); t_wo = [TB(f"p5wo{i}") for i in range(2)]
        t_a = [TB(f"p5a{i}") for i in range(2 * SPT5)]
        t_xo = [TB("p5xo0")] * 2
        t_t1 = TB("p5t1"); t_x2b = [TB("p5x20")] * 2
        t_h2 = [TB(f"p5h2_{i}") for i in range(2)]; t_h2Tb = TB("p5h2T")
        t_stats = TB("p5stats"); t_mv = TB("p5mv"); t_vt = TB("p5vt")
        t_rstd = TB("p5rstd"); t_negmr = TB("p5negmr")
        t_pw = [TB(f"p5pw{i}") for i in range(SPT5)]

        nchunk = max(1, D // 512)
        cw = min(512, D)

        def ln_chain(src_buf, t_src, out_buf, t_out, out_dt_note=None):
            for ch in range(nchunk):
                kb.op("vector", (lambda ch=ch: lambda v: v.bn_stats(
                    out=stats[:, ch * 6:(ch + 1) * 6], in_=src_buf[:, ch * cw:(ch + 1) * cw]))(),
                    reads=[(t_src, BOXF)], writes=[(t_stats, (0, 128, ch * 6, ch * 6 + 6))])
            kb.op("vector", lambda v: v.bn_aggr(out=mv[:], in_=stats[:]),
                  reads=[(t_stats, BOXF)], writes=[(t_mv, BOXF)])
            kb.op("vector", lambda v: v.tensor_scalar(
                out=vtmp[:], in0=mv[:, 1:2], scalar1=LN_EPS, scalar2=None, op0=ALU.add),
                reads=[(t_mv, BOXF)], writes=[(t_vt, BOXF)])
            kb.op("scalar", lambda s: s.activation(out=vtmp[:], in_=vtmp[:], func=AF.Ln),
                  reads=[(t_vt, BOXF)], writes=[(t_vt, BOXF)])
            kb.op("scalar", lambda s: s.activation(out=rstd[:], in_=vtmp[:], func=AF.Exp, scale=-0.5),
                  reads=[(t_vt, BOXF)], writes=[(t_rstd, BOXF)])
            kb.op("vector", lambda v: v.tensor_scalar(
                out=negmr[:], in0=mv[:, 0:1], scalar1=rstd[:], scalar2=-1.0,
                op0=ALU.mult, op1=ALU.mult),
                reads=[(t_mv, BOXF), (t_rstd, BOXF)], writes=[(t_negmr, BOXF)])
            kb.op("scalar", lambda sc: sc.activation(
                out=out_buf[:], in_=src_buf[:], func=AF.Identity,
                scale=rstd[:], bias=negmr[:]),
                reads=[(t_src, BOXF), (t_rstd, BOXF), (t_negmr, BOXF)],
                writes=[(t_out, BOXF)])

        def load_avt(ttb):
            for s in range(SPT5):
                sl = ttb * SPT5 + s
                kb.dma("sync", f"p5av{s % 2}", (lambda sl=sl, s=s: lambda e: e.dma_start(
                    out=avT_sb[:, s * NKV * 512:(s + 1) * NKV * 512].rearrange("p (k c) -> p k c", c=512),
                    in_=avT_s[sl].rearrange("k p c -> p k c")))(),
                    reads=[(t_avT, (0, 128, sl * NKV * 512, (sl + 1) * NKV * 512))],
                    writes=[(t_avs, (0, 128, s * NKV * 512, (s + 1) * NKV * 512))])

        def ln_stage5(ttb, only_s=None):
            for s in range(SPT5):
                if only_s is not None and s != only_s:
                    continue
                sl = ttb * SPT5 + s
                si = (ttb % 2) * SPT5 + s
                xb_, txb = xo[s % 2], t_xo[s % 2]
                kb.dma("gpsimd", f"p5x{s % 2}", (lambda sl=sl, xb_=xb_: lambda e: e.dma_start(
                    out=xb_[:], in_=x_own[sl]))(), writes=[(txb, BOXF)])
                ln_chain(a_sb[si], t_a[si], t1, t_t1)
                x2buf, tx2 = x2b[s % 2], t_x2b[s % 2]
                kb.op("gpsimd", (lambda x2buf=x2buf, xb_=xb_: lambda g: g.tensor_tensor(
                    out=x2buf[:], in0=t1[:], in1=xb_[:], op=ALU.add))(),
                    reads=[(t_t1, BOXF), (txb, BOXF)], writes=[(tx2, BOXF)])
                kb.dma("gpsimd", f"p5x2o{s % 2}", (lambda sl=sl, x2buf=x2buf: lambda e: e.dma_start(
                    out=x2_s[sl], in_=x2buf[:]))(),
                    reads=[(tx2, BOXF)],
                    writes=[(t_x2, (0, 128, sl * D, (sl + 1) * D))])
                h2b, t_h2b = h2[s % 2], t_h2[s % 2]
                ln_chain(x2buf, tx2, h2b, t_h2b)
                kb.dma("scalar", f"p5tr{s % 2}", (lambda h2b=h2b: lambda e: e.dma_start(
                    out=h2T.rearrange("p (b c) -> p b c", c=128),
                    in_=h2b.rearrange("p (b c) -> p b c", c=128), transpose=True))(),
                    reads=[(t_h2b, BOXF)],
                    writes=[(t_h2Tb, BOXF)])
                kb.dma("scalar", f"p5h2T{s % 2}", (lambda sl=sl: lambda e: e.dma_start(
                    out=h2T_s[sl].rearrange("h p c -> p h c"),
                    in_=h2T.rearrange("p (h c) -> p h c", c=128)))(),
                    reads=[(t_h2Tb, BOXF)],
                    writes=[(t_h2T, (0, 128, sl * D, (sl + 1) * D))])

        load_avt(0)
        for ttb in range(NTTB5):
            NHH = max(1, NH // 2)
            for oc in range(NOC):
                for hf in range(2 if NH > 1 else 1):
                    bi5 = (oc * 2 + hf) % 2
                    wob, twob = wo_sb[bi5], t_wo[bi5]
                    kb.dma("sync", f"p5wo{bi5}", (lambda oc=oc, hf=hf, wob=wob: lambda e: e.dma_start(
                        out=wob.rearrange("p (h n) -> p h n", n=OCW),
                        in_=woT_bf[hf * NHH:(hf + 1) * NHH, :, oc * OCW:(oc + 1) * OCW].rearrange("h p n -> p h n")))(),
                        reads=[(t_wobf, (0, 128, hf * NHH * D + oc * OCW, ((hf * NHH + NHH - 1) * D) + (oc + 1) * OCW))],
                        writes=[(twob, BOXF)])
                    for hc in range(hf * NHH, hf * NHH + NHH):
                        for s in range(SPT5):
                            cb = (s * NKV * 512) + (hc // 4) * 512 + (hc % 4) * 128
                            kb.op("tensor", (lambda hc=hc, hf=hf, s=s, cb=cb, wob=wob: lambda t: t.matmul(
                                pw[s][:], avT_sb[:, cb:cb + 128],
                                wob[:, (hc - hf * NHH) * OCW:(hc - hf * NHH + 1) * OCW],
                                start=(hc == 0), stop=(hc == NH - 1)))(),
                                reads=[(t_avs, (0, 128, cb, cb + 128)),
                                       (twob, (0, 128, (hc - hf * NHH) * OCW, (hc - hf * NHH + 1) * OCW))],
                                writes=[(t_pw[s], BOXF)],
                                inc=(hc == NH - 1))
                for s in range(SPT5):
                    si = (ttb % 2) * SPT5 + s
                    kb.op("vector", (lambda si=si, s=s, oc=oc: lambda v: v.tensor_copy(
                        out=a_sb[si][:, oc * OCW:(oc + 1) * OCW], in_=pw[s][:]))(),
                        reads=[(t_pw[s], BOXF)],
                        writes=[(t_a[si], (0, 128, oc * OCW, (oc + 1) * OCW))])
                if ttb > 0 and oc < min(NOC, SPT5):
                    ln_stage5(ttb - 1, only_s=oc)
            if ttb + 1 < NTTB5:
                load_avt(ttb + 1)
            if ttb > 0:
                for s_rem in range(min(NOC, SPT5), SPT5):
                    ln_stage5(ttb - 1, only_s=s_rem)
            if ttb == NTTB5 - 1:
                ln_stage5(ttb)

        if "x2" in dbg:
            kb.dma("sync", "dbgx2", lambda e: e.dma_start(out=dbg_out["x2"][:], in_=x2_s[:]),
                   reads=[(t_x2, BOXF)])
        if "h2T" in dbg:
            kb.dma("sync", "dbgh2T", lambda e: e.dma_start(out=dbg_out["h2T"][:], in_=h2T_s[:]),
                   reads=[(t_h2T, BOXF)])
        kb.barrier()
    if cfg.get("stop_after", 9) >= 5:
        _phase5()

    TW6 = SPT5 * 128

    def _phase6():
      with ExitStack() as ph:
        h2T_sb = [ph.enter_context(nc.sbuf_tensor(f"p6h2T{i}", [128, HC * TW6], BF16)) for i in range(2)]
        wb_sb = [ph.enter_context(nc.sbuf_tensor(f"p6wb{i}", [128, max(1, HC // 4) * OCW], BF16)) for i in range(2)]
        f_sb = [ph.enter_context(nc.sbuf_tensor(f"p6f{i}", [128, D], BF16)) for i in range(2 * SPT5)]
        aT_sb = ph.enter_context(nc.sbuf_tensor("p6aT", [128, HC * 32], BF16))
        bT_sb = ph.enter_context(nc.sbuf_tensor("p6bT", [32, D], BF16))
        rw_sb = ph.enter_context(nc.sbuf_tensor("p6rw", [128, HC * 4], BF16))
        rb_sb = ph.enter_context(nc.sbuf_tensor("p6rb", [128, 4], F32))
        pr_sb = ph.enter_context(nc.sbuf_tensor("p6pr", [128, 4], F32))
        e2_sb = ph.enter_context(nc.sbuf_tensor("p6e2", [4, 32], BF16))
        bb_sb = ph.enter_context(nc.sbuf_tensor("p6bb", [128, D], BF16))
        rl = ph.enter_context(nc.sbuf_tensor("p6rl", [128, 4], F32))
        re = ph.enter_context(nc.sbuf_tensor("p6re", [128, 4], F32))
        rp = ph.enter_context(nc.sbuf_tensor("p6rp", [128, 4], F32))
        rs = ph.enter_context(nc.sbuf_tensor("p6rs", [128, 1], F32))
        rr = ph.enter_context(nc.sbuf_tensor("p6rr", [128, 1], F32))
        alpha = ph.enter_context(nc.sbuf_tensor("p6alpha", [128, 4], BF16))
        alpha_fm = ph.enter_context(nc.sbuf_tensor("p6afm", [4, TW6], BF16))
        zf = ph.enter_context(nc.sbuf_tensor("p6zf", [32, TW6], BF16))
        zs = ph.enter_context(nc.sbuf_tensor("p6zs", [32, TW6], BF16))
        x2i = [ph.enter_context(nc.sbuf_tensor("p6x20", [128, D], F32))] * 2
        t1 = ph.enter_context(nc.sbuf_tensor("p6t1", [128, D], BF16))
        yb = [ph.enter_context(nc.sbuf_tensor("p6y0", [128, D], F32))] * 2
        stats = ph.enter_context(nc.sbuf_tensor("p6stats", [128, 6 * max(1, D // 512)], F32))
        mv = ph.enter_context(nc.sbuf_tensor("p6mv", [128, 2], F32))
        vtmp = ph.enter_context(nc.sbuf_tensor("p6vtmp", [128, 1], F32))
        rstd = ph.enter_context(nc.sbuf_tensor("p6rstd", [128, 1], F32))
        negmr = ph.enter_context(nc.sbuf_tensor("p6negmr", [128, 1], F32))
        pw = [ph.enter_context(nc.psum_tensor(f"p6pw{i}", [128, OCW], F32)) for i in range(SPT5)]
        prt = ph.enter_context(nc.psum_tensor("p6prt", [128, 512], F32))
        pal = ph.enter_context(nc.psum_tensor("p6pal", [128, 1024], BF16))
        pae = ph.enter_context(nc.psum_tensor("p6pae", [128, 512], F32))
        pzz = ph.enter_context(nc.psum_tensor("p6pzz", [128, 512], F32))

        t_h2s = [TB(f"p6h2T{i}") for i in range(2)]; t_wb = [TB(f"p6wb{i}") for i in range(2)]
        t_f = [TB(f"p6f{i}") for i in range(2 * SPT5)]
        t_aT = TB("p6aT"); t_bT = TB("p6bT"); t_rw = TB("p6rw"); t_rb = TB("p6rb")
        t_pr = TB("p6pr"); t_e2 = TB("p6e2"); t_bb = TB("p6bb")
        t_rl = TB("p6rl"); t_re = TB("p6re"); t_rp = TB("p6rp"); t_rs = TB("p6rs"); t_rr = TB("p6rr")
        t_al = TB("p6alpha"); t_afm = TB("p6afm"); t_zf = TB("p6zf"); t_zs = TB("p6zs")
        t_x2i = [TB("p6x20")] * 2
        t_t1 = TB("p6t1"); t_yb = [TB("p6y0")] * 2
        t_stats = TB("p6stats"); t_mv = TB("p6mv"); t_vt = TB("p6vt")
        t_rstd = TB("p6rstd"); t_negmr = TB("p6negmr")
        t_pw = [TB(f"p6pw{i}") for i in range(SPT5)]
        t_prt = TB("p6prt"); t_pal = TB("p6pal"); t_pae = TB("p6pae"); t_pzz = TB("p6pzz")

        nchunk = max(1, D // 512)
        cw = min(512, D)
        h2T_v = [b.rearrange("p (h w) -> p h w", w=TW6) for b in h2T_sb]

        def ln_chain6(src_buf, t_src, out_buf, t_out):
            for ch in range(nchunk):
                kb.op("vector", (lambda ch=ch: lambda v: v.bn_stats(
                    out=stats[:, ch * 6:(ch + 1) * 6], in_=src_buf[:, ch * cw:(ch + 1) * cw]))(),
                    reads=[(t_src, BOXF)], writes=[(t_stats, (0, 128, ch * 6, ch * 6 + 6))])
            kb.op("vector", lambda v: v.bn_aggr(out=mv[:], in_=stats[:]),
                  reads=[(t_stats, BOXF)], writes=[(t_mv, BOXF)])
            kb.op("vector", lambda v: v.tensor_scalar(
                out=vtmp[:], in0=mv[:, 1:2], scalar1=LN_EPS, scalar2=None, op0=ALU.add),
                reads=[(t_mv, BOXF)], writes=[(t_vt, BOXF)])
            kb.op("scalar", lambda s: s.activation(out=vtmp[:], in_=vtmp[:], func=AF.Ln),
                  reads=[(t_vt, BOXF)], writes=[(t_vt, BOXF)])
            kb.op("scalar", lambda s: s.activation(out=rstd[:], in_=vtmp[:], func=AF.Exp, scale=-0.5),
                  reads=[(t_vt, BOXF)], writes=[(t_rstd, BOXF)])
            kb.op("vector", lambda v: v.tensor_scalar(
                out=negmr[:], in0=mv[:, 0:1], scalar1=rstd[:], scalar2=-1.0,
                op0=ALU.mult, op1=ALU.mult),
                reads=[(t_mv, BOXF), (t_rstd, BOXF)], writes=[(t_negmr, BOXF)])
            kb.op("scalar", lambda sc: sc.activation(
                out=out_buf[:], in_=src_buf[:], func=AF.Identity,
                scale=rstd[:], bias=negmr[:]),
                reads=[(t_src, BOXF), (t_rstd, BOXF), (t_negmr, BOXF)],
                writes=[(t_out, BOXF)])

        kb.dma("gpsimd", "p6aT", lambda e: e.dma_start(
            out=aT_sb.rearrange("p (h n) -> p h n", n=32),
            in_=aT.rearrange("h p n -> p h n")), writes=[(t_aT, BOXF)])
        kb.dma("gpsimd", "p6bT", lambda e: e.dma_start(out=bT_sb[:], in_=bT[:]),
               writes=[(t_bT, BOXF)])
        kb.dma("gpsimd", "p6rw", lambda e: e.dma_start(
            out=rw_sb.rearrange("p (h n) -> p h n", n=4),
            in_=rwT.rearrange("h p n -> p h n")), writes=[(t_rw, BOXF)])
        kb.dma("sync", "p6rb", lambda e: e.dma_start(out=rb_sb[:], in_=rb4[:]),
               writes=[(t_rb, BOXF)])
        kb.dma("sync", "p6pr", lambda e: e.dma_start(out=pr_sb[:], in_=prior[:]),
               writes=[(t_pr, BOXF)])
        kb.op("vector", lambda v: v.tensor_scalar(out=pr_sb[:], in0=pr_sb[:], scalar1=R_EPS, scalar2=None, op0=ALU.add),
              reads=[(t_pr, BOXF)], writes=[(t_pr, BOXF)])
        kb.dma("gpsimd", "p6e2", lambda e: e.dma_start(out=e2_sb[:], in_=exp2[:]),
               writes=[(t_e2, BOXF)])
        kb.dma("gpsimd", "p6bb", lambda e: e.dma_start(out=bb_sb[:], in_=bb[:]),
               writes=[(t_bb, BOXF)])

        def load_h2t(ttb):
            hv2, t_hs2 = h2T_v[ttb % 2], t_h2s[ttb % 2]
            for s in range(SPT5):
                sl = ttb * SPT5 + s
                kb.dma("sync", f"p6h{s % 2}", (lambda sl=sl, s=s, hv2=hv2: lambda e: e.dma_start(
                    out=hv2[:, :, s * 128:(s + 1) * 128],
                    in_=h2T_s[sl].rearrange("h p c -> p h c")))(),
                    reads=[(t_h2T, (0, 128, sl * D, (sl + 1) * D))],
                    writes=[(t_hs2, (0, 128, s * 128, (HC - 1) * TW6 + (s + 1) * 128))])

        def ln3_stage(ttb, only_s=None):
            for s in range(SPT5):
                if only_s is not None and s != only_s:
                    continue
                sl = ttb * SPT5 + s
                si = (ttb % 2) * SPT5 + s
                x2buf, tx2 = x2i[s % 2], t_x2i[s % 2]
                kb.dma("gpsimd", f"p6x2{s % 2}", (lambda sl=sl, x2buf=x2buf: lambda e: e.dma_start(
                    out=x2buf[:], in_=x2_s[sl]))(),
                    reads=[(t_x2, (0, 128, sl * D, (sl + 1) * D))],
                    writes=[(tx2, BOXF)])
                ln_chain6(f_sb[si], t_f[si], t1, t_t1)
                ybuf, tyb = yb[s % 2], t_yb[s % 2]
                kb.op("gpsimd", (lambda ybuf=ybuf, x2buf=x2buf: lambda g: g.tensor_tensor(
                    out=ybuf[:], in0=t1[:], in1=x2buf[:], op=ALU.add))(),
                    reads=[(t_t1, BOXF), (tx2, BOXF)], writes=[(tyb, BOXF)])
                kb.dma("gpsimd", f"p6y{s % 2}", (lambda sl=sl, ybuf=ybuf: lambda e: e.dma_start(
                    out=y[sl], in_=ybuf[:]))(),
                    reads=[(tyb, BOXF)])

        load_h2t(0)
        for ttb in range(NTTB5):
            hv, t_hs = h2T_v[ttb % 2], t_h2s[ttb % 2]
            for s in range(SPT5):
                for hc in range(HC):
                    kb.op("tensor", (lambda hc=hc, s=s, hv=hv: lambda t: t.matmul(
                        prt[:, 0:4], hv[:, hc, s * 128:(s + 1) * 128],
                        rw_sb[:, hc * 4:(hc + 1) * 4],
                        start=(hc == 0), stop=(hc == HC - 1)))(),
                        reads=[(t_hs, (0, 128, hc * TW6 + s * 128, hc * TW6 + (s + 1) * 128)),
                               (t_rw, (0, 128, hc * 4, (hc + 1) * 4))],
                        writes=[(t_prt, BOXF)], inc=(hc == HC - 1))
                kb.op("vector", lambda v: v.tensor_tensor(
                    out=rl[:], in0=prt[:, 0:4], in1=rb_sb[:], op=ALU.add),
                    reads=[(t_prt, BOXF), (t_rb, BOXF)], writes=[(t_rl, BOXF)])
                kb.op("scalar", lambda s_: s_.activation(out=re[:], in_=rl[:], func=AF.Exp),
                      reads=[(t_rl, BOXF)], writes=[(t_re, BOXF)])
                kb.op("vector", lambda v: v.tensor_tensor(
                    out=rp[:], in0=re[:], in1=pr_sb[:], op=ALU.mult),
                    reads=[(t_re, BOXF), (t_pr, BOXF)], writes=[(t_rp, BOXF)])
                kb.op("vector", lambda v: v.tensor_reduce(
                    out=rs[:], in_=rp[:], axis=AX.X, op=ALU.add),
                    reads=[(t_rp, BOXF)], writes=[(t_rs, BOXF)])
                kb.op("vector", lambda v: v.reciprocal(out=rr[:], in_=rs[:]),
                      reads=[(t_rs, BOXF)], writes=[(t_rr, BOXF)])
                kb.op("vector", lambda v: v.tensor_scalar(
                    out=alpha[:], in0=rp[:], scalar1=rr[:], scalar2=None, op0=ALU.mult),
                    reads=[(t_rp, BOXF), (t_rr, BOXF)], writes=[(t_al, BOXF)])
                kb.op("tensor", lambda t: t.transpose(pal[0:4, 0:128], alpha[:], ident[:]),
                      reads=[(t_al, BOXF), (t_ident, BOXF)], writes=[(t_pal, BOXF)])
                kb.op("vector", (lambda s=s: lambda v: v.tensor_copy(
                    out=alpha_fm[:, s * 128:(s + 1) * 128], in_=pal[0:4, 0:128]))(),
                    reads=[(t_pal, BOXF)],
                    writes=[(t_afm, (0, 4, s * 128, (s + 1) * 128))])
            kb.op("tensor", lambda t: t.matmul(
                pae[0:32, 0:TW6], e2_sb[:], alpha_fm[:], start=True, stop=True),
                reads=[(t_e2, BOXF), (t_afm, BOXF)], writes=[(t_pae, BOXF)])
            for hc in range(HC):
                kb.op("tensor", (lambda hc=hc, hv=hv: lambda t: t.matmul(
                    pzz[0:32, 0:TW6], aT_sb[:, hc * 32:(hc + 1) * 32],
                    hv[:, hc, :], start=(hc == 0), stop=(hc == HC - 1)))(),
                    reads=[(t_aT, (0, 128, hc * 32, (hc + 1) * 32)),
                           (t_hs, (0, 128, hc * TW6, (hc + 1) * TW6))],
                    writes=[(t_pzz, BOXF)], inc=(hc == HC - 1))
            kb.op("vector", lambda v: v.tensor_copy(out=zf[:], in_=pzz[0:32, 0:TW6]),
                  reads=[(t_pzz, BOXF)], writes=[(t_zf, BOXF)])
            kb.op("vector", lambda v: v.tensor_tensor(
                out=zs[:], in0=zf[:], in1=pae[0:32, 0:TW6], op=ALU.mult),
                reads=[(t_zf, BOXF), (t_pae, BOXF)], writes=[(t_zs, BOXF)])
            HH = max(1, HC // 4)
            for oc in range(NOC):
                for hf in range(4 if HC >= 4 else 1):
                    bi6 = (oc * 4 + hf) % 2
                    wbb, twbb = wb_sb[bi6], t_wb[bi6]
                    kb.dma("sync", f"p6wb{bi6}", (lambda oc=oc, hf=hf, wbb=wbb: lambda e: e.dma_start(
                        out=wbb.rearrange("p (h n) -> p h n", n=OCW),
                        in_=wbT_bf[hf * HH:(hf + 1) * HH, :, oc * OCW:(oc + 1) * OCW].rearrange("h p n -> p h n")))(),
                        reads=[(t_wbbf, (0, 128, hf * HH * D + oc * OCW, ((hf * HH + HH - 1) * D) + (oc + 1) * OCW))],
                        writes=[(twbb, BOXF)])
                    for hc in range(hf * HH, hf * HH + HH):
                        for s in range(SPT5):
                            kb.op("tensor", (lambda hc=hc, hf=hf, s=s, wbb=wbb, hv=hv: lambda t: t.matmul(
                                pw[s][:], hv[:, hc, s * 128:(s + 1) * 128],
                                wbb[:, (hc - hf * HH) * OCW:(hc - hf * HH + 1) * OCW],
                                start=(hc == 0), stop=False))(),
                                reads=[(t_hs, (0, 128, hc * TW6 + s * 128, hc * TW6 + (s + 1) * 128)),
                                       (twbb, (0, 128, (hc - hf * HH) * OCW, (hc - hf * HH + 1) * OCW))],
                                writes=[(t_pw[s], BOXF)], inc=False)
                for s in range(SPT5):
                    kb.op("tensor", (lambda oc=oc, s=s: lambda t: t.matmul(
                        pw[s][:], zs[:, s * 128:(s + 1) * 128],
                        bT_sb[:, oc * OCW:(oc + 1) * OCW],
                        start=False, stop=True))(),
                        reads=[(t_zs, (0, 32, s * 128, (s + 1) * 128)),
                               (t_bT, (0, 32, oc * OCW, (oc + 1) * OCW))],
                        writes=[(t_pw[s], BOXF)], inc=True)
                for s in range(SPT5):
                    si = (ttb % 2) * SPT5 + s
                    kb.op("vector", (lambda si=si, oc=oc: lambda v: v.tensor_tensor(
                        out=f_sb[si][:, oc * OCW:(oc + 1) * OCW], in0=pw[si % SPT5][:],
                        in1=bb_sb[:, oc * OCW:(oc + 1) * OCW], op=ALU.add))(),
                        reads=[(t_pw[s], BOXF), (t_bb, BOXF)],
                        writes=[(t_f[si], (0, 128, oc * OCW, (oc + 1) * OCW))])
                if ttb > 0 and oc < min(NOC, SPT5):
                    ln3_stage(ttb - 1, only_s=oc)
            if ttb + 1 < NTTB5:
                load_h2t(ttb + 1)
            if ttb > 0:
                for s_rem in range(min(NOC, SPT5), SPT5):
                    ln3_stage(ttb - 1, only_s=s_rem)
            if ttb == NTTB5 - 1:
                ln3_stage(ttb)
    if cfg.get("stop_after", 9) >= 6:
        _phase6()

    kb.finalize_wait_all_dma()
    block = es.enter_context(nc.Block())
    kb.replay(block)
    es.close()
    return nc



def rope_tables(S, HD=128):
    inv = 1.0 / (10000.0 ** (np.arange(0, HD, 2, dtype=np.float32) / HD))
    t = np.arange(S, dtype=np.float32)
    fr = np.outer(t, inv)
    cos64 = np.cos(fr).astype(np.float32).T
    sin64 = np.sin(fr).astype(np.float32).T
    cos128 = np.concatenate([cos64, cos64], 0)
    sin128s = np.concatenate([-sin64, sin64], 0)
    return cos128, sin128s


def own_tiles(parity, NT):
    return [t for t in range(NT) if t % 4 in (parity, 3 - parity)]


def make_masks(parity, cfg):
    S = cfg["S"]; NT = S // 128; NSLOT = NT // 2
    tiles = own_tiles(parity, NT)
    m = np.zeros((NSLOT, 2, 128, 512), np.float32)
    for i, T in enumerate(tiles):
        KT = 2 * (i + 1)
        for j, kt in enumerate((KT - 2, KT - 1)):
            q_glob = T * 128 + np.arange(128)
            k_glob = kt * 128 + np.arange(128)
            mk = (k_glob[:, None] <= q_glob[None, :]).astype(np.float32)
            m[i, j] = np.tile(mk, (1, 4))
    return m.astype(ml_dtypes.bfloat16)


def prep_core_inputs(cfg, parity, xb, Wq, Wk, Wv, Wo, Wb, b_base, lora_A, lora_B,
                     router_W, router_b, domain_prior, active):
    S = cfg["S"]; D = cfg["D"]; NH = cfg["NH"]; NKV = cfg["NKV"]
    NT = S // 128; NSLOT = NT // 2; HC = D // 128
    tiles = own_tiles(parity, NT)
    x_tiles = np.ascontiguousarray(xb.reshape(NT, 128, D))
    x_own = np.ascontiguousarray(x_tiles[tiles])
    cos128, sin128s = rope_tables(S)
    own_cols = np.concatenate([np.arange(T * 128, (T + 1) * 128) for T in tiles])
    act = np.asarray(active)
    d = {
        "x_all": x_tiles,
        "x_own": x_own,
        "wkT": np.ascontiguousarray(Wk.T.reshape(HC, 128, NKV * 128)),
        "wvT": np.ascontiguousarray(Wv.T.reshape(HC, 128, NKV * 128)),
        "wqT": np.ascontiguousarray(Wq.T.reshape(HC, 128, NH * 128)),
        "woT": np.ascontiguousarray(Wo.T.reshape(HC, 128, D)),
        "wbT": np.ascontiguousarray(Wb.T.reshape(HC, 128, D)),
        "bb": np.ascontiguousarray(np.broadcast_to(b_base, (128, D))),
        "aT": np.ascontiguousarray(lora_A[act].transpose(2, 0, 1).reshape(D, 32).reshape(HC, 128, 32)),
        "bT": np.ascontiguousarray(lora_B[act].transpose(0, 2, 1).reshape(32, D)),
        "rwT": np.ascontiguousarray(router_W[act].T.reshape(HC, 128, 4)),
        "prior": np.ascontiguousarray(np.broadcast_to(domain_prior[act], (128, 4))).astype(np.float32),
        "exp2": np.repeat(np.eye(4, dtype=np.float32) * SCALING, 8, axis=1),
        "cosq": np.ascontiguousarray(cos128[:, own_cols]),
        "sinq": np.ascontiguousarray(sin128s[:, own_cols]),
        "cosk": cos128,
        "sink": sin128s,
        "masks": make_masks(parity, cfg),
        "ident": np.eye(128, dtype=ml_dtypes.bfloat16),
        "ones_in": np.ones((128, 128), dtype=ml_dtypes.bfloat16),
        "shift_in": np.full((128, 1), ESHIFT, np.float32),
        "rb4": np.ascontiguousarray(np.broadcast_to(router_b[act], (128, 4))).astype(np.float32),
    }
    return d



REAL_CFG = dict(S=4096, D=4096, NH=32, NKV=8)


def kernel(L0_attn_hidden_states, L0_attn_Wq, L0_attn_Wk, L0_attn_Wv, L0_attn_Wo,
           L0_ffn_W_base, L0_ffn_b_base, L0_ffn_lora_A, L0_ffn_lora_B,
           L0_ffn_router_W, L0_ffn_router_b, L0_ffn_domain_prior, L0_ffn_active):
    from concourse.bass_utils import run_bass_kernel_spmd
    cfg = REAL_CFG
    S = cfg["S"]; D = cfg["D"]
    NT = S // 128
    x = np.asarray(L0_attn_hidden_states)
    act = np.asarray(L0_ffn_active)
    HCr = D // 128

    shared = {
        "wkT": np.ascontiguousarray(L0_attn_Wk.T.reshape(HCr, 128, -1)),
        "wvT": np.ascontiguousarray(L0_attn_Wv.T.reshape(HCr, 128, -1)),
        "wqT": np.ascontiguousarray(L0_attn_Wq.T.reshape(HCr, 128, -1)),
        "woT": np.ascontiguousarray(L0_attn_Wo.T.reshape(HCr, 128, D)),
        "wbT": np.ascontiguousarray(L0_ffn_W_base.T.reshape(HCr, 128, D)),
        "bb": np.ascontiguousarray(np.broadcast_to(L0_ffn_b_base, (128, D))),
        "aT": np.ascontiguousarray(L0_ffn_lora_A[act].transpose(2, 0, 1).reshape(D, 32).reshape(HCr, 128, 32)),
        "bT": np.ascontiguousarray(L0_ffn_lora_B[act].transpose(0, 2, 1).reshape(32, D)),
        "rwT": np.ascontiguousarray(L0_ffn_router_W[act].T.reshape(HCr, 128, 4)),
        "prior": np.ascontiguousarray(np.broadcast_to(L0_ffn_domain_prior[act], (128, 4))).astype(np.float32),
        "rb4": np.ascontiguousarray(np.broadcast_to(L0_ffn_router_b[act], (128, 4))).astype(np.float32),
        "exp2": np.repeat(np.eye(4, dtype=np.float32) * SCALING, 8, axis=1),
        "ident": np.eye(128, dtype=ml_dtypes.bfloat16),
        "ones_in": np.ones((128, 128), dtype=ml_dtypes.bfloat16),
        "shift_in": np.full((128, 1), ESHIFT, np.float32),
    }
    cos128, sin128s = rope_tables(S)
    shared["cosk"] = cos128
    shared["sink"] = sin128s
    par = {}
    for p in (0, 1):
        tiles = own_tiles(p, NT)
        own_cols = np.concatenate([np.arange(T * 128, (T + 1) * 128) for T in tiles])
        par[p] = {
            "cosq": np.ascontiguousarray(cos128[:, own_cols]),
            "sinq": np.ascontiguousarray(sin128s[:, own_cols]),
            "masks": make_masks(p, cfg),
        }
    x_tiles = [np.ascontiguousarray(x[b].reshape(NT, 128, D)) for b in range(4)]
    in_maps = []
    for c in range(8):
        b, p = c // 2, c % 2
        tiles = own_tiles(p, NT)
        m = dict(shared)
        m.update(par[p])
        m["x_all"] = x_tiles[b]
        m["x_own"] = np.ascontiguousarray(x_tiles[b][tiles])
        in_maps.append(m)

    nc = build(cfg)
    res = run_bass_kernel_spmd(nc, in_maps, core_ids=list(range(8)))
    out = np.empty((4, NT, 128, D), np.float32)
    for c in range(8):
        b, p = c // 2, c % 2
        tiles = own_tiles(p, NT)
        out[b, tiles] = res.results[c]["y"]
    return out.reshape(4, S, D)
